# version 06
# baseline (speedup 1.0000x reference)
import contextlib
import numpy as np
import ml_dtypes
import concourse.bass as bass
import concourse.mybir as mybir
from concourse.bass_utils import run_bass_kernel_spmd

F32 = mybir.dt.float32
I32 = mybir.dt.int32
BF16 = mybir.dt.bfloat16
AF = mybir.ActivationFunctionType
ALU = mybir.AluOpType
AX = mybir.AxisListType

B = 2
S = 8192
D = 2048
H = 16
HL = 4
DK = 128
NCH = D // 128
SBK = 512
NSB = S // SBK
QT = 2048
NBLK = 4
NKT = S // 128
CKT = 16
CEXP = 16.0
EPS = 1e-5
SCL = 1.0 / np.sqrt(DK)
LORA_SCALE = 2.0
RSQRT_MAGIC = 1597463007.0


class Sem:
    def __init__(self, nc, es, name):
        self.h = es.enter_context(nc.semaphore(name))
        self.n = 0

    def reserve(self, k=1):
        self.n += k
        return self.n


class Emit:
    ENGS = ("sync", "gpsimd", "tensor", "scalar", "vector")

    def __init__(self):
        self.q = {e: [] for e in self.ENGS}

    def add(self, eng, fn):
        self.q[eng].append(fn)

    def flush(self, block):
        for e in self.ENGS:
            fns = self.q[e]
            self.q[e] = []
            if fns:
                def body(hdl, fns=fns):
                    for f in fns:
                        f(hdl)
                getattr(block, e)(body)


def build_program():
    nc = bass.Bass("TRN2", target_bir_lowering=False, debug=False)

    x_in = nc.dram_tensor("x", [S, D], F32, kind="ExternalInput")
    xq_in = nc.dram_tensor("xq", [QT, D], F32, kind="ExternalInput")
    wqT = nc.dram_tensor("wqT", [D, HL * DK], F32, kind="ExternalInput")
    wkT = nc.dram_tensor("wkT", [D, HL * DK], F32, kind="ExternalInput")
    wvT = nc.dram_tensor("wvT", [D, HL * DK], F32, kind="ExternalInput")
    woT = nc.dram_tensor("woT", [D, D], F32, kind="ExternalInput")
    wbT = nc.dram_tensor("wbT", [D, D], F32, kind="ExternalInput")
    awrT = nc.dram_tensor("awrT", [D, 72], F32, kind="ExternalInput")
    bflT = nc.dram_tensor("bflT", [64, D], F32, kind="ExternalInput")
    bbase = nc.dram_tensor("bbase", [128, 16], F32, kind="ExternalInput")
    ident_in = nc.dram_tensor("ident", [128, 128], BF16, kind="ExternalInput")
    identf_in = nc.dram_tensor("identf", [128, 128], F32, kind="ExternalInput")
    pswap_in = nc.dram_tensor("pswap", [128, 128], BF16, kind="ExternalInput")
    e8_in = nc.dram_tensor("e8", [8, 64], BF16, kind="ExternalInput")
    cc_tab = nc.dram_tensor("cctab", [128, S], BF16, kind="ExternalInput")
    ss2_tab = nc.dram_tensor("ss2tab", [128, S], BF16, kind="ExternalInput")
    masks_in = nc.dram_tensor("masks", [4 * 128, 512], BF16, kind="ExternalInput")

    out_q = nc.dram_tensor("out_q", [QT, D], F32, kind="ExternalOutput")

    qsp = [nc.dram_tensor(f"qsp{h}", [128, NSB, SBK], BF16) for h in range(HL)]
    ksp = [nc.dram_tensor(f"ksp{h}", [128, NKT, 128], BF16) for h in range(HL)]
    vsp = [nc.dram_tensor(f"vsp{h}", [128, NKT, 128], BF16) for h in range(HL)]
    ccg_in = [nc.dram_tensor(f"ccin{k}", [HL * DK, QT], BF16) for k in range(4)]
    ccg_out = [nc.dram_tensor(f"ccout{k}", [4 * HL * DK, QT], BF16) for k in range(4)]
    r1sp = nc.dram_tensor("r1sp", [QT, D], F32)

    es = contextlib.ExitStack()
    with es:
        sb = lambda name, shape, dt: es.enter_context(nc.sbuf_tensor(name, shape, dt))

        ids = sb("ids", [128, 128], BF16)
        idsf = sb("idsf", [128, 128], F32)
        psw = sb("psw", [128, 128], BF16)
        e8s = sb("e8s", [8, 64], BF16)
        onecol = sb("onecol", [128, 1], BF16)
        onecf = sb("onecf", [128, 1], F32)
        onerow = sb("onerow", [1, 128], F32)
        cexp = sb("cexp", [128, 1], F32)
        bnst = sb("bnst", [128, 24], F32)
        mv = sb("mv", [128, 2], F32)
        rstd = sb("rstd", [128, 1], F32)
        rs_t1 = sb("rs_t1", [128, 1], F32)
        rs_t2 = sb("rs_t2", [128, 1], F32)

        s_v = Sem(nc, es, "s_v")
        s_cst = Sem(nc, es, "s_cst")
        s_ldx = Sem(nc, es, "s_ldx")
        s_wld = Sem(nc, es, "s_wld")
        s_x = [Sem(nc, es, f"s_x{i}") for i in range(4)]
        s_spq = [Sem(nc, es, f"s_spq{i}") for i in range(2)]
        s_spv = [Sem(nc, es, f"s_spv{i}") for i in range(2)]
        s_q = [Sem(nc, es, f"s_q{i}") for i in range(2)]
        s_kv = [Sem(nc, es, f"s_kv{i}") for i in range(2)]
        s_sto = [Sem(nc, es, f"s_sto{i}") for i in range(2)]
        s_xq = [Sem(nc, es, f"s_xq{i}") for i in range(2)]
        s_sr = [Sem(nc, es, f"s_sr{i}") for i in range(2)]
        s_ra = [Sem(nc, es, f"s_ra{i}") for i in range(2)]
        s_rc = [Sem(nc, es, f"s_rc{i}") for i in range(2)]
        s_so = [Sem(nc, es, f"s_so{i}") for i in range(2)]
        s_ln = Sem(nc, es, "s_ln")
        s_ln2 = Sem(nc, es, "s_ln2")
        s_tp = Sem(nc, es, "s_tp")
        s_evT = Sem(nc, es, "s_evT")
        s_proj = Sem(nc, es, "s_proj")
        s_pe = Sem(nc, es, "s_pe")
        s_sw = Sem(nc, es, "s_sw")
        s_rope = Sem(nc, es, "s_rope")
        s_vev = Sem(nc, es, "s_vev")
        s_mm = Sem(nc, es, "s_mm")
        s_exp = Sem(nc, es, "s_exp")
        s_msk = Sem(nc, es, "s_msk")
        s_av = Sem(nc, es, "s_av")
        s_zi = Sem(nc, es, "s_zi")
        s_bc = Sem(nc, es, "s_bc")
        s_o = Sem(nc, es, "s_o")
        s_cc = Sem(nc, es, "s_cc")
        s_lda = Sem(nc, es, "s_lda")
        s_mm3 = Sem(nc, es, "s_mm3")
        s_yev = Sem(nc, es, "s_yev")
        s_r1 = Sem(nc, es, "s_r1")
        s_g = Sem(nc, es, "s_g")
        s_rs = Sem(nc, es, "s_rs")
        s_zd = Sem(nc, es, "s_zd")
        s_zp = Sem(nc, es, "s_zp")
        s_zc = Sem(nc, es, "s_zc")
        s_rp1 = Sem(nc, es, "s_rp1")
        s_pl = Sem(nc, es, "s_pl")

        def queue_rsqrt(E):
            r = [s_v.reserve() for _ in range(9)]
            dn = s_rs.reserve()

            def body(v, r=r, dn=dn):
                v.tensor_scalar(out=rs_t1[:, :], in0=mv[:, 1:2], scalar1=EPS,
                                scalar2=None, op0=ALU.add).then_inc(s_v.h, 1)
                v.wait_ge(s_v.h, r[0])
                v.tensor_scalar(out=rs_t2[:, :], in0=rs_t1[:, :], scalar1=-0.5,
                                scalar2=None, op0=ALU.mult).then_inc(s_v.h, 1)
                v.wait_ge(s_v.h, r[1])
                v.tensor_scalar(out=rstd[:, :].bitcast(I32),
                                in0=rs_t1[:, :].bitcast(I32), scalar1=1,
                                scalar2=None,
                                op0=ALU.logical_shift_right).then_inc(s_v.h, 1)
                v.wait_ge(s_v.h, r[2])
                v.tensor_scalar(out=rstd[:, :].bitcast(I32),
                                in0=rstd[:, :].bitcast(I32), scalar1=-1.0,
                                scalar2=RSQRT_MAGIC, op0=ALU.mult,
                                op1=ALU.add).then_inc(s_v.h, 1)
                v.wait_ge(s_v.h, r[3])
                k = 4
                for it in range(2):
                    v.tensor_tensor(rs_t1[:, :], rstd[:, :], rstd[:, :],
                                    op=ALU.mult).then_inc(s_v.h, 1)
                    v.wait_ge(s_v.h, r[k]); k += 1
                    v.tensor_scalar(out=rs_t1[:, :], in0=rs_t1[:, :],
                                    scalar1=rs_t2[:, :], scalar2=1.5,
                                    op0=ALU.mult, op1=ALU.add).then_inc(s_v.h, 1)
                    v.wait_ge(s_v.h, r[k]); k += 1
                    inst = v.tensor_tensor(rstd[:, :], rstd[:, :], rs_t1[:, :],
                                           op=ALU.mult)
                    if it == 1:
                        inst.then_inc(s_rs.h, 1)
                    else:
                        inst.then_inc(s_v.h, 1)
                        v.wait_ge(s_v.h, r[k]); k += 1
            E.add("vector", body)
            return dn

        def queue_ln(E, src_fn, out_fn):
            t_stats = s_v.reserve()
            t_aggr = s_ln.reserve()

            def v_stats(v, src_fn=src_fn, t_stats=t_stats, t_aggr=t_aggr):
                src = src_fn()
                for i in range(3):
                    v.bn_stats(bnst[:, i * 6:(i + 1) * 6], src[:, i * 512:(i + 1) * 512])
                v.bn_stats(bnst[:, 18:24], src[:, 1536:2048]).then_inc(s_v.h, 1)
                v.wait_ge(s_v.h, t_stats)
                v.bn_aggr(mv[:, :], bnst[:, :]).then_inc(s_ln.h, 1)
                v.wait_ge(s_ln.h, t_aggr)
            E.add("vector", v_stats)
            dn = queue_rsqrt(E)
            t_norm = s_ln2.reserve()

            def v_norm(v, src_fn=src_fn, out_fn=out_fn, dn=dn):
                v.wait_ge(s_rs.h, dn)
                v.tensor_scalar(out=out_fn(), in0=src_fn(),
                                scalar1=mv[:, 0:1], scalar2=rstd[:, :],
                                op0=ALU.subtract, op1=ALU.mult).then_inc(s_ln2.h, 1)
            E.add("vector", v_norm)
            return t_norm

        with contextlib.ExitStack() as es1:
            sb1 = lambda n_, sh, dt: es1.enter_context(nc.sbuf_tensor(n_, sh, dt))
            ps1 = lambda n_, sh, dt: es1.enter_context(nc.psum_tensor(n_, sh, dt))
            xs = sb1("xs", [128, 4, D], F32)
            lnb = sb1("lnb", [128, 2, D], BF16)
            mv4 = sb1("mv4", [128, 4, 2], F32)
            rstd4 = sb1("rstd4", [128, 4], F32)
            rs4a = sb1("rs4a", [128, 4], F32)
            rs4b = sb1("rs4b", [128, 4], F32)
            lnT = sb1("lnT", [128, NCH, SBK], BF16)
            wq_s = sb1("wq_s", [128, NCH, HL * DK], BF16)
            wk_s = sb1("wk_s", [128, NCH, HL * DK], BF16)
            wv_s = sb1("wv_s", [128, NCH, HL * DK], BF16)
            ccs = sb1("ccs", [128, S], BF16)
            ss2s = sb1("ss2s", [128, S], BF16)
            pre = sb1("pre", [128, 2, SBK], BF16)
            swp = sb1("swp", [128, 2, SBK], BF16)
            tmpA = sb1("tmpA", [128, 2, SBK], BF16)
            ropo = sb1("ropo", [128, 2, SBK], BF16)
            vstg = sb1("vstg", [128, 2, HL * DK], BF16)
            pT = ps1("pT", [128, 1024], BF16)
            pPJ0 = ps1("pPJ0", [128, 512], F32)
            pPJ1 = ps1("pPJ1", [128, 512], F32)
            pSw0 = ps1("pSw0", [128, 512], F32)
            pSw1 = ps1("pSw1", [128, 512], F32)
            pPJ = [pPJ0, pPJ1]
            pSw = [pSw0, pSw1]

            blk1 = es1.enter_context(nc.Block("p1"))
            E = Emit()

            n_ldi1 = s_ldx.reserve(16 * 6)

            def ld_init(sync):
                sync.dma_start(out=ids[:, :], in_=ident_in[:, :]).then_inc(s_ldx.h, 16)
                sync.dma_start(out=idsf[:, :], in_=identf_in[:, :]).then_inc(s_ldx.h, 16)
                sync.dma_start(out=psw[:, :], in_=pswap_in[:, :]).then_inc(s_ldx.h, 16)
                sync.dma_start(out=e8s[:, :], in_=e8_in[:, :]).then_inc(s_ldx.h, 16)
                sync.dma_start(out=ccs[:, :], in_=cc_tab[:, :]).then_inc(s_ldx.h, 16)
                sync.dma_start(out=ss2s[:, :], in_=ss2_tab[:, :]).then_inc(s_ldx.h, 16)
            E.add("sync", ld_init)

            n_wld1 = s_wld.reserve(16 * 3 * NCH)

            def ld_w(g):
                for w_d, w_s in ((wqT, wq_s), (wkT, wk_s), (wvT, wv_s)):
                    w_r = w_d.rearrange("(c p) o -> c p o", p=128)
                    for c in range(NCH):
                        g.dma_start(out=w_s[:, c, :], in_=w_r[c]).then_inc(s_wld.h, 16)
            E.add("gpsimd", ld_w)

            n_cst = s_cst.reserve()

            def consts(v):
                v.memset(onecol[:, :], 1.0)
                v.memset(onecf[:, :], 1.0)
                v.memset(onerow[:, :], 1.0)
                v.memset(cexp[:, :], -CEXP).then_inc(s_cst.h, 1)
                v.wait_ge(s_ldx.h, n_ldi1)
                v.wait_ge(s_wld.h, n_wld1)
            E.add("vector", consts)

            def eng_gate1(e, a=n_cst, b_=n_ldi1, c_=n_wld1):
                e.wait_ge(s_cst.h, a)
                e.wait_ge(s_ldx.h, b_)
                e.wait_ge(s_wld.h, c_)
            E.add("scalar", eng_gate1)
            E.add("tensor", eng_gate1)

            x_r = x_in.rearrange("(g p) d -> g p d", p=128)
            ln2_mark = []
            tp_mark = []
            for s_i in range(NSB):
                for tt in range(4):
                    g = 4 * s_i + tt
                    t_x = s_x[tt].reserve(16)
                    need_gx = ln2_mark[4 * (s_i - 1) + tt] if s_i >= 1 else 0

                    def ld_x(sync, g=g, tt=tt, need=need_gx):
                        if need > 0:
                            sync.wait_ge(s_ln2.h, need)
                        sync.dma_start(out=xs[:, tt, :], in_=x_r[g]).then_inc(
                            s_x[tt].h, 16)
                    E.add("sync", ld_x)

                    va = s_v.reserve()
                    t_ag = s_ln.reserve()

                    def v_st(v, tt=tt, t_x=t_x, va=va, t_ag=t_ag):
                        v.wait_ge(s_x[tt].h, t_x)
                        v.wait_ge(s_ln.h, t_ag - 1)
                        for i in range(3):
                            v.bn_stats(bnst[:, i * 6:(i + 1) * 6],
                                       xs[:, tt, i * 512:(i + 1) * 512])
                        v.bn_stats(bnst[:, 18:24], xs[:, tt, 1536:2048]).then_inc(s_v.h, 1)
                        v.wait_ge(s_v.h, va)
                        v.bn_aggr(mv4[:, tt, :], bnst[:, :]).then_inc(s_ln.h, 1)
                    E.add("vector", v_st)

                agg_all = s_ln.n
                r4 = [s_v.reserve() for _ in range(11)]
                t_rs4 = s_rs.reserve()

                def v_rs4(v, agg_all=agg_all, r4=r4):
                    v.wait_ge(s_ln.h, agg_all)
                    v.tensor_scalar(out=rs4a[:, :], in0=mv4[:, :, 1], scalar1=EPS,
                                    scalar2=None, op0=ALU.add).then_inc(s_v.h, 1)
                    v.wait_ge(s_v.h, r4[0])
                    v.tensor_scalar(out=rs4b[:, :], in0=rs4a[:, :], scalar1=-0.5,
                                    scalar2=None, op0=ALU.mult).then_inc(s_v.h, 1)
                    v.wait_ge(s_v.h, r4[1])
                    v.tensor_scalar(out=rstd4[:, :].bitcast(I32),
                                    in0=rs4a[:, :].bitcast(I32), scalar1=1,
                                    scalar2=None,
                                    op0=ALU.logical_shift_right).then_inc(s_v.h, 1)
                    v.wait_ge(s_v.h, r4[2])
                    v.tensor_scalar(out=rstd4[:, :].bitcast(I32),
                                    in0=rstd4[:, :].bitcast(I32), scalar1=-1.0,
                                    scalar2=RSQRT_MAGIC, op0=ALU.mult,
                                    op1=ALU.add).then_inc(s_v.h, 1)
                    v.wait_ge(s_v.h, r4[3])
                    k = 4
                    for itn in range(2):
                        v.tensor_tensor(rs4a[:, :], rstd4[:, :], rstd4[:, :],
                                        op=ALU.mult).then_inc(s_v.h, 1)
                        v.wait_ge(s_v.h, r4[k]); k += 1
                        v.tensor_tensor(rs4a[:, :], rs4a[:, :], rs4b[:, :],
                                        op=ALU.mult).then_inc(s_v.h, 1)
                        v.wait_ge(s_v.h, r4[k]); k += 1
                        if itn == 0:
                            v.tensor_scalar(out=rs4a[:, :], in0=rs4a[:, :], scalar1=1.5,
                                            scalar2=None, op0=ALU.add).then_inc(s_v.h, 1)
                            v.wait_ge(s_v.h, r4[k]); k += 1
                            v.tensor_tensor(rstd4[:, :], rstd4[:, :], rs4a[:, :],
                                            op=ALU.mult).then_inc(s_v.h, 1)
                            v.wait_ge(s_v.h, r4[k]); k += 1
                        else:
                            v.tensor_scalar(out=rs4a[:, :], in0=rs4a[:, :], scalar1=1.5,
                                            scalar2=None, op0=ALU.add).then_inc(s_v.h, 1)
                            v.wait_ge(s_v.h, r4[k]); k += 1
                            v.tensor_tensor(rstd4[:, :], rstd4[:, :], rs4a[:, :],
                                            op=ALU.mult).then_inc(s_rs.h, 1)
                E.add("vector", v_rs4)

                for tt in range(4):
                    g = 4 * s_i + tt
                    need_tp = tp_mark[g - 2] if g >= 2 else 0
                    t_nm = s_ln2.reserve()

                    def v_nm(v, tt=tt, t_rs4=t_rs4, need_tp=need_tp):
                        v.wait_ge(s_rs.h, t_rs4)
                        if need_tp > 0:
                            v.wait_ge(s_tp.h, need_tp)
                        v.tensor_scalar(out=lnb[:, tt % 2, :], in0=xs[:, tt, :],
                                        scalar1=mv4[:, tt, 0:1],
                                        scalar2=rstd4[:, tt:tt + 1],
                                        op0=ALU.subtract, op1=ALU.mult).then_inc(
                                        s_ln2.h, 1)
                    E.add("vector", v_nm)
                    ln2_mark.append(s_ln2.n)

                    for rnd in range(2):
                        need_ev = s_evT.n
                        t_tpv = s_tp.reserve()

                        def t_tp(t, tt=tt, rnd=rnd, need_ln=t_nm, need_ev=need_ev):
                            if rnd == 0:
                                t.wait_ge(s_ln2.h, need_ln)
                            if need_ev > 0:
                                t.wait_ge(s_evT.h, need_ev)
                            for i in range(8):
                                c = rnd * 8 + i
                                mmi = t.transpose(pT[:, i * 128:(i + 1) * 128],
                                                  lnb[:, tt % 2, c * 128:(c + 1) * 128],
                                                  ids[:, :])
                            mmi.then_inc(s_tp.h, 1)
                        E.add("tensor", t_tp)

                        s_evT.reserve()

                        def v_ev(v, tt=tt, rnd=rnd, need=t_tpv):
                            v.wait_ge(s_tp.h, need)
                            v.tensor_copy(
                                lnT[:, rnd * 8:(rnd + 1) * 8, tt * 128:(tt + 1) * 128],
                                pT[:, :].rearrange("p (c f) -> p c f", f=128)).then_inc(
                                s_evT.h, 1)
                        E.add("vector", v_ev)
                    tp_mark.append(s_tp.n)

                ev_all = s_evT.n

                def t_sb_gate(t, need=ev_all):
                    t.wait_ge(s_evT.h, need)
                E.add("tensor", t_sb_gate)

                for h in range(HL):
                    for which, w_s in (("q", wq_s), ("k", wk_s)):
                        pj = s_proj.n % 2
                        need_pe = s_pe.n - 1
                        t_prj = s_proj.reserve()

                        def t_proj(t, h=h, w_s=w_s, pj=pj, need_pe=need_pe):
                            if need_pe > 0:
                                t.wait_ge(s_pe.h, need_pe)
                            for c in range(NCH):
                                mmi = t.matmul(pPJ[pj][:, :], w_s[:, c, h * 128:(h + 1) * 128],
                                               lnT[:, c, :], start=(c == 0),
                                               stop=(c == NCH - 1))
                            mmi.then_inc(s_proj.h, 1)
                        E.add("tensor", t_proj)

                        prslot = s_rope.n % 2
                        need_rope = s_rope.n - 1
                        t_pev = s_pe.reserve()

                        def v_pe(v, need=t_prj, pj=pj, prslot=prslot,
                                 need_rope=need_rope):
                            v.wait_ge(s_proj.h, need)
                            if need_rope > 0:
                                v.wait_ge(s_rope.h, need_rope)
                            v.tensor_copy(pre[:, prslot, :], pPJ[pj][:, :]).then_inc(
                                s_pe.h, 1)
                        E.add("vector", v_pe)

                        need_rp = s_rope.n - 1
                        t_swv = s_sw.reserve()

                        def t_sw(t, need=t_pev, prslot=prslot, need_rp=need_rp):
                            t.wait_ge(s_pe.h, need)
                            if need_rp > 0:
                                t.wait_ge(s_rope.h, need_rp)
                            t.matmul(pSw[prslot][:, :], psw[:, :], pre[:, prslot, :],
                                     start=True, stop=True).then_inc(s_sw.h, 1)
                        E.add("tensor", t_sw)

                        ridx = s_rope.n
                        rslot = ridx % 2
                        va = s_v.reserve(2)
                        need_wr = s_spq[rslot].n if ridx >= 2 else 0
                        t_rp = s_rope.reserve()

                        def v_rope(v, s_i=s_i, rslot=rslot, need=t_swv, va=va,
                                   need_wr=need_wr):
                            v.wait_ge(s_sw.h, need)
                            v.tensor_copy(swp[:, rslot, :], pSw[rslot][:, :])
                            v.tensor_tensor(tmpA[:, rslot, :], pre[:, rslot, :],
                                            ccs[:, s_i * SBK:(s_i + 1) * SBK],
                                            op=ALU.mult).then_inc(s_v.h, 1)
                            v.wait_ge(s_v.h, va - 1)
                            v.tensor_tensor(swp[:, rslot, :], swp[:, rslot, :],
                                            ss2s[:, s_i * SBK:(s_i + 1) * SBK],
                                            op=ALU.mult).then_inc(s_v.h, 1)
                            v.wait_ge(s_v.h, va)
                            if need_wr > 0:
                                v.wait_ge(s_spq[rslot].h, need_wr)
                            v.tensor_tensor(ropo[:, rslot, :], tmpA[:, rslot, :],
                                            swp[:, rslot, :], op=ALU.add).then_inc(s_rope.h, 1)
                        E.add("vector", v_rope)

                        s_spq[rslot].reserve(16)

                        def sc_spill(scl, s_i=s_i, h=h, which=which, rslot=rslot,
                                     need=t_rp):
                            scl.wait_ge(s_rope.h, need)
                            if which == "q":
                                dst = qsp[h][:, s_i, :]
                            else:
                                dst = ksp[h][:, 4 * s_i:4 * s_i + 4, :].rearrange(
                                    "p a b -> p (a b)")
                            scl.dma_start(out=dst, in_=ropo[:, rslot, :]).then_inc(
                                s_spq[rslot].h, 16)
                        E.add("scalar", sc_spill)

                for tt in range(4):
                    g = 4 * s_i + tt
                    pj = s_proj.n % 2
                    need_pe = s_pe.n - 1
                    t_vp = s_proj.reserve()

                    def t_v(t, tt=tt, pj=pj, need_pe=need_pe):
                        if need_pe > 0:
                            t.wait_ge(s_pe.h, need_pe)
                        for c in range(NCH):
                            mmi = t.matmul(pPJ[pj][:, :], lnT[:, c, tt * 128:(tt + 1) * 128],
                                           wv_s[:, c, :], start=(c == 0), stop=(c == NCH - 1))
                        mmi.then_inc(s_proj.h, 1)
                    E.add("tensor", t_v)

                    vslot = g % 2
                    need_wr = s_spv[vslot].n if g >= 2 else 0
                    t_vev = s_pe.reserve()

                    def v_vev(v, vslot=vslot, pj=pj, need=t_vp, need_wr=need_wr):
                        v.wait_ge(s_proj.h, need)
                        if need_wr > 0:
                            v.wait_ge(s_spv[vslot].h, need_wr)
                        v.tensor_copy(vstg[:, vslot, :], pPJ[pj][:, :]).then_inc(s_pe.h, 1)
                    E.add("vector", v_vev)

                    s_spv[vslot].reserve(16 * HL)

                    def sc_vsp(scl, g=g, vslot=vslot, need=t_vev):
                        scl.wait_ge(s_pe.h, need)
                        for h in range(HL):
                            scl.dma_start(
                                out=vsp[h][:, g, :],
                                in_=vstg[:, vslot, h * 128:(h + 1) * 128]).then_inc(
                                s_spv[vslot].h, 16)
                    E.add("scalar", sc_vsp)

            def p1_fin(scl, a0=s_spq[0].n, a1=s_spq[1].n, b0=s_spv[0].n, b1=s_spv[1].n):
                scl.wait_ge(s_spq[0].h, a0)
                scl.wait_ge(s_spq[1].h, a1)
                scl.wait_ge(s_spv[0].h, b0)
                scl.wait_ge(s_spv[1].h, b1)
            E.add("scalar", p1_fin)
            E.flush(blk1)

        with contextlib.ExitStack() as es2:
            sb2 = lambda n_, sh, dt: es2.enter_context(nc.sbuf_tensor(n_, sh, dt))
            ps2 = lambda n_, sh, dt: es2.enter_context(nc.psum_tensor(n_, sh, dt))
            msk = sb2("msk", [128, 4, 512], BF16)
            qbuf = sb2("qbuf", [128, 2, 512], BF16)
            kbuf = sb2("kbuf", [128, 2, CKT * 128], BF16)
            vbuf = sb2("vbuf", [128, 2, CKT * 128], BF16)
            pb = sb2("pb", [128, 4, 512], BF16)
            zp = sb2("zp", [128, 4, 512], F32)
            zinv = sb2("zinv", [1, 512], F32)
            binv = sb2("binv", [128, 512], F32)
            oout = sb2("oout", [128, 2, 512], BF16)
            pS = ps2("pS", [128, 4, 512], F32)
            pO0 = ps2("pO0", [128, 512], F32)
            pO1 = ps2("pO1", [128, 512], F32)
            pOb = [pO0, pO1]
            pZ = ps2("pZ", [128, 512], F32)
            pB = ps2("pB", [128, 512], F32)

            blk2 = es2.enter_context(nc.Block("p2"))
            E = Emit()

            n_mskld = s_ldx.reserve(16 * 4)

            def ld_msk(sync):
                m_r = masks_in.rearrange("(t p) q -> t p q", p=128)
                for i in range(4):
                    sync.dma_start(out=msk[:, i, :], in_=m_r[i]).then_inc(s_ldx.h, 16)
            E.add("sync", ld_msk)

            def eng_gate2(e, need=n_mskld):
                e.wait_ge(s_ldx.h, need)
            E.add("vector", eng_gate2)
            E.add("tensor", eng_gate2)
            E.add("scalar", eng_gate2)

            mm_mark_iter = []
            kcons_mark = []
            gchunk = 0
            for qb in range(NSB):
                nkt = 4 * (qb + 1)
                nchunks = (nkt + CKT - 1) // CKT
                for h in range(HL):
                    it = qb * HL + h
                    qslot = it % 2
                    need_qg = mm_mark_iter[it - 2] if it >= 2 else 0
                    q_mark = s_q[qslot].reserve(16)

                    def ld_q(sync, qb=qb, h=h, qslot=qslot, need=need_qg):
                        if need > 0:
                            sync.wait_ge(s_mm.h, need)
                        sync.dma_start(out=qbuf[:, qslot, :],
                                       in_=qsp[h][:, qb, :]).then_inc(s_q[qslot].h, 16)
                    E.add("sync", ld_q)

                    ch_base = gchunk
                    ld_marks = {}

                    def emit_ld_kv(ch, qb=qb, h=h, nkt=nkt, ch_base=ch_base,
                                   ld_marks=ld_marks):
                        nonlocal gchunk
                        k0 = ch * CKT
                        nk = min(CKT, nkt - k0)
                        gc = ch_base + ch
                        cs = gc % 2
                        need = kcons_mark[gc - 2] if gc >= 2 else 0
                        ld_marks[ch] = s_kv[cs].reserve(32)

                        def ld_kv(sync, h=h, k0=k0, nk=nk, cs=cs, need=need):
                            if need > 0:
                                sync.wait_ge(s_av.h, need)
                            sync.dma_start(
                                out=kbuf[:, cs, 0:nk * 128],
                                in_=ksp[h][:, k0:k0 + nk, :].rearrange(
                                    "p a b -> p (a b)")).then_inc(s_kv[cs].h, 16)
                            sync.dma_start(
                                out=vbuf[:, cs, 0:nk * 128],
                                in_=vsp[h][:, k0:k0 + nk, :].rearrange(
                                    "p a b -> p (a b)")).then_inc(s_kv[cs].h, 16)
                        E.add("sync", ld_kv)
                        gchunk += 1

                    emit_ld_kv(0)
                    if nchunks > 1:
                        emit_ld_kv(1)

                    o_guard = s_o.n - 1
                    pO = pOb[it % 2]

                    def emit_t_s(kt):
                        ch = kt // CKT
                        cs = (ch_base + ch) % 2
                        ko = kt - ch * CKT
                        sslot = kt % 4
                        need_exp = s_exp.n - 1
                        t_sv = s_mm.reserve()

                        def t_s(t, kt=kt, cs=cs, ko=ko, sslot=sslot, qslot=qslot,
                                need_ld=ld_marks[ch], need_exp=need_exp,
                                o_guard=o_guard, q_mark=q_mark):
                            if kt == 0:
                                t.wait_ge(s_q[qslot].h, q_mark)
                                t.wait_ge(s_kv[cs].h, need_ld)
                            elif ko == 0:
                                t.wait_ge(s_kv[cs].h, need_ld)
                            if need_exp > 0:
                                t.wait_ge(s_exp.h, need_exp)
                            t.matmul(pS[:, sslot, :],
                                     kbuf[:, cs, ko * 128:(ko + 1) * 128],
                                     qbuf[:, qslot, :], start=True,
                                     stop=True).then_inc(s_mm.h, 1)
                        E.add("tensor", t_s)
                        return t_sv

                    ts_marks = {0: emit_t_s(0), 1: emit_t_s(1)}
                    zsem = [s_zd, s_zp]
                    zslot_mark = {}
                    zcmb_prev = s_zd.n

                    for bp in range(nkt // 2):
                        kt0 = 2 * bp
                        pp = (bp % 2) * 2
                        ze = zsem[bp % 2]
                        need_av = s_av.n - 2
                        need_z0 = zslot_mark.get(kt0 - 4, 0)
                        need_z1 = zslot_mark.get(kt0 - 3, 0)
                        t_ev = s_exp.reserve()

                        def s_e2(scl, pp=pp, need_mm=ts_marks[kt0 + 1],
                                 need_av=need_av, nz=max(need_z0, need_z1),
                                 ze=ze):
                            scl.wait_ge(s_mm.h, need_mm)
                            if need_av > 0:
                                scl.wait_ge(s_av.h, need_av)
                            if nz > 0:
                                scl.wait_ge(ze.h, nz)
                            scl.activation(out=pb[:, pp:pp + 2, :],
                                           in_=pS[:, pp:pp + 2, :],
                                           func=AF.Exp, bias=cexp[:, :],
                                           scale=SCL).then_inc(s_exp.h, 1)
                        E.add("scalar", s_e2)

                        for j in range(2):
                            kt = kt0 + j
                            diag = kt >= 4 * qb
                            if diag:
                                t_mk = s_msk.reserve()

                                def v_m(v, qb=qb, kt=kt, pslot=pp + j, need=t_ev):
                                    v.wait_ge(s_exp.h, need)
                                    v.tensor_tensor(pb[:, pslot, :], pb[:, pslot, :],
                                                    msk[:, kt - 4 * qb, :],
                                                    op=ALU.mult).then_inc(s_msk.h, 1)
                                E.add("vector", v_m)
                            need_m = s_msk.n

                            need_zsl = zslot_mark.get(kt - 4, 0)
                            zslot_mark[kt] = ze.reserve()

                            def v_z(v, kt=kt, pslot=pp + j, diag=diag, need_e=t_ev,
                                    need_m=need_m, need_zsl=need_zsl, ze=ze,
                                    zprev=zcmb_prev):
                                if diag:
                                    v.wait_ge(s_msk.h, need_m)
                                else:
                                    v.wait_ge(s_exp.h, need_e)
                                if kt < 4 and zprev > 0:
                                    v.wait_ge(s_zd.h, zprev)
                                if kt >= 4:
                                    v.wait_ge(ze.h, need_zsl)
                                    v.tensor_tensor(zp[:, kt % 4, :], zp[:, kt % 4, :],
                                                    pb[:, pslot, :],
                                                    op=ALU.add).then_inc(ze.h, 1)
                                else:
                                    v.tensor_copy(zp[:, kt % 4, :],
                                                  pb[:, pslot, :]).then_inc(ze.h, 1)
                            E.add("vector" if bp % 2 == 0 else "gpsimd", v_z)

                        if kt0 + 2 < nkt:
                            ts_marks[kt0 + 2] = emit_t_s(kt0 + 2)
                            ts_marks[kt0 + 3] = emit_t_s(kt0 + 3)

                        for j in range(2):
                            kt = kt0 + j
                            ch = kt // CKT
                            cs = (ch_base + ch) % 2
                            ko = kt - ch * CKT
                            diag = kt >= 4 * qb
                            lastk = (ko == CKT - 1 or kt == nkt - 1)
                            need_m = s_msk.n
                            t_avv = s_av.reserve()

                            def t_av(t, kt=kt, cs=cs, ko=ko, pslot=pp + j, diag=diag,
                                     need_e=t_ev, need_m=need_m, nkt=nkt,
                                     pO=pO, o_guard=o_guard):
                                if kt == 0 and o_guard > 0:
                                    t.wait_ge(s_o.h, o_guard)
                                if diag:
                                    t.wait_ge(s_msk.h, need_m)
                                else:
                                    t.wait_ge(s_exp.h, need_e)
                                t.matmul(pO[:, :], vbuf[:, cs, ko * 128:(ko + 1) * 128],
                                         pb[:, pslot, :], start=(kt == 0),
                                         stop=(kt == nkt - 1)).then_inc(s_av.h, 1)
                            E.add("tensor", t_av)
                            if lastk:
                                kcons_mark.append(s_av.n)
                                if ch + 2 < nchunks:
                                    emit_ld_kv(ch + 2)

                    mm_mark_iter.append(s_mm.n)

                    zd_all = s_zd.n
                    zp_all = s_zp.n
                    va = s_v.reserve(2)
                    t_zcmb = s_zd.reserve()

                    def v_zcmb(v, zd_all=zd_all, zp_all=zp_all, va=va):
                        v.wait_ge(s_zd.h, zd_all)
                        v.wait_ge(s_zp.h, zp_all)
                        v.tensor_tensor(zp[:, 0, :], zp[:, 0, :], zp[:, 1, :],
                                        op=ALU.add).then_inc(s_v.h, 1)
                        v.tensor_tensor(zp[:, 2, :], zp[:, 2, :], zp[:, 3, :],
                                        op=ALU.add).then_inc(s_v.h, 1)
                        v.wait_ge(s_v.h, va)
                        v.tensor_tensor(zp[:, 0, :], zp[:, 0, :], zp[:, 2, :],
                                        op=ALU.add).then_inc(s_zd.h, 1)
                    E.add("vector", v_zcmb)

                    t_zcv = s_zc.reserve()

                    def t_zc(t, need=t_zcmb):
                        t.wait_ge(s_zd.h, need)
                        t.matmul(pZ[0:1, :], onecf[:, :], zp[:, 0, :],
                                 start=True, stop=True).then_inc(s_zc.h, 1)
                    E.add("tensor", t_zc)

                    t_ziv = s_zi.reserve()

                    def v_zi(v, need_zc=t_zcv, need_av=s_av.n):
                        v.wait_ge(s_zc.h, need_zc)
                        v.wait_ge(s_av.h, need_av)
                        v.reciprocal(zinv[:, :], pZ[0:1, :]).then_inc(s_zi.h, 1)
                    E.add("vector", v_zi)

                    t_bcv = s_bc.reserve()

                    def t_bc(t, need=t_ziv):
                        t.wait_ge(s_zi.h, need)
                        t.matmul(pB[:, :], onerow[:, :], zinv[:, :],
                                 start=True, stop=True).then_inc(s_bc.h, 1)
                    E.add("tensor", t_bc)

                    oslot = it % 2
                    need_st = s_sto[oslot].n if it >= 2 else 0
                    va = s_v.reserve()
                    t_ov = s_o.reserve()

                    def v_o(v, oslot=oslot, need_bc=t_bcv, need_st=need_st, va=va,
                            pO=pOb[it % 2]):
                        v.wait_ge(s_bc.h, need_bc)
                        v.tensor_copy(binv[:, :], pB[:, :]).then_inc(s_v.h, 1)
                        if need_st > 0:
                            v.wait_ge(s_sto[oslot].h, need_st)
                        v.wait_ge(s_v.h, va)
                        v.tensor_tensor(oout[:, oslot, :], pO[:, :], binv[:, :],
                                        op=ALU.mult).then_inc(s_o.h, 1)
                    E.add("vector", v_o)

                    s_sto[oslot].reserve(16)

                    def sc_o(scl, qb=qb, h=h, oslot=oslot, need=t_ov):
                        scl.wait_ge(s_o.h, need)
                        k = qb // 4
                        scl.dma_start(
                            out=ccg_in[k][h * 128:(h + 1) * 128,
                                          (qb % 4) * 512:(qb % 4 + 1) * 512],
                            in_=oout[:, oslot, :]).then_inc(s_sto[oslot].h, 16)
                    E.add("scalar", sc_o)

                if qb % 4 == 3:
                    s_cc.reserve()

                    def g_cc(g, k=qb // 4, need0=s_sto[0].n, need1=s_sto[1].n):
                        g.wait_ge(s_sto[0].h, need0)
                        g.wait_ge(s_sto[1].h, need1)
                        g.collective_compute(
                            "AllGather", ALU.bypass,
                            replica_groups=[[0, 1, 2, 3], [4, 5, 6, 7]],
                            ins=[ccg_in[k].ap().opt()],
                            outs=[ccg_out[k].ap().opt()]).then_inc(s_cc.h, 1)
                    E.add("gpsimd", g_cc)
            E.flush(blk2)

        with contextlib.ExitStack() as es3:
            sb3 = lambda n_, sh, dt: es3.enter_context(nc.sbuf_tensor(n_, sh, dt))
            ps3 = lambda n_, sh, dt: es3.enter_context(nc.psum_tensor(n_, sh, dt))
            wo_s = sb3("wo_s", [128, NCH, D], BF16)
            aT = sb3("aT", [128, NCH, 512], BF16)
            y1T = sb3("y1T", [128, NCH, 512], BF16)
            y1 = sb3("y1", [128, 4, D], BF16)
            xqb = sb3("xqb", [128, 2, D], F32)
            t3 = sb3("t3", [128, D], F32)
            r1b = sb3("r1b", [128, 2, D], F32)
            pY = ps3("pY", [128, 512], F32)
            pT3 = ps3("pT3", [128, 1024], BF16)

            blk3 = es3.enter_context(nc.Block("p3"))
            E = Emit()

            n_wld3 = s_wld.reserve(16 * NCH)

            def ld_wo(g):
                w_r = woT.rearrange("(c p) o -> c p o", p=128)
                for c in range(NCH):
                    g.dma_start(out=wo_s[:, c, :], in_=w_r[c]).then_inc(s_wld.h, 16)
            E.add("gpsimd", ld_wo)

            def t_gate3(t, need=n_wld3):
                t.wait_ge(s_wld.h, need)
            E.add("tensor", t_gate3)

            xq_r = xq_in.rearrange("(g p) d -> g p d", p=128)
            r1_r = r1sp.rearrange("(g p) d -> g p d", p=128)

            r1_mark = []
            mm3_mark = []
            for tb in range(NBLK):
                E.flush(blk3)
                need_mm = mm3_mark[tb - 1] if tb >= 1 else 0
                lda_tb = s_lda.reserve(16 * NCH)

                def sync_aT(sync, tb=tb, need_mm=need_mm):
                    if need_mm > 0:
                        sync.wait_ge(s_mm3.h, need_mm)
                    pid = sync.partition_id()
                    for core in nc.Switch(engines=[sync], index=[pid], n=8):
                        tq = core % 4
                        sync.wait_ge(s_cc.h, tq + 1)
                        co = ccg_out[tq]
                        for cb in range(NCH):
                            sync.dma_start(
                                out=aT[:, cb, :],
                                in_=co[cb * 128:(cb + 1) * 128,
                                       tb * 512:(tb + 1) * 512]).then_inc(s_lda.h, 16)
                blk3.sync(sync_aT)

                for oc in range(NCH):
                    need_ev = s_yev.n
                    t_mmv = s_mm3.reserve()

                    def t_op(t, oc=oc, need_ld=lda_tb, need_ev=need_ev):
                        if oc == 0:
                            t.wait_ge(s_lda.h, need_ld)
                        if need_ev > 0:
                            t.wait_ge(s_yev.h, need_ev)
                        for c in range(NCH):
                            mmi = t.matmul(pY[:, :], wo_s[:, c, oc * 128:(oc + 1) * 128],
                                           aT[:, c, :], start=(c == 0), stop=(c == NCH - 1))
                        mmi.then_inc(s_mm3.h, 1)
                    E.add("tensor", t_op)

                    s_yev.reserve()

                    def v_ye(v, oc=oc, need=t_mmv):
                        v.wait_ge(s_mm3.h, need)
                        v.tensor_copy(y1T[:, oc, :], pY[:, :]).then_inc(s_yev.h, 1)
                    E.add("vector", v_ye)
                mm3_mark.append(s_mm3.n)

                for rnd in range(8):
                    need_ev = s_yev.n
                    need_pt = s_evT.n
                    t_tpv = s_tp.reserve()

                    def t_tp3(t, rnd=rnd, need_ev=need_ev, need_pt=need_pt):
                        if rnd == 0:
                            t.wait_ge(s_yev.h, need_ev)
                        if need_pt > 0:
                            t.wait_ge(s_evT.h, need_pt)
                        for i in range(8):
                            j = rnd * 8 + i
                            oc, sub = j // 4, j % 4
                            mmi = t.transpose(pT3[:, i * 128:(i + 1) * 128],
                                              y1T[:, oc, sub * 128:(sub + 1) * 128],
                                              ids[:, :])
                        mmi.then_inc(s_tp.h, 1)
                    E.add("tensor", t_tp3)

                    s_evT.reserve()

                    def v_pt3(v, rnd=rnd, need=t_tpv):
                        v.wait_ge(s_tp.h, need)
                        oc0 = rnd * 2
                        v.tensor_copy(
                            y1[:, :, oc0 * 128:(oc0 + 2) * 128].rearrange(
                                "p s (c f) -> p c s f", f=128),
                            pT3[:, :].rearrange("p (c s f) -> p c s f",
                                                c=2, s=4)).then_inc(s_evT.h, 1)
                    E.add("vector", v_pt3)

                for sub in range(4):
                    g = tb * 4 + sub
                    need_r1g = r1_mark[g - 2] if g >= 2 else 0
                    t_xqv = s_xq[g % 2].reserve(16)

                    def ld_xq(sync, g=g, need=need_r1g):
                        if need > 0:
                            sync.wait_ge(s_r1.h, need)
                        sync.dma_start(out=xqb[:, g % 2, :], in_=xq_r[g]).then_inc(
                            s_xq[g % 2].h, 16)
                    E.add("sync", ld_xq)

                    need_t = s_evT.n

                    def v_lngate(v, need_t=need_t):
                        v.wait_ge(s_evT.h, need_t)
                    E.add("vector", v_lngate)
                    tnorm = queue_ln(E, (lambda sub=sub: y1[:, sub, :]),
                                     (lambda: t3[:, :]))

                    need_wr = s_sr[g % 2].n if g >= 2 else 0
                    t_r1v = s_r1.reserve()

                    def v_r1(v, g=g, need_ld=t_xqv, need_wr=need_wr, tnorm=tnorm):
                        v.wait_ge(s_xq[g % 2].h, need_ld)
                        v.wait_ge(s_ln2.h, tnorm)
                        if need_wr > 0:
                            v.wait_ge(s_sr[g % 2].h, need_wr)
                        v.tensor_tensor(r1b[:, g % 2, :], t3[:, :],
                                        xqb[:, g % 2, :], op=ALU.add).then_inc(s_r1.h, 1)
                    E.add("vector", v_r1)
                    r1_mark.append(s_r1.n)

                    s_sr[g % 2].reserve(16)

                    def sc_r1(scl, g=g, need=t_r1v):
                        scl.wait_ge(s_r1.h, need)
                        scl.dma_start(out=r1_r[g], in_=r1b[:, g % 2, :]).then_inc(
                            s_sr[g % 2].h, 16)
                    E.add("scalar", sc_r1)

            def p3_fin(scl, a0=s_sr[0].n, a1=s_sr[1].n):
                scl.wait_ge(s_sr[0].h, a0)
                scl.wait_ge(s_sr[1].h, a1)
            E.add("scalar", p3_fin)
            E.flush(blk3)

        with contextlib.ExitStack() as es4:
            sb4 = lambda n_, sh, dt: es4.enter_context(nc.sbuf_tensor(n_, sh, dt))
            ps4 = lambda n_, sh, dt: es4.enter_context(nc.psum_tensor(n_, sh, dt))
            wb_s = sb4("wb_s", [128, NCH, D], BF16)
            awr_s = sb4("awr_s", [128, NCH, 72], BF16)
            bfl_s = sb4("bfl_s", [64, D], BF16)
            bb_s = sb4("bb_s", [128, 16], F32)
            r1a = sb4("r1a", [128, 2, D], F32)
            r1c = sb4("r1c", [128, 2, D], F32)
            ln2 = sb4("ln2", [128, D], BF16)
            ln2T = sb4("ln2T", [128, NCH, 512], BF16)
            hbuf = sb4("hbuf", [128, 512], BF16)
            hbufL = sb4("hbufL", [8, 512], F32)
            lgt = sb4("lgt", [128, 4, 8], F32)
            negm1 = sb4("negm1", [128, 1], F32)
            m2b = sb4("m2b", [128, 1], F32)
            eqb = sb4("eqb", [128, 8], F32)
            lmk = sb4("lmk", [128, 8], F32)
            pexp = sb4("pexp", [128, 8], F32)
            indb = sb4("indb", [128, 8], F32)
            wgt = sb4("wgt", [128, 8], F32)
            zg = sb4("zg", [128, 1], F32)
            zgi = sb4("zgi", [128, 1], F32)
            wn = sb4("wn", [128, 8], F32)
            wnT = sb4("wnT", [8, 128], BF16)
            wexpT = sb4("wexpT", [64, 512], BF16)
            h2 = sb4("h2", [64, 512], BF16)
            ffnT = sb4("ffnT", [128, NCH, 512], BF16)
            ffn = sb4("ffn", [128, 4, D], BF16)
            t4 = sb4("t4", [128, D], F32)
            r2b = sb4("r2b", [128, 2, D], F32)
            pY4 = ps4("pY4", [128, 512], F32)
            pT4 = ps4("pT4", [128, 1024], BF16)
            pH = ps4("pH", [128, 512], F32)
            pG = ps4("pG", [128, 512], F32)

            blk4 = es4.enter_context(nc.Block("p4"))
            E = Emit()

            n_wld4 = s_wld.reserve(16 * (2 * NCH + 1))

            def ld_w4(g):
                w_r = wbT.rearrange("(c p) o -> c p o", p=128)
                for c in range(NCH):
                    g.dma_start(out=wb_s[:, c, :], in_=w_r[c]).then_inc(s_wld.h, 16)
                a_r = awrT.rearrange("(c p) o -> c p o", p=128)
                for c in range(NCH):
                    g.dma_start(out=awr_s[:, c, :], in_=a_r[c]).then_inc(s_wld.h, 16)
                g.dma_start(out=bfl_s[:, :], in_=bflT[:, :]).then_inc(s_wld.h, 16)
            E.add("gpsimd", ld_w4)

            n_ld4 = s_ldx.reserve(16)

            def ld_bb(sync):
                sync.dma_start(out=bb_s[:, :], in_=bbase[:, :]).then_inc(s_ldx.h, 16)
            E.add("sync", ld_bb)

            def gate4(e, a=n_wld4, b_=n_ld4):
                e.wait_ge(s_wld.h, a)
                e.wait_ge(s_ldx.h, b_)
            E.add("tensor", gate4)
            E.add("vector", gate4)
            E.add("scalar", gate4)

            out_r = out_q.rearrange("(g p) d -> g p d", p=128)

            ln3_mark = []
            r2_mark = []
            for tb in range(NBLK):
                for sub in range(4):
                    g = tb * 4 + sub
                    need_lg = ln3_mark[g - 2] if g >= 2 else 0
                    t_rav = s_ra[g % 2].reserve(16)

                    def ld_r1(sync, g=g, need=need_lg):
                        if need > 0:
                            sync.wait_ge(s_ln2.h, need)
                        sync.dma_start(out=r1a[:, g % 2, :], in_=r1_r[g]).then_inc(
                            s_ra[g % 2].h, 16)
                    E.add("sync", ld_r1)

                    need_tp = s_tp.n

                    def v_lngate4(v, g=g, t_rav=t_rav, need_tp=need_tp):
                        v.wait_ge(s_ra[g % 2].h, t_rav)
                        if need_tp > 0:
                            v.wait_ge(s_tp.h, need_tp)
                    E.add("vector", v_lngate4)
                    queue_ln(E, (lambda g=g: r1a[:, g % 2, :]), (lambda: ln2[:, :]))
                    ln3_mark.append(s_ln2.n)

                    for rnd in range(2):
                        need_ln = s_ln2.n
                        need_ev = s_evT.n
                        t_tpv = s_tp.reserve()

                        def t_tp4(t, rnd=rnd, need_ln=need_ln, need_ev=need_ev):
                            if rnd == 0:
                                t.wait_ge(s_ln2.h, need_ln)
                            if need_ev > 0:
                                t.wait_ge(s_evT.h, need_ev)
                            for i in range(8):
                                c = rnd * 8 + i
                                mmi = t.transpose(pT4[:, i * 128:(i + 1) * 128],
                                                  ln2[:, c * 128:(c + 1) * 128], ids[:, :])
                            mmi.then_inc(s_tp.h, 1)
                        E.add("tensor", t_tp4)

                        s_evT.reserve()

                        def v_ev4(v, sub=sub, rnd=rnd, need=t_tpv):
                            v.wait_ge(s_tp.h, need)
                            v.tensor_copy(
                                ln2T[:, rnd * 8:(rnd + 1) * 8, sub * 128:(sub + 1) * 128],
                                pT4[:, :].rearrange("p (c f) -> p c f", f=128)).then_inc(
                                s_evT.h, 1)
                        E.add("vector", v_ev4)

                need_g = s_g.n
                t_awrv = s_mm3.reserve()

                def t_awr(t, need=s_evT.n, need_g=need_g):
                    t.wait_ge(s_evT.h, need)
                    if need_g > 0:
                        t.wait_ge(s_g.h, need_g)
                    for c in range(NCH):
                        mmi = t.matmul(pH[0:72, :], awr_s[:, c, :], ln2T[:, c, :],
                                       start=(c == 0), stop=(c == NCH - 1))
                    mmi.then_inc(s_mm3.h, 1)
                E.add("tensor", t_awr)

                va = s_v.reserve()
                t_gv = s_g.reserve()

                def v_hev(v, need=t_awrv, va=va):
                    v.wait_ge(s_mm3.h, need)
                    v.tensor_copy(hbuf[0:64, :], pH[0:64, :]).then_inc(s_v.h, 1)
                    v.wait_ge(s_v.h, va)
                    v.tensor_copy(hbufL[:, :], pH[64:72, :]).then_inc(s_g.h, 1)
                E.add("vector", v_hev)

                need_cons = s_lda.n
                for sub in range(4):
                    t_lgv = s_tp.reserve()

                    def t_lg(t, sub=sub, need=t_gv, need_cons=need_cons):
                        if sub == 0:
                            t.wait_ge(s_g.h, need)
                            if need_cons > 0:
                                t.wait_ge(s_lda.h, need_cons)
                        t.transpose(pG[:, sub * 8:(sub + 1) * 8],
                                    hbufL[:, sub * 128:(sub + 1) * 128],
                                    idsf[0:8, 0:8]).then_inc(s_tp.h, 1)
                    E.add("tensor", t_lg)

                lg_need = s_tp.n
                t_lgc = s_lda.reserve()

                def v_lg(v, need=lg_need):
                    v.wait_ge(s_tp.h, need)
                    v.tensor_copy(
                        lgt[:, :, :], pG[:, 0:32].rearrange(
                            "p (s e) -> p s e", e=8)).then_inc(s_lda.h, 1)
                E.add("vector", v_lg)

                for sub in range(4):
                    va = s_v.reserve(4)
                    t_ind = s_msk.reserve()

                    def v_g1(v, sub=sub, need=t_lgc, va=va):
                        if sub == 0:
                            v.wait_ge(s_lda.h, need)
                        l = lgt[:, sub, :]
                        v.reduce_max(negm1[:, :], l, axis=AX.X,
                                     negate=True).then_inc(s_v.h, 1)
                        v.wait_ge(s_v.h, va - 3)
                        v.tensor_scalar(out=eqb[:, :], in0=l, scalar1=negm1[:, :],
                                        scalar2=0.0, op0=ALU.add,
                                        op1=ALU.is_ge).then_inc(s_v.h, 1)
                        v.wait_ge(s_v.h, va - 2)
                        v.scalar_tensor_tensor(out=lmk[:, :], in0=eqb[:, :],
                                               scalar=-30000.0, in1=l,
                                               op0=ALU.mult, op1=ALU.add).then_inc(s_v.h, 1)
                        v.wait_ge(s_v.h, va - 1)
                        v.reduce_max(m2b[:, :], lmk[:, :], axis=AX.X).then_inc(s_v.h, 1)
                        v.wait_ge(s_v.h, va)
                        v.tensor_scalar(out=indb[:, :], in0=l, scalar1=m2b[:, :],
                                        scalar2=None, op0=ALU.is_ge).then_inc(s_msk.h, 1)
                    E.add("vector", v_g1)

                    t_px = s_rs.reserve()

                    def s_px(scl, sub=sub, need=t_ind):
                        scl.wait_ge(s_msk.h, need)
                        scl.activation(out=pexp[:, :], in_=lgt[:, sub, :],
                                       func=AF.Exp, bias=negm1[:, :],
                                       scale=1.0).then_inc(s_rs.h, 1)
                    E.add("scalar", s_px)

                    va = s_v.reserve(3)
                    t_wnv = s_ln2.reserve()

                    def v_g2(v, need=t_px, va=va):
                        v.wait_ge(s_rs.h, need)
                        v.tensor_tensor(wgt[:, :], indb[:, :], pexp[:, :],
                                        op=ALU.mult).then_inc(s_v.h, 1)
                        v.wait_ge(s_v.h, va - 2)
                        v.reduce_sum(zg[:, :], wgt[:, :], axis=AX.X).then_inc(s_v.h, 1)
                        v.wait_ge(s_v.h, va - 1)
                        v.reciprocal(zgi[:, :], zg[:, :]).then_inc(s_v.h, 1)
                        v.wait_ge(s_v.h, va)
                        v.tensor_scalar(out=wn[:, :], in0=wgt[:, :],
                                        scalar1=zgi[:, :], scalar2=LORA_SCALE,
                                        op0=ALU.mult, op1=ALU.mult).then_inc(s_ln2.h, 1)
                    E.add("vector", v_g2)

                    need_ev = s_yev.n
                    t_wtv = s_tp.reserve()

                    def t_wnT(t, need=t_wnv, need_ev=need_ev):
                        t.wait_ge(s_ln2.h, need)
                        if need_ev > 0:
                            t.wait_ge(s_yev.h, need_ev)
                        t.transpose(pG[0:8, 128:256], wn[:, :],
                                    idsf[:, :]).then_inc(s_tp.h, 1)
                    E.add("tensor", t_wnT)

                    t_wcv = s_r1.reserve()

                    def v_wnT(v, need=t_wtv):
                        v.wait_ge(s_tp.h, need)
                        v.tensor_copy(wnT[:, :], pG[0:8, 128:256]).then_inc(s_r1.h, 1)
                    E.add("vector", v_wnT)

                    t_e8v = s_bc.reserve()

                    def t_e8(t, need=t_wcv):
                        t.wait_ge(s_r1.h, need)
                        t.matmul(pG[0:64, 256:384], e8s[:, :], wnT[:, :],
                                 start=True, stop=True).then_inc(s_bc.h, 1)
                    E.add("tensor", t_e8)

                    s_yev.reserve()

                    def v_we(v, sub=sub, need=t_e8v):
                        v.wait_ge(s_bc.h, need)
                        v.tensor_copy(wexpT[:, sub * 128:(sub + 1) * 128],
                                      pG[0:64, 256:384]).then_inc(s_yev.h, 1)
                    E.add("vector", v_we)

                t_h2v = s_g.reserve()

                def v_h2(v, need=s_yev.n):
                    v.wait_ge(s_yev.h, need)
                    v.tensor_tensor(h2[:, :], hbuf[0:64, :], wexpT[:, :],
                                    op=ALU.mult).then_inc(s_g.h, 1)
                E.add("vector", v_h2)

                for oc in range(NCH):
                    need_ev = s_exp.n
                    t_fv = s_mm.reserve()

                    def t_ffn(t, oc=oc, need_h2=t_h2v, need_ev=need_ev):
                        if oc == 0:
                            t.wait_ge(s_g.h, need_h2)
                        if need_ev > 0:
                            t.wait_ge(s_exp.h, need_ev)
                        for c in range(NCH):
                            t.matmul(pY4[:, :], wb_s[:, c, oc * 128:(oc + 1) * 128],
                                     ln2T[:, c, :], start=(c == 0), stop=False)
                        t.matmul(pY4[:, :], bfl_s[:, oc * 128:(oc + 1) * 128],
                                 h2[:, :], start=False, stop=True).then_inc(s_mm.h, 1)
                    E.add("tensor", t_ffn)

                    s_exp.reserve()

                    def s_fev(scl, oc=oc, need=t_fv):
                        scl.wait_ge(s_mm.h, need)
                        scl.activation(out=ffnT[:, oc, :], in_=pY4[:, :],
                                       func=AF.Identity, bias=bb_s[:, oc:oc + 1],
                                       scale=1.0).then_inc(s_exp.h, 1)
                    E.add("scalar", s_fev)

                for rnd in range(8):
                    need_ev = s_exp.n
                    need_pt = s_evT.n
                    t_tpv = s_tp.reserve()

                    def t_tpf(t, rnd=rnd, need_ev=need_ev, need_pt=need_pt):
                        if rnd == 0:
                            t.wait_ge(s_exp.h, need_ev)
                        if need_pt > 0:
                            t.wait_ge(s_evT.h, need_pt)
                        for i in range(8):
                            j = rnd * 8 + i
                            oc, sub = j // 4, j % 4
                            mmi = t.transpose(pT4[:, i * 128:(i + 1) * 128],
                                              ffnT[:, oc, sub * 128:(sub + 1) * 128],
                                              ids[:, :])
                        mmi.then_inc(s_tp.h, 1)
                    E.add("tensor", t_tpf)

                    s_evT.reserve()

                    def v_ptf(v, rnd=rnd, need=t_tpv):
                        v.wait_ge(s_tp.h, need)
                        oc0 = rnd * 2
                        v.tensor_copy(
                            ffn[:, :, oc0 * 128:(oc0 + 2) * 128].rearrange(
                                "p s (c f) -> p c s f", f=128),
                            pT4[:, :].rearrange("p (c s f) -> p c s f",
                                                c=2, s=4)).then_inc(s_evT.h, 1)
                    E.add("vector", v_ptf)

                for sub in range(4):
                    g = tb * 4 + sub
                    need_r2g = r2_mark[g - 2] if g >= 2 else 0
                    t_rcv = s_rc[g % 2].reserve(16)

                    def ld_r1b(sync, g=g, need=need_r2g):
                        if need > 0:
                            sync.wait_ge(s_o.h, need)
                        sync.dma_start(out=r1c[:, g % 2, :], in_=r1_r[g]).then_inc(
                            s_rc[g % 2].h, 16)
                    E.add("sync", ld_r1b)

                    need_t = s_evT.n

                    def v_lngate5(v, need_t=need_t):
                        v.wait_ge(s_evT.h, need_t)
                    E.add("vector", v_lngate5)
                    tnorm = queue_ln(E, (lambda sub=sub: ffn[:, sub, :]),
                                     (lambda: t4[:, :]))

                    need_st = s_so[g % 2].n if g >= 2 else 0
                    t_r2v = s_o.reserve()

                    def v_r2(v, g=g, need_ld=t_rcv, need_st=need_st, tnorm=tnorm):
                        v.wait_ge(s_rc[g % 2].h, need_ld)
                        v.wait_ge(s_ln2.h, tnorm)
                        if need_st > 0:
                            v.wait_ge(s_so[g % 2].h, need_st)
                        v.tensor_tensor(r2b[:, g % 2, :], t4[:, :],
                                        r1c[:, g % 2, :], op=ALU.add).then_inc(s_o.h, 1)
                    E.add("vector", v_r2)
                    r2_mark.append(s_o.n)

                    s_so[g % 2].reserve(16)

                    def sc_out(scl, g=g, need=t_r2v):
                        scl.wait_ge(s_o.h, need)
                        scl.dma_start(out=out_r[g], in_=r2b[:, g % 2, :]).then_inc(
                            s_so[g % 2].h, 16)
                    E.add("scalar", sc_out)

            def p4_fin(scl, a0=s_so[0].n, a1=s_so[1].n):
                scl.wait_ge(s_so[0].h, a0)
                scl.wait_ge(s_so[1].h, a1)
            E.add("scalar", p4_fin)
            E.flush(blk4)

    return nc


def _host_tables():
    half = DK // 2
    inv_freq = 10000.0 ** (-2.0 * np.arange(half) / DK)
    ang = np.arange(S)[:, None] * inv_freq[None, :]
    cosT = np.cos(ang).T.astype(np.float32)
    sinT = np.sin(ang).T.astype(np.float32)
    cc = np.concatenate([cosT, cosT], axis=0).astype(ml_dtypes.bfloat16)
    ss2 = np.concatenate([-sinT, sinT], axis=0).astype(ml_dtypes.bfloat16)
    masks = np.zeros((4 * 128, 512), dtype=ml_dtypes.bfloat16)
    for i in range(4):
        for kk in range(128):
            masks[i * 128 + kk, i * 128 + kk:] = 1.0
    ident = np.eye(128, dtype=ml_dtypes.bfloat16)
    pswap = np.zeros((128, 128), dtype=ml_dtypes.bfloat16)
    for i in range(64):
        pswap[i + 64, i] = 1.0
        pswap[i, i + 64] = 1.0
    e8 = np.zeros((8, 64), dtype=ml_dtypes.bfloat16)
    for e in range(8):
        e8[e, e * 8:(e + 1) * 8] = 1.0
    return cc, ss2, masks, ident, pswap, e8


def _perm_rows(w):
    perm = []
    for h in range(H):
        base = h * DK
        perm += [base + 2 * i for i in range(DK // 2)]
        perm += [base + 2 * i + 1 for i in range(DK // 2)]
    return w[perm, :]


_NC_CACHE = {}


def kernel(L0_attn_x, L0_attn_wq, L0_attn_wk, L0_attn_wv, L0_attn_wo,
           L0_ffn_W_base, L0_ffn_b_base, L0_ffn_W_router, L0_ffn_A_w, L0_ffn_B_w):
    x = np.ascontiguousarray(np.asarray(L0_attn_x, dtype=np.float32))
    wq = np.asarray(L0_attn_wq, dtype=np.float32)
    wk = np.asarray(L0_attn_wk, dtype=np.float32)
    wv = np.asarray(L0_attn_wv, dtype=np.float32)
    wo = np.asarray(L0_attn_wo, dtype=np.float32)
    wb = np.asarray(L0_ffn_W_base, dtype=np.float32)
    bb = np.asarray(L0_ffn_b_base, dtype=np.float32)
    wr = np.asarray(L0_ffn_W_router, dtype=np.float32)
    aw = np.asarray(L0_ffn_A_w, dtype=np.float32)
    bw = np.asarray(L0_ffn_B_w, dtype=np.float32)

    cc, ss2, masks, ident, pswap, e8 = _host_tables()

    wqTf = np.ascontiguousarray(_perm_rows(wq).T)
    wkTf = np.ascontiguousarray(_perm_rows(wk).T)
    wvTf = np.ascontiguousarray(wv.T)
    woTf = np.ascontiguousarray(wo.T)
    wbTf = np.ascontiguousarray(wb.T)
    aflT = aw.reshape(64, D).T
    wrTf = wr.T
    awrTf = np.ascontiguousarray(np.concatenate([aflT, wrTf], axis=1))
    bflTf = np.ascontiguousarray(bw.transpose(0, 2, 1).reshape(64, D))
    bbasef = np.ascontiguousarray(bb.reshape(16, 128).T)

    if "nc" not in _NC_CACHE:
        _NC_CACHE["nc"] = build_program()
    nc = _NC_CACHE["nc"]

    in_maps = []
    for c in range(8):
        b = c // 4
        hg = c % 4
        tq = c % 4
        in_maps.append({
            "x": x[b],
            "xq": np.ascontiguousarray(x[b, tq * QT:(tq + 1) * QT, :]),
            "wqT": np.ascontiguousarray(wqTf[:, hg * 512:(hg + 1) * 512]),
            "wkT": np.ascontiguousarray(wkTf[:, hg * 512:(hg + 1) * 512]),
            "wvT": np.ascontiguousarray(wvTf[:, hg * 512:(hg + 1) * 512]),
            "woT": woTf, "wbT": wbTf, "awrT": awrTf, "bflT": bflTf, "bbase": bbasef,
            "ident": ident, "identf": np.eye(128, dtype=np.float32),
            "pswap": pswap, "e8": e8,
            "cctab": cc, "ss2tab": ss2, "masks": masks,
        })
    res = run_bass_kernel_spmd(nc, in_maps, core_ids=list(range(8)))
    out = np.empty((B, S, D), dtype=np.float32)
    for c in range(8):
        b = c // 4
        tq = c % 4
        out[b, tq * QT:(tq + 1) * QT, :] = res.results[c]["out_q"]
    return out


# version 07
# speedup vs baseline: 1.3916x; 1.3916x over previous
import contextlib
import numpy as np
import ml_dtypes
import concourse.bass as bass
import concourse.mybir as mybir
from concourse.bass_utils import run_bass_kernel_spmd

F32 = mybir.dt.float32
I32 = mybir.dt.int32
BF16 = mybir.dt.bfloat16
AF = mybir.ActivationFunctionType
ALU = mybir.AluOpType
AX = mybir.AxisListType

B = 2
S = 8192
D = 2048
H = 16
HL = 4
DK = 128
NCH = D // 128
SBK = 512
NSB = S // SBK
QT = 2048
NBLK = 4
NKT = S // 128
CKT = 16
CEXP = 16.0
EPS = 1e-5
SCL = 1.0 / np.sqrt(DK)
LORA_SCALE = 2.0
RSQRT_MAGIC = 1597463007.0


class Sem:
    def __init__(self, nc, es, name):
        self.h = es.enter_context(nc.semaphore(name))
        self.n = 0

    def reserve(self, k=1):
        self.n += k
        return self.n


class Emit:
    ENGS = ("sync", "gpsimd", "tensor", "scalar", "vector")

    def __init__(self):
        self.q = {e: [] for e in self.ENGS}

    def add(self, eng, fn):
        self.q[eng].append(fn)

    def flush(self, block):
        for e in self.ENGS:
            fns = self.q[e]
            self.q[e] = []
            if fns:
                def body(hdl, fns=fns):
                    for f in fns:
                        f(hdl)
                getattr(block, e)(body)


def build_program():
    nc = bass.Bass("TRN2", target_bir_lowering=False, debug=False)

    x_in = nc.dram_tensor("x", [S, D], F32, kind="ExternalInput")
    xq_in = nc.dram_tensor("xq", [QT, D], F32, kind="ExternalInput")
    wqT = nc.dram_tensor("wqT", [D, HL * DK], F32, kind="ExternalInput")
    wkT = nc.dram_tensor("wkT", [D, HL * DK], F32, kind="ExternalInput")
    wvT = nc.dram_tensor("wvT", [D, HL * DK], F32, kind="ExternalInput")
    woT = nc.dram_tensor("woT", [D, D], F32, kind="ExternalInput")
    wbT = nc.dram_tensor("wbT", [D, D], F32, kind="ExternalInput")
    awrT = nc.dram_tensor("awrT", [D, 72], F32, kind="ExternalInput")
    bflT = nc.dram_tensor("bflT", [64, D], F32, kind="ExternalInput")
    bbase = nc.dram_tensor("bbase", [128, 16], F32, kind="ExternalInput")
    ident_in = nc.dram_tensor("ident", [128, 128], BF16, kind="ExternalInput")
    identf_in = nc.dram_tensor("identf", [128, 128], F32, kind="ExternalInput")
    pswap_in = nc.dram_tensor("pswap", [128, 128], BF16, kind="ExternalInput")
    e8_in = nc.dram_tensor("e8", [8, 64], BF16, kind="ExternalInput")
    cc_tab = nc.dram_tensor("cctab", [128, S], BF16, kind="ExternalInput")
    ss2_tab = nc.dram_tensor("ss2tab", [128, S], BF16, kind="ExternalInput")
    masks_in = nc.dram_tensor("masks", [4 * 128, 512], BF16, kind="ExternalInput")

    out_q = nc.dram_tensor("out_q", [QT, D], F32, kind="ExternalOutput")

    qsp = [nc.dram_tensor(f"qsp{h}", [128, NSB, SBK], BF16) for h in range(HL)]
    ksp = [nc.dram_tensor(f"ksp{h}", [128, NKT, 128], BF16) for h in range(HL)]
    vsp = [nc.dram_tensor(f"vsp{h}", [128, NKT, 128], BF16) for h in range(HL)]
    ccg_in = [nc.dram_tensor(f"ccin{k}", [HL * DK, QT], BF16) for k in range(4)]
    ccg_out = [nc.dram_tensor(f"ccout{k}", [4 * HL * DK, QT], BF16) for k in range(4)]
    r1sp = nc.dram_tensor("r1sp", [QT, D], F32)

    es = contextlib.ExitStack()
    with es:
        sb = lambda name, shape, dt: es.enter_context(nc.sbuf_tensor(name, shape, dt))

        ids = sb("ids", [128, 128], BF16)
        idsf = sb("idsf", [128, 128], F32)
        psw = sb("psw", [128, 128], BF16)
        e8s = sb("e8s", [8, 64], BF16)
        onecol = sb("onecol", [128, 1], BF16)
        onecf = sb("onecf", [128, 1], F32)
        onerow = sb("onerow", [1, 128], F32)
        cexp = sb("cexp", [128, 1], F32)
        bnst = sb("bnst", [128, 24], F32)
        mv = sb("mv", [128, 2], F32)
        rstd = sb("rstd", [128, 1], F32)
        rs_t1 = sb("rs_t1", [128, 1], F32)
        rs_t2 = sb("rs_t2", [128, 1], F32)

        s_v = Sem(nc, es, "s_v")
        s_cst = Sem(nc, es, "s_cst")
        s_ldx = Sem(nc, es, "s_ldx")
        s_wld = Sem(nc, es, "s_wld")
        s_x = [Sem(nc, es, f"s_x{i}") for i in range(4)]
        s_spq = [Sem(nc, es, f"s_spq{i}") for i in range(2)]
        s_spv = [Sem(nc, es, f"s_spv{i}") for i in range(2)]
        s_q = [Sem(nc, es, f"s_q{i}") for i in range(2)]
        s_kv = [Sem(nc, es, f"s_kv{i}") for i in range(2)]
        s_sto = [Sem(nc, es, f"s_sto{i}") for i in range(2)]
        s_xq = [Sem(nc, es, f"s_xq{i}") for i in range(2)]
        s_sr = [Sem(nc, es, f"s_sr{i}") for i in range(2)]
        s_ra = [Sem(nc, es, f"s_ra{i}") for i in range(2)]
        s_rc = [Sem(nc, es, f"s_rc{i}") for i in range(2)]
        s_so = [Sem(nc, es, f"s_so{i}") for i in range(2)]
        s_ln = Sem(nc, es, "s_ln")
        s_ln2 = Sem(nc, es, "s_ln2")
        s_tp = Sem(nc, es, "s_tp")
        s_evT = Sem(nc, es, "s_evT")
        s_proj = Sem(nc, es, "s_proj")
        s_pe = Sem(nc, es, "s_pe")
        s_sw = Sem(nc, es, "s_sw")
        s_rope = Sem(nc, es, "s_rope")
        s_vev = Sem(nc, es, "s_vev")
        s_mm = Sem(nc, es, "s_mm")
        s_exp = Sem(nc, es, "s_exp")
        s_msk = Sem(nc, es, "s_msk")
        s_av = Sem(nc, es, "s_av")
        s_zi = Sem(nc, es, "s_zi")
        s_bc = Sem(nc, es, "s_bc")
        s_o = Sem(nc, es, "s_o")
        s_cc = Sem(nc, es, "s_cc")
        s_lda = Sem(nc, es, "s_lda")
        s_mm3 = Sem(nc, es, "s_mm3")
        s_yev = Sem(nc, es, "s_yev")
        s_r1 = Sem(nc, es, "s_r1")
        s_g = Sem(nc, es, "s_g")
        s_rs = Sem(nc, es, "s_rs")
        s_zd = Sem(nc, es, "s_zd")
        s_zp = Sem(nc, es, "s_zp")
        s_zc = Sem(nc, es, "s_zc")
        s_rp1 = Sem(nc, es, "s_rp1")
        s_pl = Sem(nc, es, "s_pl")

        def queue_rsqrt(E):
            r = [s_v.reserve() for _ in range(9)]
            dn = s_rs.reserve()

            def body(v, r=r, dn=dn):
                v.tensor_scalar(out=rs_t1[:, :], in0=mv[:, 1:2], scalar1=EPS,
                                scalar2=None, op0=ALU.add).then_inc(s_v.h, 1)
                v.wait_ge(s_v.h, r[0])
                v.tensor_scalar(out=rs_t2[:, :], in0=rs_t1[:, :], scalar1=-0.5,
                                scalar2=None, op0=ALU.mult).then_inc(s_v.h, 1)
                v.wait_ge(s_v.h, r[1])
                v.tensor_scalar(out=rstd[:, :].bitcast(I32),
                                in0=rs_t1[:, :].bitcast(I32), scalar1=1,
                                scalar2=None,
                                op0=ALU.logical_shift_right).then_inc(s_v.h, 1)
                v.wait_ge(s_v.h, r[2])
                v.tensor_scalar(out=rstd[:, :].bitcast(I32),
                                in0=rstd[:, :].bitcast(I32), scalar1=-1.0,
                                scalar2=RSQRT_MAGIC, op0=ALU.mult,
                                op1=ALU.add).then_inc(s_v.h, 1)
                v.wait_ge(s_v.h, r[3])
                k = 4
                for it in range(2):
                    v.tensor_tensor(rs_t1[:, :], rstd[:, :], rstd[:, :],
                                    op=ALU.mult).then_inc(s_v.h, 1)
                    v.wait_ge(s_v.h, r[k]); k += 1
                    v.tensor_scalar(out=rs_t1[:, :], in0=rs_t1[:, :],
                                    scalar1=rs_t2[:, :], scalar2=1.5,
                                    op0=ALU.mult, op1=ALU.add).then_inc(s_v.h, 1)
                    v.wait_ge(s_v.h, r[k]); k += 1
                    inst = v.tensor_tensor(rstd[:, :], rstd[:, :], rs_t1[:, :],
                                           op=ALU.mult)
                    if it == 1:
                        inst.then_inc(s_rs.h, 1)
                    else:
                        inst.then_inc(s_v.h, 1)
                        v.wait_ge(s_v.h, r[k]); k += 1
            E.add("vector", body)
            return dn

        def queue_ln(E, src_fn, out_fn):
            t_stats = s_v.reserve()
            t_aggr = s_ln.reserve()

            def v_stats(v, src_fn=src_fn, t_stats=t_stats, t_aggr=t_aggr):
                src = src_fn()
                for i in range(3):
                    v.bn_stats(bnst[:, i * 6:(i + 1) * 6], src[:, i * 512:(i + 1) * 512])
                v.bn_stats(bnst[:, 18:24], src[:, 1536:2048]).then_inc(s_v.h, 1)
                v.wait_ge(s_v.h, t_stats)
                v.bn_aggr(mv[:, :], bnst[:, :]).then_inc(s_ln.h, 1)
                v.wait_ge(s_ln.h, t_aggr)
            E.add("vector", v_stats)
            dn = queue_rsqrt(E)
            t_norm = s_ln2.reserve()

            def v_norm(v, src_fn=src_fn, out_fn=out_fn, dn=dn):
                v.wait_ge(s_rs.h, dn)
                v.tensor_scalar(out=out_fn(), in0=src_fn(),
                                scalar1=mv[:, 0:1], scalar2=rstd[:, :],
                                op0=ALU.subtract, op1=ALU.mult).then_inc(s_ln2.h, 1)
            E.add("vector", v_norm)
            return t_norm

        with contextlib.ExitStack() as es1:
            sb1 = lambda n_, sh, dt: es1.enter_context(nc.sbuf_tensor(n_, sh, dt))
            ps1 = lambda n_, sh, dt: es1.enter_context(nc.psum_tensor(n_, sh, dt))
            xs = sb1("xs", [128, 4, D], F32)
            lnb = sb1("lnb", [128, 2, D], BF16)
            mv4 = sb1("mv4", [128, 4, 2], F32)
            rstd4 = sb1("rstd4", [128, 4], F32)
            rs4a = sb1("rs4a", [128, 4], F32)
            rs4b = sb1("rs4b", [128, 4], F32)
            lnT = sb1("lnT", [128, NCH, SBK], BF16)
            wq_s = sb1("wq_s", [128, NCH, HL * DK], BF16)
            wk_s = sb1("wk_s", [128, NCH, HL * DK], BF16)
            wv_s = sb1("wv_s", [128, NCH, HL * DK], BF16)
            ccs = sb1("ccs", [128, S], BF16)
            ss2s = sb1("ss2s", [128, S], BF16)
            pre = sb1("pre", [128, 2, SBK], BF16)
            swp = sb1("swp", [128, 2, SBK], BF16)
            tmpA = sb1("tmpA", [128, 2, SBK], BF16)
            ropo = sb1("ropo", [128, 2, SBK], BF16)
            vstg = sb1("vstg", [128, 2, HL * DK], BF16)
            pT = ps1("pT", [128, 1024], BF16)
            pPJ0 = ps1("pPJ0", [128, 512], F32)
            pPJ1 = ps1("pPJ1", [128, 512], F32)
            pSw0 = ps1("pSw0", [128, 512], F32)
            pSw1 = ps1("pSw1", [128, 512], F32)
            pPJ = [pPJ0, pPJ1]
            pSw = [pSw0, pSw1]

            blk1 = es1.enter_context(nc.Block("p1"))
            E = Emit()

            n_ldi1 = s_ldx.reserve(16 * 6)

            def ld_init(sync):
                sync.dma_start(out=ids[:, :], in_=ident_in[:, :]).then_inc(s_ldx.h, 16)
                sync.dma_start(out=idsf[:, :], in_=identf_in[:, :]).then_inc(s_ldx.h, 16)
                sync.dma_start(out=psw[:, :], in_=pswap_in[:, :]).then_inc(s_ldx.h, 16)
                sync.dma_start(out=e8s[:, :], in_=e8_in[:, :]).then_inc(s_ldx.h, 16)
                sync.dma_start(out=ccs[:, :], in_=cc_tab[:, :]).then_inc(s_ldx.h, 16)
                sync.dma_start(out=ss2s[:, :], in_=ss2_tab[:, :]).then_inc(s_ldx.h, 16)
            E.add("sync", ld_init)

            n_wld1 = s_wld.reserve(16 * 3 * NCH)

            def ld_w(g):
                for w_d, w_s in ((wqT, wq_s), (wkT, wk_s), (wvT, wv_s)):
                    w_r = w_d.rearrange("(c p) o -> c p o", p=128)
                    for c in range(NCH):
                        g.dma_start(out=w_s[:, c, :], in_=w_r[c]).then_inc(s_wld.h, 16)
            E.add("gpsimd", ld_w)

            n_cst = s_cst.reserve()

            def consts(v):
                v.memset(onecol[:, :], 1.0)
                v.memset(onecf[:, :], 1.0)
                v.memset(onerow[:, :], 1.0)
                v.memset(cexp[:, :], -CEXP).then_inc(s_cst.h, 1)
                v.wait_ge(s_ldx.h, n_ldi1)
                v.wait_ge(s_wld.h, n_wld1)
            E.add("vector", consts)

            def eng_gate1(e, a=n_cst, b_=n_ldi1, c_=n_wld1):
                e.wait_ge(s_cst.h, a)
                e.wait_ge(s_ldx.h, b_)
                e.wait_ge(s_wld.h, c_)
            E.add("scalar", eng_gate1)
            E.add("tensor", eng_gate1)

            x_r = x_in.rearrange("(g p) d -> g p d", p=128)
            ln2_mark = []
            tp_mark = []
            for s_i in range(NSB):
                for tt in range(4):
                    g = 4 * s_i + tt
                    t_x = s_x[tt].reserve(16)
                    need_gx = ln2_mark[4 * (s_i - 1) + tt] if s_i >= 1 else 0

                    def ld_x(sync, g=g, tt=tt, need=need_gx):
                        if need > 0:
                            sync.wait_ge(s_ln2.h, need)
                        sync.dma_start(out=xs[:, tt, :], in_=x_r[g]).then_inc(
                            s_x[tt].h, 16)
                    E.add("sync", ld_x)

                    va = s_v.reserve()
                    t_ag = s_ln.reserve()

                    def v_st(v, tt=tt, t_x=t_x, va=va, t_ag=t_ag):
                        v.wait_ge(s_x[tt].h, t_x)
                        v.wait_ge(s_ln.h, t_ag - 1)
                        for i in range(3):
                            v.bn_stats(bnst[:, i * 6:(i + 1) * 6],
                                       xs[:, tt, i * 512:(i + 1) * 512])
                        v.bn_stats(bnst[:, 18:24], xs[:, tt, 1536:2048]).then_inc(s_v.h, 1)
                        v.wait_ge(s_v.h, va)
                        v.bn_aggr(mv4[:, tt, :], bnst[:, :]).then_inc(s_ln.h, 1)
                    E.add("vector", v_st)

                agg_all = s_ln.n
                r4 = [s_v.reserve() for _ in range(11)]
                t_rs4 = s_rs.reserve()

                def v_rs4(v, agg_all=agg_all, r4=r4):
                    v.wait_ge(s_ln.h, agg_all)
                    v.tensor_scalar(out=rs4a[:, :], in0=mv4[:, :, 1], scalar1=EPS,
                                    scalar2=None, op0=ALU.add).then_inc(s_v.h, 1)
                    v.wait_ge(s_v.h, r4[0])
                    v.tensor_scalar(out=rs4b[:, :], in0=rs4a[:, :], scalar1=-0.5,
                                    scalar2=None, op0=ALU.mult).then_inc(s_v.h, 1)
                    v.wait_ge(s_v.h, r4[1])
                    v.tensor_scalar(out=rstd4[:, :].bitcast(I32),
                                    in0=rs4a[:, :].bitcast(I32), scalar1=1,
                                    scalar2=None,
                                    op0=ALU.logical_shift_right).then_inc(s_v.h, 1)
                    v.wait_ge(s_v.h, r4[2])
                    v.tensor_scalar(out=rstd4[:, :].bitcast(I32),
                                    in0=rstd4[:, :].bitcast(I32), scalar1=-1.0,
                                    scalar2=RSQRT_MAGIC, op0=ALU.mult,
                                    op1=ALU.add).then_inc(s_v.h, 1)
                    v.wait_ge(s_v.h, r4[3])
                    k = 4
                    for itn in range(2):
                        v.tensor_tensor(rs4a[:, :], rstd4[:, :], rstd4[:, :],
                                        op=ALU.mult).then_inc(s_v.h, 1)
                        v.wait_ge(s_v.h, r4[k]); k += 1
                        v.tensor_tensor(rs4a[:, :], rs4a[:, :], rs4b[:, :],
                                        op=ALU.mult).then_inc(s_v.h, 1)
                        v.wait_ge(s_v.h, r4[k]); k += 1
                        if itn == 0:
                            v.tensor_scalar(out=rs4a[:, :], in0=rs4a[:, :], scalar1=1.5,
                                            scalar2=None, op0=ALU.add).then_inc(s_v.h, 1)
                            v.wait_ge(s_v.h, r4[k]); k += 1
                            v.tensor_tensor(rstd4[:, :], rstd4[:, :], rs4a[:, :],
                                            op=ALU.mult).then_inc(s_v.h, 1)
                            v.wait_ge(s_v.h, r4[k]); k += 1
                        else:
                            v.tensor_scalar(out=rs4a[:, :], in0=rs4a[:, :], scalar1=1.5,
                                            scalar2=None, op0=ALU.add).then_inc(s_v.h, 1)
                            v.wait_ge(s_v.h, r4[k]); k += 1
                            v.tensor_tensor(rstd4[:, :], rstd4[:, :], rs4a[:, :],
                                            op=ALU.mult).then_inc(s_rs.h, 1)
                E.add("vector", v_rs4)

                for tt in range(4):
                    g = 4 * s_i + tt
                    need_tp = tp_mark[g - 2] if g >= 2 else 0
                    t_nm = s_ln2.reserve()

                    def v_nm(v, tt=tt, t_rs4=t_rs4, need_tp=need_tp):
                        v.wait_ge(s_rs.h, t_rs4)
                        if need_tp > 0:
                            v.wait_ge(s_tp.h, need_tp)
                        v.tensor_scalar(out=lnb[:, tt % 2, :], in0=xs[:, tt, :],
                                        scalar1=mv4[:, tt, 0:1],
                                        scalar2=rstd4[:, tt:tt + 1],
                                        op0=ALU.subtract, op1=ALU.mult).then_inc(
                                        s_ln2.h, 1)
                    E.add("vector", v_nm)
                    ln2_mark.append(s_ln2.n)

                    for rnd in range(2):
                        need_ev = s_evT.n
                        t_tpv = s_tp.reserve()

                        def t_tp(t, tt=tt, rnd=rnd, need_ln=t_nm, need_ev=need_ev):
                            if rnd == 0:
                                t.wait_ge(s_ln2.h, need_ln)
                            if need_ev > 0:
                                t.wait_ge(s_evT.h, need_ev)
                            for i in range(8):
                                c = rnd * 8 + i
                                mmi = t.transpose(pT[:, i * 128:(i + 1) * 128],
                                                  lnb[:, tt % 2, c * 128:(c + 1) * 128],
                                                  ids[:, :])
                            mmi.then_inc(s_tp.h, 1)
                        E.add("tensor", t_tp)

                        s_evT.reserve()

                        def v_ev(v, tt=tt, rnd=rnd, need=t_tpv):
                            v.wait_ge(s_tp.h, need)
                            v.tensor_copy(
                                lnT[:, rnd * 8:(rnd + 1) * 8, tt * 128:(tt + 1) * 128],
                                pT[:, :].rearrange("p (c f) -> p c f", f=128)).then_inc(
                                s_evT.h, 1)
                        E.add("vector", v_ev)
                    tp_mark.append(s_tp.n)

                ev_all = s_evT.n

                def t_sb_gate(t, need=ev_all):
                    t.wait_ge(s_evT.h, need)
                E.add("tensor", t_sb_gate)

                for h in range(HL):
                    for which, w_s in (("q", wq_s), ("k", wk_s)):
                        pj = s_proj.n % 2
                        need_pe = s_pe.n - 1
                        t_prj = s_proj.reserve()

                        def t_proj(t, h=h, w_s=w_s, pj=pj, need_pe=need_pe):
                            if need_pe > 0:
                                t.wait_ge(s_pe.h, need_pe)
                            for c in range(NCH):
                                mmi = t.matmul(pPJ[pj][:, :], w_s[:, c, h * 128:(h + 1) * 128],
                                               lnT[:, c, :], start=(c == 0),
                                               stop=(c == NCH - 1))
                            mmi.then_inc(s_proj.h, 1)
                        E.add("tensor", t_proj)

                        prslot = s_rope.n % 2
                        need_rope = s_rope.n - 1
                        t_pev = s_pe.reserve()

                        def v_pe(v, need=t_prj, pj=pj, prslot=prslot,
                                 need_rope=need_rope):
                            v.wait_ge(s_proj.h, need)
                            if need_rope > 0:
                                v.wait_ge(s_rope.h, need_rope)
                            v.tensor_copy(pre[:, prslot, :], pPJ[pj][:, :]).then_inc(
                                s_pe.h, 1)
                        E.add("vector", v_pe)

                        need_rp = s_rope.n - 1
                        t_swv = s_sw.reserve()

                        def t_sw(t, need=t_pev, prslot=prslot, need_rp=need_rp):
                            t.wait_ge(s_pe.h, need)
                            if need_rp > 0:
                                t.wait_ge(s_rope.h, need_rp)
                            t.matmul(pSw[prslot][:, :], psw[:, :], pre[:, prslot, :],
                                     start=True, stop=True).then_inc(s_sw.h, 1)
                        E.add("tensor", t_sw)

                        ridx = s_rope.n
                        rslot = ridx % 2
                        va = s_v.reserve(2)
                        need_wr = s_spq[rslot].n if ridx >= 2 else 0
                        t_rp = s_rope.reserve()

                        def v_rope(v, s_i=s_i, rslot=rslot, need=t_swv, va=va,
                                   need_wr=need_wr):
                            v.wait_ge(s_sw.h, need)
                            v.tensor_copy(swp[:, rslot, :], pSw[rslot][:, :])
                            v.tensor_tensor(tmpA[:, rslot, :], pre[:, rslot, :],
                                            ccs[:, s_i * SBK:(s_i + 1) * SBK],
                                            op=ALU.mult).then_inc(s_v.h, 1)
                            v.wait_ge(s_v.h, va - 1)
                            v.tensor_tensor(swp[:, rslot, :], swp[:, rslot, :],
                                            ss2s[:, s_i * SBK:(s_i + 1) * SBK],
                                            op=ALU.mult).then_inc(s_v.h, 1)
                            v.wait_ge(s_v.h, va)
                            if need_wr > 0:
                                v.wait_ge(s_spq[rslot].h, need_wr)
                            v.tensor_tensor(ropo[:, rslot, :], tmpA[:, rslot, :],
                                            swp[:, rslot, :], op=ALU.add).then_inc(s_rope.h, 1)
                        E.add("vector", v_rope)

                        s_spq[rslot].reserve(16)

                        def sc_spill(scl, s_i=s_i, h=h, which=which, rslot=rslot,
                                     need=t_rp):
                            scl.wait_ge(s_rope.h, need)
                            if which == "q":
                                dst = qsp[h][:, s_i, :]
                            else:
                                dst = ksp[h][:, 4 * s_i:4 * s_i + 4, :].rearrange(
                                    "p a b -> p (a b)")
                            scl.dma_start(out=dst, in_=ropo[:, rslot, :]).then_inc(
                                s_spq[rslot].h, 16)
                        E.add("scalar", sc_spill)

                for tt in range(4):
                    g = 4 * s_i + tt
                    pj = s_proj.n % 2
                    need_pe = s_pe.n - 1
                    t_vp = s_proj.reserve()

                    def t_v(t, tt=tt, pj=pj, need_pe=need_pe):
                        if need_pe > 0:
                            t.wait_ge(s_pe.h, need_pe)
                        for c in range(NCH):
                            mmi = t.matmul(pPJ[pj][:, :], lnT[:, c, tt * 128:(tt + 1) * 128],
                                           wv_s[:, c, :], start=(c == 0), stop=(c == NCH - 1))
                        mmi.then_inc(s_proj.h, 1)
                    E.add("tensor", t_v)

                    vslot = g % 2
                    need_wr = s_spv[vslot].n if g >= 2 else 0
                    t_vev = s_pe.reserve()

                    def v_vev(v, vslot=vslot, pj=pj, need=t_vp, need_wr=need_wr):
                        v.wait_ge(s_proj.h, need)
                        if need_wr > 0:
                            v.wait_ge(s_spv[vslot].h, need_wr)
                        v.tensor_copy(vstg[:, vslot, :], pPJ[pj][:, :]).then_inc(s_pe.h, 1)
                    E.add("vector", v_vev)

                    s_spv[vslot].reserve(16 * HL)

                    def sc_vsp(scl, g=g, vslot=vslot, need=t_vev):
                        scl.wait_ge(s_pe.h, need)
                        for h in range(HL):
                            scl.dma_start(
                                out=vsp[h][:, g, :],
                                in_=vstg[:, vslot, h * 128:(h + 1) * 128]).then_inc(
                                s_spv[vslot].h, 16)
                    E.add("scalar", sc_vsp)

            def p1_fin(scl, a0=s_spq[0].n, a1=s_spq[1].n, b0=s_spv[0].n, b1=s_spv[1].n):
                scl.wait_ge(s_spq[0].h, a0)
                scl.wait_ge(s_spq[1].h, a1)
                scl.wait_ge(s_spv[0].h, b0)
                scl.wait_ge(s_spv[1].h, b1)
            E.add("scalar", p1_fin)
            E.flush(blk1)

        with contextlib.ExitStack() as es2:
            sb2 = lambda n_, sh, dt: es2.enter_context(nc.sbuf_tensor(n_, sh, dt))
            ps2 = lambda n_, sh, dt: es2.enter_context(nc.psum_tensor(n_, sh, dt))
            msk = sb2("msk", [128, 4, 512], BF16)
            qbuf = sb2("qbuf", [128, 2, 512], BF16)
            kbuf = sb2("kbuf", [128, 2, CKT * 128], BF16)
            vbuf = sb2("vbuf", [128, 2, CKT * 128], BF16)
            pb = sb2("pb", [128, 4, 512], BF16)
            zrow = sb2("zrow", [1, 512], F32)
            binv = sb2("binv", [128, 512], F32)
            oout = sb2("oout", [128, 2, 512], BF16)
            pS = ps2("pS", [128, 4, 512], F32)
            pO0 = ps2("pO0", [128, 512], F32)
            pO1 = ps2("pO1", [128, 512], F32)
            pOb = [pO0, pO1]
            pZ = ps2("pZ", [128, 512], F32)
            pB = ps2("pB", [128, 512], F32)

            blk2 = es2.enter_context(nc.Block("p2"))
            E = Emit()

            n_mskld = s_ldx.reserve(16 * 4)

            def ld_msk(sync):
                m_r = masks_in.rearrange("(t p) q -> t p q", p=128)
                for i in range(4):
                    sync.dma_start(out=msk[:, i, :], in_=m_r[i]).then_inc(s_ldx.h, 16)
            E.add("sync", ld_msk)

            def eng_gate2(e, need=n_mskld):
                e.wait_ge(s_ldx.h, need)
            E.add("vector", eng_gate2)
            E.add("tensor", eng_gate2)
            E.add("scalar", eng_gate2)

            mm_mark_iter = []
            kcons_mark = []
            zr_mark = []
            gchunk = 0
            for qb in range(NSB):
                nkt = 4 * (qb + 1)
                nchunks = (nkt + CKT - 1) // CKT
                for h in range(HL):
                    it = qb * HL + h
                    qslot = it % 2
                    need_qg = mm_mark_iter[it - 2] if it >= 2 else 0
                    q_mark = s_q[qslot].reserve(16)

                    def ld_q(sync, qb=qb, h=h, qslot=qslot, need=need_qg):
                        if need > 0:
                            sync.wait_ge(s_mm.h, need)
                        sync.dma_start(out=qbuf[:, qslot, :],
                                       in_=qsp[h][:, qb, :]).then_inc(s_q[qslot].h, 16)
                    E.add("sync", ld_q)

                    ch_base = gchunk
                    ld_marks = {}

                    def emit_ld_kv(ch, qb=qb, h=h, nkt=nkt, ch_base=ch_base,
                                   ld_marks=ld_marks):
                        nonlocal gchunk
                        k0 = ch * CKT
                        nk = min(CKT, nkt - k0)
                        gc = ch_base + ch
                        cs = gc % 2
                        need = kcons_mark[gc - 2] if gc >= 2 else 0
                        ld_marks[ch] = s_kv[cs].reserve(32)

                        def ld_kv(sync, h=h, k0=k0, nk=nk, cs=cs, need=need):
                            if need > 0:
                                sync.wait_ge(s_av.h, need)
                            sync.dma_start(
                                out=kbuf[:, cs, 0:nk * 128],
                                in_=ksp[h][:, k0:k0 + nk, :].rearrange(
                                    "p a b -> p (a b)")).then_inc(s_kv[cs].h, 16)
                            sync.dma_start(
                                out=vbuf[:, cs, 0:nk * 128],
                                in_=vsp[h][:, k0:k0 + nk, :].rearrange(
                                    "p a b -> p (a b)")).then_inc(s_kv[cs].h, 16)
                        E.add("sync", ld_kv)
                        gchunk += 1

                    emit_ld_kv(0)
                    if nchunks > 1:
                        emit_ld_kv(1)

                    o_guard = s_o.n - 1
                    pO = pOb[it % 2]

                    def emit_t_s(kt):
                        ch = kt // CKT
                        cs = (ch_base + ch) % 2
                        ko = kt - ch * CKT
                        sslot = kt % 4
                        need_exp = s_exp.n - 1
                        t_sv = s_mm.reserve()

                        def t_s(t, kt=kt, cs=cs, ko=ko, sslot=sslot, qslot=qslot,
                                need_ld=ld_marks[ch], need_exp=need_exp,
                                o_guard=o_guard, q_mark=q_mark):
                            if kt == 0:
                                t.wait_ge(s_q[qslot].h, q_mark)
                                t.wait_ge(s_kv[cs].h, need_ld)
                            elif ko == 0:
                                t.wait_ge(s_kv[cs].h, need_ld)
                            if need_exp > 0:
                                t.wait_ge(s_exp.h, need_exp)
                            t.matmul(pS[:, sslot, :],
                                     kbuf[:, cs, ko * 128:(ko + 1) * 128],
                                     qbuf[:, qslot, :], start=True,
                                     stop=True).then_inc(s_mm.h, 1)
                        E.add("tensor", t_s)
                        return t_sv

                    ts_marks = {0: emit_t_s(0), 1: emit_t_s(1)}
                    zprev = zr_mark[it - 1] if it >= 1 else 0

                    for bp in range(nkt // 2):
                        kt0 = 2 * bp
                        pp = (bp % 2) * 2
                        need_z = s_zp.n - 2
                        t_ev = s_exp.reserve()

                        def s_e2(scl, pp=pp, need_mm=ts_marks[kt0 + 1],
                                 need_z=need_z):
                            scl.wait_ge(s_mm.h, need_mm)
                            if need_z > 0:
                                scl.wait_ge(s_zp.h, need_z)
                            scl.activation(out=pb[:, pp:pp + 2, :],
                                           in_=pS[:, pp:pp + 2, :],
                                           func=AF.Exp, bias=cexp[:, :],
                                           scale=SCL).then_inc(s_exp.h, 1)
                        E.add("scalar", s_e2)

                        for j in range(2):
                            kt = kt0 + j
                            diag = kt >= 4 * qb
                            if diag:
                                s_msk.reserve()

                                def v_m(v, qb=qb, kt=kt, pslot=pp + j, need=t_ev):
                                    v.wait_ge(s_exp.h, need)
                                    v.tensor_tensor(pb[:, pslot, :], pb[:, pslot, :],
                                                    msk[:, kt - 4 * qb, :],
                                                    op=ALU.mult).then_inc(s_msk.h, 1)
                                E.add("vector", v_m)

                        if kt0 + 2 < nkt:
                            ts_marks[kt0 + 2] = emit_t_s(kt0 + 2)
                            ts_marks[kt0 + 3] = emit_t_s(kt0 + 3)

                        for j in range(2):
                            kt = kt0 + j
                            ch = kt // CKT
                            cs = (ch_base + ch) % 2
                            ko = kt - ch * CKT
                            diag = kt >= 4 * qb
                            lastk = (ko == CKT - 1 or kt == nkt - 1)
                            need_m = s_msk.n
                            t_avv = s_av.reserve()

                            def t_av(t, kt=kt, cs=cs, ko=ko, pslot=pp + j, diag=diag,
                                     need_e=t_ev, need_m=need_m, nkt=nkt,
                                     pO=pO, o_guard=o_guard):
                                if kt == 0 and o_guard > 0:
                                    t.wait_ge(s_o.h, o_guard)
                                if diag:
                                    t.wait_ge(s_msk.h, need_m)
                                else:
                                    t.wait_ge(s_exp.h, need_e)
                                t.matmul(pO[:, :], vbuf[:, cs, ko * 128:(ko + 1) * 128],
                                         pb[:, pslot, :], start=(kt == 0),
                                         stop=(kt == nkt - 1)).then_inc(s_av.h, 1)
                            E.add("tensor", t_av)
                            if lastk:
                                kcons_mark.append(s_av.n)
                                if ch + 2 < nchunks:
                                    emit_ld_kv(ch + 2)

                        for j in range(2):
                            kt = kt0 + j
                            s_zp.reserve()

                            def t_z(t, kt=kt, pslot=pp + j, zprev=zprev, nkt=nkt):
                                if kt == 0 and zprev > 0:
                                    t.wait_ge(s_zd.h, zprev)
                                t.matmul(pZ[0:1, :], onecol[:, :], pb[:, pslot, :],
                                         start=(kt == 0),
                                         stop=(kt == nkt - 1)).then_inc(s_zp.h, 1)
                            E.add("tensor", t_z)

                    mm_mark_iter.append(s_mm.n)

                    zm_all = s_zp.n
                    prev_bc = s_bc.n
                    t_zr = s_zd.reserve()

                    def sc_zr(scl, zm_all=zm_all, prev_bc=prev_bc):
                        scl.wait_ge(s_zp.h, zm_all)
                        if prev_bc > 0:
                            scl.wait_ge(s_bc.h, prev_bc)
                        scl.copy(out=zrow[:, :], in_=pZ[0:1, :]).then_inc(s_zd.h, 1)
                    E.add("scalar", sc_zr)
                    zr_mark.append(s_zd.n)

                    prev_zi = s_zi.n
                    t_bcv = s_bc.reserve()

                    def t_bc(t, t_zr=t_zr, prev_zi=prev_zi):
                        t.wait_ge(s_zd.h, t_zr)
                        if prev_zi > 0:
                            t.wait_ge(s_zi.h, prev_zi)
                        t.matmul(pB[:, :], onerow[:, :], zrow[:, :],
                                 start=True, stop=True).then_inc(s_bc.h, 1)
                    E.add("tensor", t_bc)

                    t_ziv = s_zi.reserve()

                    def v_zi(v, t_bcv=t_bcv):
                        v.wait_ge(s_bc.h, t_bcv)
                        v.reciprocal(binv[:, :], pB[:, :]).then_inc(s_zi.h, 1)
                    E.add("vector", v_zi)

                    oslot = it % 2
                    need_st = s_sto[oslot].n if it >= 2 else 0
                    t_ov = s_o.reserve()

                    def v_o(v, oslot=oslot, t_ziv=t_ziv, need_st=need_st,
                            need_av=s_av.n, pO=pOb[it % 2]):
                        v.wait_ge(s_zi.h, t_ziv)
                        v.wait_ge(s_av.h, need_av)
                        if need_st > 0:
                            v.wait_ge(s_sto[oslot].h, need_st)
                        v.tensor_tensor(oout[:, oslot, :], pO[:, :], binv[:, :],
                                        op=ALU.mult).then_inc(s_o.h, 1)
                    E.add("vector", v_o)

                    s_sto[oslot].reserve(16)

                    def gp_o(g, qb=qb, h=h, oslot=oslot, need=t_ov):
                        g.wait_ge(s_o.h, need)
                        k = qb // 4
                        g.dma_start(
                            out=ccg_in[k][h * 128:(h + 1) * 128,
                                          (qb % 4) * 512:(qb % 4 + 1) * 512],
                            in_=oout[:, oslot, :]).then_inc(s_sto[oslot].h, 16)
                    E.add("gpsimd", gp_o)

                if qb % 4 == 3:
                    s_cc.reserve()

                    def g_cc(g, k=qb // 4, need0=s_sto[0].n, need1=s_sto[1].n):
                        g.wait_ge(s_sto[0].h, need0)
                        g.wait_ge(s_sto[1].h, need1)
                        g.collective_compute(
                            "AllGather", ALU.bypass,
                            replica_groups=[[0, 1, 2, 3], [4, 5, 6, 7]],
                            ins=[ccg_in[k].ap().opt()],
                            outs=[ccg_out[k].ap().opt()]).then_inc(s_cc.h, 1)
                    E.add("gpsimd", g_cc)
            E.flush(blk2)

        with contextlib.ExitStack() as es3:
            sb3 = lambda n_, sh, dt: es3.enter_context(nc.sbuf_tensor(n_, sh, dt))
            ps3 = lambda n_, sh, dt: es3.enter_context(nc.psum_tensor(n_, sh, dt))
            wo_s = sb3("wo_s", [128, NCH, D], BF16)
            aT = sb3("aT", [128, NCH, 512], BF16)
            y1T = sb3("y1T", [128, NCH, 512], BF16)
            y1 = sb3("y1", [128, 4, D], BF16)
            xqb = sb3("xqb", [128, 2, D], F32)
            t3 = sb3("t3", [128, D], F32)
            r1b = sb3("r1b", [128, 2, D], F32)
            pY = ps3("pY", [128, 512], F32)
            pT3 = ps3("pT3", [128, 1024], BF16)

            blk3 = es3.enter_context(nc.Block("p3"))
            E = Emit()

            n_wld3 = s_wld.reserve(16 * NCH)

            def ld_wo(g):
                w_r = woT.rearrange("(c p) o -> c p o", p=128)
                for c in range(NCH):
                    g.dma_start(out=wo_s[:, c, :], in_=w_r[c]).then_inc(s_wld.h, 16)
            E.add("gpsimd", ld_wo)

            def t_gate3(t, need=n_wld3):
                t.wait_ge(s_wld.h, need)
            E.add("tensor", t_gate3)

            xq_r = xq_in.rearrange("(g p) d -> g p d", p=128)
            r1_r = r1sp.rearrange("(g p) d -> g p d", p=128)

            r1_mark = []
            mm3_mark = []
            for tb in range(NBLK):
                E.flush(blk3)
                need_mm = mm3_mark[tb - 1] if tb >= 1 else 0
                lda_tb = s_lda.reserve(16 * NCH)

                def sync_aT(sync, tb=tb, need_mm=need_mm):
                    if need_mm > 0:
                        sync.wait_ge(s_mm3.h, need_mm)
                    pid = sync.partition_id()
                    for core in nc.Switch(engines=[sync], index=[pid], n=8):
                        tq = core % 4
                        sync.wait_ge(s_cc.h, tq + 1)
                        co = ccg_out[tq]
                        for cb in range(NCH):
                            sync.dma_start(
                                out=aT[:, cb, :],
                                in_=co[cb * 128:(cb + 1) * 128,
                                       tb * 512:(tb + 1) * 512]).then_inc(s_lda.h, 16)
                blk3.sync(sync_aT)

                for oc in range(NCH):
                    need_ev = s_yev.n
                    t_mmv = s_mm3.reserve()

                    def t_op(t, oc=oc, need_ld=lda_tb, need_ev=need_ev):
                        if oc == 0:
                            t.wait_ge(s_lda.h, need_ld)
                        if need_ev > 0:
                            t.wait_ge(s_yev.h, need_ev)
                        for c in range(NCH):
                            mmi = t.matmul(pY[:, :], wo_s[:, c, oc * 128:(oc + 1) * 128],
                                           aT[:, c, :], start=(c == 0), stop=(c == NCH - 1))
                        mmi.then_inc(s_mm3.h, 1)
                    E.add("tensor", t_op)

                    s_yev.reserve()

                    def v_ye(v, oc=oc, need=t_mmv):
                        v.wait_ge(s_mm3.h, need)
                        v.tensor_copy(y1T[:, oc, :], pY[:, :]).then_inc(s_yev.h, 1)
                    E.add("vector", v_ye)
                mm3_mark.append(s_mm3.n)

                for rnd in range(8):
                    need_ev = s_yev.n
                    need_pt = s_evT.n
                    t_tpv = s_tp.reserve()

                    def t_tp3(t, rnd=rnd, need_ev=need_ev, need_pt=need_pt):
                        if rnd == 0:
                            t.wait_ge(s_yev.h, need_ev)
                        if need_pt > 0:
                            t.wait_ge(s_evT.h, need_pt)
                        for i in range(8):
                            j = rnd * 8 + i
                            oc, sub = j // 4, j % 4
                            mmi = t.transpose(pT3[:, i * 128:(i + 1) * 128],
                                              y1T[:, oc, sub * 128:(sub + 1) * 128],
                                              ids[:, :])
                        mmi.then_inc(s_tp.h, 1)
                    E.add("tensor", t_tp3)

                    s_evT.reserve()

                    def v_pt3(v, rnd=rnd, need=t_tpv):
                        v.wait_ge(s_tp.h, need)
                        oc0 = rnd * 2
                        v.tensor_copy(
                            y1[:, :, oc0 * 128:(oc0 + 2) * 128].rearrange(
                                "p s (c f) -> p c s f", f=128),
                            pT3[:, :].rearrange("p (c s f) -> p c s f",
                                                c=2, s=4)).then_inc(s_evT.h, 1)
                    E.add("vector", v_pt3)

                for sub in range(4):
                    g = tb * 4 + sub
                    need_r1g = r1_mark[g - 2] if g >= 2 else 0
                    t_xqv = s_xq[g % 2].reserve(16)

                    def ld_xq(sync, g=g, need=need_r1g):
                        if need > 0:
                            sync.wait_ge(s_r1.h, need)
                        sync.dma_start(out=xqb[:, g % 2, :], in_=xq_r[g]).then_inc(
                            s_xq[g % 2].h, 16)
                    E.add("sync", ld_xq)

                    need_t = s_evT.n

                    def v_lngate(v, need_t=need_t):
                        v.wait_ge(s_evT.h, need_t)
                    E.add("vector", v_lngate)
                    tnorm = queue_ln(E, (lambda sub=sub: y1[:, sub, :]),
                                     (lambda: t3[:, :]))

                    need_wr = s_sr[g % 2].n if g >= 2 else 0
                    t_r1v = s_r1.reserve()

                    def v_r1(v, g=g, need_ld=t_xqv, need_wr=need_wr, tnorm=tnorm):
                        v.wait_ge(s_xq[g % 2].h, need_ld)
                        v.wait_ge(s_ln2.h, tnorm)
                        if need_wr > 0:
                            v.wait_ge(s_sr[g % 2].h, need_wr)
                        v.tensor_tensor(r1b[:, g % 2, :], t3[:, :],
                                        xqb[:, g % 2, :], op=ALU.add).then_inc(s_r1.h, 1)
                    E.add("vector", v_r1)
                    r1_mark.append(s_r1.n)

                    s_sr[g % 2].reserve(16)

                    def sc_r1(scl, g=g, need=t_r1v):
                        scl.wait_ge(s_r1.h, need)
                        scl.dma_start(out=r1_r[g], in_=r1b[:, g % 2, :]).then_inc(
                            s_sr[g % 2].h, 16)
                    E.add("scalar", sc_r1)

            def p3_fin(scl, a0=s_sr[0].n, a1=s_sr[1].n):
                scl.wait_ge(s_sr[0].h, a0)
                scl.wait_ge(s_sr[1].h, a1)
            E.add("scalar", p3_fin)
            E.flush(blk3)

        with contextlib.ExitStack() as es4:
            sb4 = lambda n_, sh, dt: es4.enter_context(nc.sbuf_tensor(n_, sh, dt))
            ps4 = lambda n_, sh, dt: es4.enter_context(nc.psum_tensor(n_, sh, dt))
            wb_s = sb4("wb_s", [128, NCH, D], BF16)
            awr_s = sb4("awr_s", [128, NCH, 72], BF16)
            bfl_s = sb4("bfl_s", [64, D], BF16)
            bb_s = sb4("bb_s", [128, 16], F32)
            r1a = sb4("r1a", [128, 2, D], F32)
            r1c = sb4("r1c", [128, 2, D], F32)
            ln2 = sb4("ln2", [128, D], BF16)
            ln2T = sb4("ln2T", [128, NCH, 512], BF16)
            hbuf = sb4("hbuf", [128, 512], BF16)
            hbufL = sb4("hbufL", [8, 512], F32)
            lgt = sb4("lgt", [128, 4, 8], F32)
            negm1 = sb4("negm1", [128, 1], F32)
            m2b = sb4("m2b", [128, 1], F32)
            eqb = sb4("eqb", [128, 8], F32)
            lmk = sb4("lmk", [128, 8], F32)
            pexp = sb4("pexp", [128, 8], F32)
            indb = sb4("indb", [128, 8], F32)
            wgt = sb4("wgt", [128, 8], F32)
            zg = sb4("zg", [128, 1], F32)
            zgi = sb4("zgi", [128, 1], F32)
            wn = sb4("wn", [128, 8], F32)
            wnT = sb4("wnT", [8, 128], BF16)
            wexpT = sb4("wexpT", [64, 512], BF16)
            h2 = sb4("h2", [64, 512], BF16)
            ffnT = sb4("ffnT", [128, NCH, 512], BF16)
            ffn = sb4("ffn", [128, 4, D], BF16)
            t4 = sb4("t4", [128, D], F32)
            r2b = sb4("r2b", [128, 2, D], F32)
            pY4 = ps4("pY4", [128, 512], F32)
            pT4 = ps4("pT4", [128, 1024], BF16)
            pH = ps4("pH", [128, 512], F32)
            pG = ps4("pG", [128, 512], F32)

            blk4 = es4.enter_context(nc.Block("p4"))
            E = Emit()

            n_wld4 = s_wld.reserve(16 * (2 * NCH + 1))

            def ld_w4(g):
                w_r = wbT.rearrange("(c p) o -> c p o", p=128)
                for c in range(NCH):
                    g.dma_start(out=wb_s[:, c, :], in_=w_r[c]).then_inc(s_wld.h, 16)
                a_r = awrT.rearrange("(c p) o -> c p o", p=128)
                for c in range(NCH):
                    g.dma_start(out=awr_s[:, c, :], in_=a_r[c]).then_inc(s_wld.h, 16)
                g.dma_start(out=bfl_s[:, :], in_=bflT[:, :]).then_inc(s_wld.h, 16)
            E.add("gpsimd", ld_w4)

            n_ld4 = s_ldx.reserve(16)

            def ld_bb(sync):
                sync.dma_start(out=bb_s[:, :], in_=bbase[:, :]).then_inc(s_ldx.h, 16)
            E.add("sync", ld_bb)

            def gate4(e, a=n_wld4, b_=n_ld4):
                e.wait_ge(s_wld.h, a)
                e.wait_ge(s_ldx.h, b_)
            E.add("tensor", gate4)
            E.add("vector", gate4)
            E.add("scalar", gate4)

            out_r = out_q.rearrange("(g p) d -> g p d", p=128)

            ln3_mark = []
            r2_mark = []
            for tb in range(NBLK):
                for sub in range(4):
                    g = tb * 4 + sub
                    need_lg = ln3_mark[g - 2] if g >= 2 else 0
                    t_rav = s_ra[g % 2].reserve(16)

                    def ld_r1(sync, g=g, need=need_lg):
                        if need > 0:
                            sync.wait_ge(s_ln2.h, need)
                        sync.dma_start(out=r1a[:, g % 2, :], in_=r1_r[g]).then_inc(
                            s_ra[g % 2].h, 16)
                    E.add("sync", ld_r1)

                    need_tp = s_tp.n

                    def v_lngate4(v, g=g, t_rav=t_rav, need_tp=need_tp):
                        v.wait_ge(s_ra[g % 2].h, t_rav)
                        if need_tp > 0:
                            v.wait_ge(s_tp.h, need_tp)
                    E.add("vector", v_lngate4)
                    queue_ln(E, (lambda g=g: r1a[:, g % 2, :]), (lambda: ln2[:, :]))
                    ln3_mark.append(s_ln2.n)

                    for rnd in range(2):
                        need_ln = s_ln2.n
                        need_ev = s_evT.n
                        t_tpv = s_tp.reserve()

                        def t_tp4(t, rnd=rnd, need_ln=need_ln, need_ev=need_ev):
                            if rnd == 0:
                                t.wait_ge(s_ln2.h, need_ln)
                            if need_ev > 0:
                                t.wait_ge(s_evT.h, need_ev)
                            for i in range(8):
                                c = rnd * 8 + i
                                mmi = t.transpose(pT4[:, i * 128:(i + 1) * 128],
                                                  ln2[:, c * 128:(c + 1) * 128], ids[:, :])
                            mmi.then_inc(s_tp.h, 1)
                        E.add("tensor", t_tp4)

                        s_evT.reserve()

                        def v_ev4(v, sub=sub, rnd=rnd, need=t_tpv):
                            v.wait_ge(s_tp.h, need)
                            v.tensor_copy(
                                ln2T[:, rnd * 8:(rnd + 1) * 8, sub * 128:(sub + 1) * 128],
                                pT4[:, :].rearrange("p (c f) -> p c f", f=128)).then_inc(
                                s_evT.h, 1)
                        E.add("vector", v_ev4)

                need_g = s_g.n
                t_awrv = s_mm3.reserve()

                def t_awr(t, need=s_evT.n, need_g=need_g):
                    t.wait_ge(s_evT.h, need)
                    if need_g > 0:
                        t.wait_ge(s_g.h, need_g)
                    for c in range(NCH):
                        mmi = t.matmul(pH[0:72, :], awr_s[:, c, :], ln2T[:, c, :],
                                       start=(c == 0), stop=(c == NCH - 1))
                    mmi.then_inc(s_mm3.h, 1)
                E.add("tensor", t_awr)

                va = s_v.reserve()
                t_gv = s_g.reserve()

                def v_hev(v, need=t_awrv, va=va):
                    v.wait_ge(s_mm3.h, need)
                    v.tensor_copy(hbuf[0:64, :], pH[0:64, :]).then_inc(s_v.h, 1)
                    v.wait_ge(s_v.h, va)
                    v.tensor_copy(hbufL[:, :], pH[64:72, :]).then_inc(s_g.h, 1)
                E.add("vector", v_hev)

                need_cons = s_lda.n
                for sub in range(4):
                    t_lgv = s_tp.reserve()

                    def t_lg(t, sub=sub, need=t_gv, need_cons=need_cons):
                        if sub == 0:
                            t.wait_ge(s_g.h, need)
                            if need_cons > 0:
                                t.wait_ge(s_lda.h, need_cons)
                        t.transpose(pG[:, sub * 8:(sub + 1) * 8],
                                    hbufL[:, sub * 128:(sub + 1) * 128],
                                    idsf[0:8, 0:8]).then_inc(s_tp.h, 1)
                    E.add("tensor", t_lg)

                lg_need = s_tp.n
                t_lgc = s_lda.reserve()

                def v_lg(v, need=lg_need):
                    v.wait_ge(s_tp.h, need)
                    v.tensor_copy(
                        lgt[:, :, :], pG[:, 0:32].rearrange(
                            "p (s e) -> p s e", e=8)).then_inc(s_lda.h, 1)
                E.add("vector", v_lg)

                for sub in range(4):
                    va = s_v.reserve(4)
                    t_ind = s_msk.reserve()

                    def v_g1(v, sub=sub, need=t_lgc, va=va):
                        if sub == 0:
                            v.wait_ge(s_lda.h, need)
                        l = lgt[:, sub, :]
                        v.reduce_max(negm1[:, :], l, axis=AX.X,
                                     negate=True).then_inc(s_v.h, 1)
                        v.wait_ge(s_v.h, va - 3)
                        v.tensor_scalar(out=eqb[:, :], in0=l, scalar1=negm1[:, :],
                                        scalar2=0.0, op0=ALU.add,
                                        op1=ALU.is_ge).then_inc(s_v.h, 1)
                        v.wait_ge(s_v.h, va - 2)
                        v.scalar_tensor_tensor(out=lmk[:, :], in0=eqb[:, :],
                                               scalar=-30000.0, in1=l,
                                               op0=ALU.mult, op1=ALU.add).then_inc(s_v.h, 1)
                        v.wait_ge(s_v.h, va - 1)
                        v.reduce_max(m2b[:, :], lmk[:, :], axis=AX.X).then_inc(s_v.h, 1)
                        v.wait_ge(s_v.h, va)
                        v.tensor_scalar(out=indb[:, :], in0=l, scalar1=m2b[:, :],
                                        scalar2=None, op0=ALU.is_ge).then_inc(s_msk.h, 1)
                    E.add("vector", v_g1)

                    t_px = s_rs.reserve()

                    def s_px(scl, sub=sub, need=t_ind):
                        scl.wait_ge(s_msk.h, need)
                        scl.activation(out=pexp[:, :], in_=lgt[:, sub, :],
                                       func=AF.Exp, bias=negm1[:, :],
                                       scale=1.0).then_inc(s_rs.h, 1)
                    E.add("scalar", s_px)

                    va = s_v.reserve(3)
                    t_wnv = s_ln2.reserve()

                    def v_g2(v, need=t_px, va=va):
                        v.wait_ge(s_rs.h, need)
                        v.tensor_tensor(wgt[:, :], indb[:, :], pexp[:, :],
                                        op=ALU.mult).then_inc(s_v.h, 1)
                        v.wait_ge(s_v.h, va - 2)
                        v.reduce_sum(zg[:, :], wgt[:, :], axis=AX.X).then_inc(s_v.h, 1)
                        v.wait_ge(s_v.h, va - 1)
                        v.reciprocal(zgi[:, :], zg[:, :]).then_inc(s_v.h, 1)
                        v.wait_ge(s_v.h, va)
                        v.tensor_scalar(out=wn[:, :], in0=wgt[:, :],
                                        scalar1=zgi[:, :], scalar2=LORA_SCALE,
                                        op0=ALU.mult, op1=ALU.mult).then_inc(s_ln2.h, 1)
                    E.add("vector", v_g2)

                    need_ev = s_yev.n
                    t_wtv = s_tp.reserve()

                    def t_wnT(t, need=t_wnv, need_ev=need_ev):
                        t.wait_ge(s_ln2.h, need)
                        if need_ev > 0:
                            t.wait_ge(s_yev.h, need_ev)
                        t.transpose(pG[0:8, 128:256], wn[:, :],
                                    idsf[:, :]).then_inc(s_tp.h, 1)
                    E.add("tensor", t_wnT)

                    t_wcv = s_r1.reserve()

                    def v_wnT(v, need=t_wtv):
                        v.wait_ge(s_tp.h, need)
                        v.tensor_copy(wnT[:, :], pG[0:8, 128:256]).then_inc(s_r1.h, 1)
                    E.add("vector", v_wnT)

                    t_e8v = s_bc.reserve()

                    def t_e8(t, need=t_wcv):
                        t.wait_ge(s_r1.h, need)
                        t.matmul(pG[0:64, 256:384], e8s[:, :], wnT[:, :],
                                 start=True, stop=True).then_inc(s_bc.h, 1)
                    E.add("tensor", t_e8)

                    s_yev.reserve()

                    def v_we(v, sub=sub, need=t_e8v):
                        v.wait_ge(s_bc.h, need)
                        v.tensor_copy(wexpT[:, sub * 128:(sub + 1) * 128],
                                      pG[0:64, 256:384]).then_inc(s_yev.h, 1)
                    E.add("vector", v_we)

                t_h2v = s_g.reserve()

                def v_h2(v, need=s_yev.n):
                    v.wait_ge(s_yev.h, need)
                    v.tensor_tensor(h2[:, :], hbuf[0:64, :], wexpT[:, :],
                                    op=ALU.mult).then_inc(s_g.h, 1)
                E.add("vector", v_h2)

                for oc in range(NCH):
                    need_ev = s_exp.n
                    t_fv = s_mm.reserve()

                    def t_ffn(t, oc=oc, need_h2=t_h2v, need_ev=need_ev):
                        if oc == 0:
                            t.wait_ge(s_g.h, need_h2)
                        if need_ev > 0:
                            t.wait_ge(s_exp.h, need_ev)
                        for c in range(NCH):
                            t.matmul(pY4[:, :], wb_s[:, c, oc * 128:(oc + 1) * 128],
                                     ln2T[:, c, :], start=(c == 0), stop=False)
                        t.matmul(pY4[:, :], bfl_s[:, oc * 128:(oc + 1) * 128],
                                 h2[:, :], start=False, stop=True).then_inc(s_mm.h, 1)
                    E.add("tensor", t_ffn)

                    s_exp.reserve()

                    def s_fev(scl, oc=oc, need=t_fv):
                        scl.wait_ge(s_mm.h, need)
                        scl.activation(out=ffnT[:, oc, :], in_=pY4[:, :],
                                       func=AF.Identity, bias=bb_s[:, oc:oc + 1],
                                       scale=1.0).then_inc(s_exp.h, 1)
                    E.add("scalar", s_fev)

                for rnd in range(8):
                    need_ev = s_exp.n
                    need_pt = s_evT.n
                    t_tpv = s_tp.reserve()

                    def t_tpf(t, rnd=rnd, need_ev=need_ev, need_pt=need_pt):
                        if rnd == 0:
                            t.wait_ge(s_exp.h, need_ev)
                        if need_pt > 0:
                            t.wait_ge(s_evT.h, need_pt)
                        for i in range(8):
                            j = rnd * 8 + i
                            oc, sub = j // 4, j % 4
                            mmi = t.transpose(pT4[:, i * 128:(i + 1) * 128],
                                              ffnT[:, oc, sub * 128:(sub + 1) * 128],
                                              ids[:, :])
                        mmi.then_inc(s_tp.h, 1)
                    E.add("tensor", t_tpf)

                    s_evT.reserve()

                    def v_ptf(v, rnd=rnd, need=t_tpv):
                        v.wait_ge(s_tp.h, need)
                        oc0 = rnd * 2
                        v.tensor_copy(
                            ffn[:, :, oc0 * 128:(oc0 + 2) * 128].rearrange(
                                "p s (c f) -> p c s f", f=128),
                            pT4[:, :].rearrange("p (c s f) -> p c s f",
                                                c=2, s=4)).then_inc(s_evT.h, 1)
                    E.add("vector", v_ptf)

                for sub in range(4):
                    g = tb * 4 + sub
                    need_r2g = r2_mark[g - 2] if g >= 2 else 0
                    t_rcv = s_rc[g % 2].reserve(16)

                    def ld_r1b(sync, g=g, need=need_r2g):
                        if need > 0:
                            sync.wait_ge(s_o.h, need)
                        sync.dma_start(out=r1c[:, g % 2, :], in_=r1_r[g]).then_inc(
                            s_rc[g % 2].h, 16)
                    E.add("sync", ld_r1b)

                    need_t = s_evT.n

                    def v_lngate5(v, need_t=need_t):
                        v.wait_ge(s_evT.h, need_t)
                    E.add("vector", v_lngate5)
                    tnorm = queue_ln(E, (lambda sub=sub: ffn[:, sub, :]),
                                     (lambda: t4[:, :]))

                    need_st = s_so[g % 2].n if g >= 2 else 0
                    t_r2v = s_o.reserve()

                    def v_r2(v, g=g, need_ld=t_rcv, need_st=need_st, tnorm=tnorm):
                        v.wait_ge(s_rc[g % 2].h, need_ld)
                        v.wait_ge(s_ln2.h, tnorm)
                        if need_st > 0:
                            v.wait_ge(s_so[g % 2].h, need_st)
                        v.tensor_tensor(r2b[:, g % 2, :], t4[:, :],
                                        r1c[:, g % 2, :], op=ALU.add).then_inc(s_o.h, 1)
                    E.add("vector", v_r2)
                    r2_mark.append(s_o.n)

                    s_so[g % 2].reserve(16)

                    def sc_out(scl, g=g, need=t_r2v):
                        scl.wait_ge(s_o.h, need)
                        scl.dma_start(out=out_r[g], in_=r2b[:, g % 2, :]).then_inc(
                            s_so[g % 2].h, 16)
                    E.add("scalar", sc_out)

            def p4_fin(scl, a0=s_so[0].n, a1=s_so[1].n):
                scl.wait_ge(s_so[0].h, a0)
                scl.wait_ge(s_so[1].h, a1)
            E.add("scalar", p4_fin)
            E.flush(blk4)

    return nc


def _host_tables():
    half = DK // 2
    inv_freq = 10000.0 ** (-2.0 * np.arange(half) / DK)
    ang = np.arange(S)[:, None] * inv_freq[None, :]
    cosT = np.cos(ang).T.astype(np.float32)
    sinT = np.sin(ang).T.astype(np.float32)
    cc = np.concatenate([cosT, cosT], axis=0).astype(ml_dtypes.bfloat16)
    ss2 = np.concatenate([-sinT, sinT], axis=0).astype(ml_dtypes.bfloat16)
    masks = np.zeros((4 * 128, 512), dtype=ml_dtypes.bfloat16)
    for i in range(4):
        for kk in range(128):
            masks[i * 128 + kk, i * 128 + kk:] = 1.0
    ident = np.eye(128, dtype=ml_dtypes.bfloat16)
    pswap = np.zeros((128, 128), dtype=ml_dtypes.bfloat16)
    for i in range(64):
        pswap[i + 64, i] = 1.0
        pswap[i, i + 64] = 1.0
    e8 = np.zeros((8, 64), dtype=ml_dtypes.bfloat16)
    for e in range(8):
        e8[e, e * 8:(e + 1) * 8] = 1.0
    return cc, ss2, masks, ident, pswap, e8


def _perm_rows(w):
    perm = []
    for h in range(H):
        base = h * DK
        perm += [base + 2 * i for i in range(DK // 2)]
        perm += [base + 2 * i + 1 for i in range(DK // 2)]
    return w[perm, :]


_NC_CACHE = {}


def kernel(L0_attn_x, L0_attn_wq, L0_attn_wk, L0_attn_wv, L0_attn_wo,
           L0_ffn_W_base, L0_ffn_b_base, L0_ffn_W_router, L0_ffn_A_w, L0_ffn_B_w):
    x = np.ascontiguousarray(np.asarray(L0_attn_x, dtype=np.float32))
    wq = np.asarray(L0_attn_wq, dtype=np.float32)
    wk = np.asarray(L0_attn_wk, dtype=np.float32)
    wv = np.asarray(L0_attn_wv, dtype=np.float32)
    wo = np.asarray(L0_attn_wo, dtype=np.float32)
    wb = np.asarray(L0_ffn_W_base, dtype=np.float32)
    bb = np.asarray(L0_ffn_b_base, dtype=np.float32)
    wr = np.asarray(L0_ffn_W_router, dtype=np.float32)
    aw = np.asarray(L0_ffn_A_w, dtype=np.float32)
    bw = np.asarray(L0_ffn_B_w, dtype=np.float32)

    cc, ss2, masks, ident, pswap, e8 = _host_tables()

    wqTf = np.ascontiguousarray(_perm_rows(wq).T)
    wkTf = np.ascontiguousarray(_perm_rows(wk).T)
    wvTf = np.ascontiguousarray(wv.T)
    woTf = np.ascontiguousarray(wo.T)
    wbTf = np.ascontiguousarray(wb.T)
    aflT = aw.reshape(64, D).T
    wrTf = wr.T
    awrTf = np.ascontiguousarray(np.concatenate([aflT, wrTf], axis=1))
    bflTf = np.ascontiguousarray(bw.transpose(0, 2, 1).reshape(64, D))
    bbasef = np.ascontiguousarray(bb.reshape(16, 128).T)

    if "nc" not in _NC_CACHE:
        _NC_CACHE["nc"] = build_program()
    nc = _NC_CACHE["nc"]

    in_maps = []
    for c in range(8):
        b = c // 4
        hg = c % 4
        tq = c % 4
        in_maps.append({
            "x": x[b],
            "xq": np.ascontiguousarray(x[b, tq * QT:(tq + 1) * QT, :]),
            "wqT": np.ascontiguousarray(wqTf[:, hg * 512:(hg + 1) * 512]),
            "wkT": np.ascontiguousarray(wkTf[:, hg * 512:(hg + 1) * 512]),
            "wvT": np.ascontiguousarray(wvTf[:, hg * 512:(hg + 1) * 512]),
            "woT": woTf, "wbT": wbTf, "awrT": awrTf, "bflT": bflTf, "bbase": bbasef,
            "ident": ident, "identf": np.eye(128, dtype=np.float32),
            "pswap": pswap, "e8": e8,
            "cctab": cc, "ss2tab": ss2, "masks": masks,
        })
    res = run_bass_kernel_spmd(nc, in_maps, core_ids=list(range(8)))
    out = np.empty((B, S, D), dtype=np.float32)
    for c in range(8):
        b = c // 4
        tq = c % 4
        out[b, tq * QT:(tq + 1) * QT, :] = res.results[c]["out_q"]
    return out


# version 08
# speedup vs baseline: 1.4407x; 1.0352x over previous
import contextlib
import numpy as np
import ml_dtypes
import concourse.bass as bass
import concourse.mybir as mybir
from concourse.bass_utils import run_bass_kernel_spmd

F32 = mybir.dt.float32
I32 = mybir.dt.int32
BF16 = mybir.dt.bfloat16
AF = mybir.ActivationFunctionType
ALU = mybir.AluOpType
AX = mybir.AxisListType

B = 2
S = 8192
D = 2048
H = 16
HL = 4
DK = 128
NCH = D // 128
SBK = 512
NSB = S // SBK
QT = 2048
NBLK = 4
NKT = S // 128
CKT = 16
CEXP = 16.0
EPS = 1e-5
SCL = 1.0 / np.sqrt(DK)
LORA_SCALE = 2.0
RSQRT_MAGIC = 1597463007.0


class Sem:
    def __init__(self, nc, es, name):
        self.h = es.enter_context(nc.semaphore(name))
        self.n = 0

    def reserve(self, k=1):
        self.n += k
        return self.n


class Emit:
    ENGS = ("sync", "gpsimd", "tensor", "scalar", "vector")

    def __init__(self):
        self.q = {e: [] for e in self.ENGS}

    def add(self, eng, fn):
        self.q[eng].append(fn)

    def flush(self, block):
        for e in self.ENGS:
            fns = self.q[e]
            self.q[e] = []
            if fns:
                def body(hdl, fns=fns):
                    for f in fns:
                        f(hdl)
                getattr(block, e)(body)


def build_program():
    nc = bass.Bass("TRN2", target_bir_lowering=False, debug=False)

    x_in = nc.dram_tensor("x", [S, D], F32, kind="ExternalInput")
    xq_in = nc.dram_tensor("xq", [QT, D], F32, kind="ExternalInput")
    wqT = nc.dram_tensor("wqT", [D, HL * DK], F32, kind="ExternalInput")
    wkT = nc.dram_tensor("wkT", [D, HL * DK], F32, kind="ExternalInput")
    wvT = nc.dram_tensor("wvT", [D, HL * DK], F32, kind="ExternalInput")
    woT = nc.dram_tensor("woT", [D, D], F32, kind="ExternalInput")
    wbT = nc.dram_tensor("wbT", [D, D], F32, kind="ExternalInput")
    awrT = nc.dram_tensor("awrT", [D, 72], F32, kind="ExternalInput")
    bflT = nc.dram_tensor("bflT", [64, D], F32, kind="ExternalInput")
    bbase = nc.dram_tensor("bbase", [128, 16], F32, kind="ExternalInput")
    ident_in = nc.dram_tensor("ident", [128, 128], BF16, kind="ExternalInput")
    identf_in = nc.dram_tensor("identf", [128, 128], F32, kind="ExternalInput")
    pswap_in = nc.dram_tensor("pswap", [128, 128], BF16, kind="ExternalInput")
    e8_in = nc.dram_tensor("e8", [8, 64], BF16, kind="ExternalInput")
    cc_tab = nc.dram_tensor("cctab", [128, S], BF16, kind="ExternalInput")
    ss2_tab = nc.dram_tensor("ss2tab", [128, S], BF16, kind="ExternalInput")
    masks_in = nc.dram_tensor("masks", [4 * 128, 512], BF16, kind="ExternalInput")

    out_q = nc.dram_tensor("out_q", [QT, D], F32, kind="ExternalOutput")

    qsp = [nc.dram_tensor(f"qsp{h}", [128, NSB, SBK], BF16) for h in range(HL)]
    ksp = [nc.dram_tensor(f"ksp{h}", [128, NKT, 128], BF16) for h in range(HL)]
    vsp = [nc.dram_tensor(f"vsp{h}", [128, NKT, 128], BF16) for h in range(HL)]
    ccg_in = [nc.dram_tensor(f"ccin{k}", [HL * DK, QT], BF16) for k in range(4)]
    ccg_out = [nc.dram_tensor(f"ccout{k}", [4 * HL * DK, QT], BF16) for k in range(4)]
    r1sp = nc.dram_tensor("r1sp", [QT, D], F32)

    es = contextlib.ExitStack()
    with es:
        sb = lambda name, shape, dt: es.enter_context(nc.sbuf_tensor(name, shape, dt))

        ids = sb("ids", [128, 128], BF16)
        idsf = sb("idsf", [128, 128], F32)
        psw = sb("psw", [128, 128], BF16)
        e8s = sb("e8s", [8, 64], BF16)
        ones128 = sb("ones128", [128, 128], BF16)
        onecol = sb("onecol", [128, 1], BF16)
        onecf = sb("onecf", [128, 1], F32)
        onerow = sb("onerow", [1, 128], F32)
        cexp = sb("cexp", [128, 1], F32)
        bnst = sb("bnst", [128, 24], F32)
        mv = sb("mv", [128, 2], F32)
        rstd = sb("rstd", [128, 1], F32)
        rs_t1 = sb("rs_t1", [128, 1], F32)
        rs_t2 = sb("rs_t2", [128, 1], F32)

        s_v = Sem(nc, es, "s_v")
        s_cst = Sem(nc, es, "s_cst")
        s_ldx = Sem(nc, es, "s_ldx")
        s_wld = Sem(nc, es, "s_wld")
        s_x = [Sem(nc, es, f"s_x{i}") for i in range(4)]
        s_spq = [Sem(nc, es, f"s_spq{i}") for i in range(2)]
        s_spv = [Sem(nc, es, f"s_spv{i}") for i in range(2)]
        s_q = [Sem(nc, es, f"s_q{i}") for i in range(2)]
        s_kv = [Sem(nc, es, f"s_kv{i}") for i in range(2)]
        s_sto = [Sem(nc, es, f"s_sto{i}") for i in range(2)]
        s_xq = [Sem(nc, es, f"s_xq{i}") for i in range(2)]
        s_sr = [Sem(nc, es, f"s_sr{i}") for i in range(2)]
        s_ra = [Sem(nc, es, f"s_ra{i}") for i in range(2)]
        s_rc = [Sem(nc, es, f"s_rc{i}") for i in range(2)]
        s_so = [Sem(nc, es, f"s_so{i}") for i in range(2)]
        s_ln = Sem(nc, es, "s_ln")
        s_ln2 = Sem(nc, es, "s_ln2")
        s_tp = Sem(nc, es, "s_tp")
        s_evT = Sem(nc, es, "s_evT")
        s_proj = Sem(nc, es, "s_proj")
        s_pe = Sem(nc, es, "s_pe")
        s_sw = Sem(nc, es, "s_sw")
        s_rope = Sem(nc, es, "s_rope")
        s_vev = Sem(nc, es, "s_vev")
        s_mm = Sem(nc, es, "s_mm")
        s_exp = Sem(nc, es, "s_exp")
        s_msk = Sem(nc, es, "s_msk")
        s_av = Sem(nc, es, "s_av")
        s_zi = Sem(nc, es, "s_zi")
        s_bc = Sem(nc, es, "s_bc")
        s_o = Sem(nc, es, "s_o")
        s_cc = Sem(nc, es, "s_cc")
        s_lda = Sem(nc, es, "s_lda")
        s_mm3 = Sem(nc, es, "s_mm3")
        s_yev = Sem(nc, es, "s_yev")
        s_r1 = Sem(nc, es, "s_r1")
        s_g = Sem(nc, es, "s_g")
        s_rs = Sem(nc, es, "s_rs")
        s_zd = Sem(nc, es, "s_zd")
        s_zp = Sem(nc, es, "s_zp")
        s_zc = Sem(nc, es, "s_zc")
        s_rp1 = Sem(nc, es, "s_rp1")
        s_pl = Sem(nc, es, "s_pl")

        def queue_rsqrt(E):
            r = [s_v.reserve() for _ in range(9)]
            dn = s_rs.reserve()

            def body(v, r=r, dn=dn):
                v.tensor_scalar(out=rs_t1[:, :], in0=mv[:, 1:2], scalar1=EPS,
                                scalar2=None, op0=ALU.add).then_inc(s_v.h, 1)
                v.wait_ge(s_v.h, r[0])
                v.tensor_scalar(out=rs_t2[:, :], in0=rs_t1[:, :], scalar1=-0.5,
                                scalar2=None, op0=ALU.mult).then_inc(s_v.h, 1)
                v.wait_ge(s_v.h, r[1])
                v.tensor_scalar(out=rstd[:, :].bitcast(I32),
                                in0=rs_t1[:, :].bitcast(I32), scalar1=1,
                                scalar2=None,
                                op0=ALU.logical_shift_right).then_inc(s_v.h, 1)
                v.wait_ge(s_v.h, r[2])
                v.tensor_scalar(out=rstd[:, :].bitcast(I32),
                                in0=rstd[:, :].bitcast(I32), scalar1=-1.0,
                                scalar2=RSQRT_MAGIC, op0=ALU.mult,
                                op1=ALU.add).then_inc(s_v.h, 1)
                v.wait_ge(s_v.h, r[3])
                k = 4
                for it in range(2):
                    v.tensor_tensor(rs_t1[:, :], rstd[:, :], rstd[:, :],
                                    op=ALU.mult).then_inc(s_v.h, 1)
                    v.wait_ge(s_v.h, r[k]); k += 1
                    v.tensor_scalar(out=rs_t1[:, :], in0=rs_t1[:, :],
                                    scalar1=rs_t2[:, :], scalar2=1.5,
                                    op0=ALU.mult, op1=ALU.add).then_inc(s_v.h, 1)
                    v.wait_ge(s_v.h, r[k]); k += 1
                    inst = v.tensor_tensor(rstd[:, :], rstd[:, :], rs_t1[:, :],
                                           op=ALU.mult)
                    if it == 1:
                        inst.then_inc(s_rs.h, 1)
                    else:
                        inst.then_inc(s_v.h, 1)
                        v.wait_ge(s_v.h, r[k]); k += 1
            E.add("vector", body)
            return dn

        def queue_ln(E, src_fn, out_fn):
            t_stats = s_v.reserve()
            t_aggr = s_ln.reserve()

            def v_stats(v, src_fn=src_fn, t_stats=t_stats, t_aggr=t_aggr):
                src = src_fn()
                for i in range(3):
                    v.bn_stats(bnst[:, i * 6:(i + 1) * 6], src[:, i * 512:(i + 1) * 512])
                v.bn_stats(bnst[:, 18:24], src[:, 1536:2048]).then_inc(s_v.h, 1)
                v.wait_ge(s_v.h, t_stats)
                v.bn_aggr(mv[:, :], bnst[:, :]).then_inc(s_ln.h, 1)
                v.wait_ge(s_ln.h, t_aggr)
            E.add("vector", v_stats)
            dn = queue_rsqrt(E)
            t_norm = s_ln2.reserve()

            def v_norm(v, src_fn=src_fn, out_fn=out_fn, dn=dn):
                v.wait_ge(s_rs.h, dn)
                v.tensor_scalar(out=out_fn(), in0=src_fn(),
                                scalar1=mv[:, 0:1], scalar2=rstd[:, :],
                                op0=ALU.subtract, op1=ALU.mult).then_inc(s_ln2.h, 1)
            E.add("vector", v_norm)
            return t_norm

        with contextlib.ExitStack() as es1:
            sb1 = lambda n_, sh, dt: es1.enter_context(nc.sbuf_tensor(n_, sh, dt))
            ps1 = lambda n_, sh, dt: es1.enter_context(nc.psum_tensor(n_, sh, dt))
            xs = sb1("xs", [128, 4, D], F32)
            lnb = sb1("lnb", [128, 2, D], BF16)
            mv4 = sb1("mv4", [128, 4, 2], F32)
            rstd4 = sb1("rstd4", [128, 4], F32)
            rs4a = sb1("rs4a", [128, 4], F32)
            rs4b = sb1("rs4b", [128, 4], F32)
            lnT = sb1("lnT", [128, NCH, SBK], BF16)
            wq_s = sb1("wq_s", [128, NCH, HL * DK], BF16)
            wk_s = sb1("wk_s", [128, NCH, HL * DK], BF16)
            wv_s = sb1("wv_s", [128, NCH, HL * DK], BF16)
            ccs = sb1("ccs", [128, S], BF16)
            ss2s = sb1("ss2s", [128, S], BF16)
            pre = sb1("pre", [128, 2, SBK], BF16)
            swp = sb1("swp", [128, 2, SBK], BF16)
            tmpA = sb1("tmpA", [128, 2, SBK], BF16)
            ropo = sb1("ropo", [128, 2, SBK], BF16)
            vstg = sb1("vstg", [128, 2, HL * DK], BF16)
            pT = ps1("pT", [128, 1024], BF16)
            pPJ0 = ps1("pPJ0", [128, 512], F32)
            pPJ1 = ps1("pPJ1", [128, 512], F32)
            pSw0 = ps1("pSw0", [128, 512], F32)
            pSw1 = ps1("pSw1", [128, 512], F32)
            pPJ = [pPJ0, pPJ1]
            pSw = [pSw0, pSw1]

            blk1 = es1.enter_context(nc.Block("p1"))
            E = Emit()

            n_ldi1 = s_ldx.reserve(16 * 6)

            def ld_init(sync):
                sync.dma_start(out=ids[:, :], in_=ident_in[:, :]).then_inc(s_ldx.h, 16)
                sync.dma_start(out=idsf[:, :], in_=identf_in[:, :]).then_inc(s_ldx.h, 16)
                sync.dma_start(out=psw[:, :], in_=pswap_in[:, :]).then_inc(s_ldx.h, 16)
                sync.dma_start(out=e8s[:, :], in_=e8_in[:, :]).then_inc(s_ldx.h, 16)
                sync.dma_start(out=ccs[:, :], in_=cc_tab[:, :]).then_inc(s_ldx.h, 16)
                sync.dma_start(out=ss2s[:, :], in_=ss2_tab[:, :]).then_inc(s_ldx.h, 16)
            E.add("sync", ld_init)

            n_wld1 = s_wld.reserve(16 * 3 * NCH)

            def ld_w(g):
                for w_d, w_s in ((wqT, wq_s), (wkT, wk_s), (wvT, wv_s)):
                    w_r = w_d.rearrange("(c p) o -> c p o", p=128)
                    for c in range(NCH):
                        g.dma_start(out=w_s[:, c, :], in_=w_r[c]).then_inc(s_wld.h, 16)
            E.add("gpsimd", ld_w)

            n_cst = s_cst.reserve()

            def consts(v):
                v.memset(onecol[:, :], 1.0)
                v.memset(onecf[:, :], 1.0)
                v.memset(onerow[:, :], 1.0)
                v.memset(ones128[:, :], 1.0)
                v.memset(cexp[:, :], -CEXP).then_inc(s_cst.h, 1)
                v.wait_ge(s_ldx.h, n_ldi1)
                v.wait_ge(s_wld.h, n_wld1)
            E.add("vector", consts)

            def eng_gate1(e, a=n_cst, b_=n_ldi1, c_=n_wld1):
                e.wait_ge(s_cst.h, a)
                e.wait_ge(s_ldx.h, b_)
                e.wait_ge(s_wld.h, c_)
            E.add("scalar", eng_gate1)
            E.add("tensor", eng_gate1)

            x_r = x_in.rearrange("(g p) d -> g p d", p=128)
            ln2_mark = []
            tp_mark = []
            for s_i in range(NSB):
                for tt in range(4):
                    g = 4 * s_i + tt
                    t_x = s_x[tt].reserve(16)
                    need_gx = ln2_mark[4 * (s_i - 1) + tt] if s_i >= 1 else 0

                    def ld_x(sync, g=g, tt=tt, need=need_gx):
                        if need > 0:
                            sync.wait_ge(s_ln2.h, need)
                        sync.dma_start(out=xs[:, tt, :], in_=x_r[g]).then_inc(
                            s_x[tt].h, 16)
                    E.add("sync", ld_x)

                    va = s_v.reserve()
                    t_ag = s_ln.reserve()

                    def v_st(v, tt=tt, t_x=t_x, va=va, t_ag=t_ag):
                        v.wait_ge(s_x[tt].h, t_x)
                        v.wait_ge(s_ln.h, t_ag - 1)
                        for i in range(3):
                            v.bn_stats(bnst[:, i * 6:(i + 1) * 6],
                                       xs[:, tt, i * 512:(i + 1) * 512])
                        v.bn_stats(bnst[:, 18:24], xs[:, tt, 1536:2048]).then_inc(s_v.h, 1)
                        v.wait_ge(s_v.h, va)
                        v.bn_aggr(mv4[:, tt, :], bnst[:, :]).then_inc(s_ln.h, 1)
                    E.add("vector", v_st)

                agg_all = s_ln.n
                r4 = [s_v.reserve() for _ in range(11)]
                t_rs4 = s_rs.reserve()

                def v_rs4(v, agg_all=agg_all, r4=r4):
                    v.wait_ge(s_ln.h, agg_all)
                    v.tensor_scalar(out=rs4a[:, :], in0=mv4[:, :, 1], scalar1=EPS,
                                    scalar2=None, op0=ALU.add).then_inc(s_v.h, 1)
                    v.wait_ge(s_v.h, r4[0])
                    v.tensor_scalar(out=rs4b[:, :], in0=rs4a[:, :], scalar1=-0.5,
                                    scalar2=None, op0=ALU.mult).then_inc(s_v.h, 1)
                    v.wait_ge(s_v.h, r4[1])
                    v.tensor_scalar(out=rstd4[:, :].bitcast(I32),
                                    in0=rs4a[:, :].bitcast(I32), scalar1=1,
                                    scalar2=None,
                                    op0=ALU.logical_shift_right).then_inc(s_v.h, 1)
                    v.wait_ge(s_v.h, r4[2])
                    v.tensor_scalar(out=rstd4[:, :].bitcast(I32),
                                    in0=rstd4[:, :].bitcast(I32), scalar1=-1.0,
                                    scalar2=RSQRT_MAGIC, op0=ALU.mult,
                                    op1=ALU.add).then_inc(s_v.h, 1)
                    v.wait_ge(s_v.h, r4[3])
                    k = 4
                    for itn in range(2):
                        v.tensor_tensor(rs4a[:, :], rstd4[:, :], rstd4[:, :],
                                        op=ALU.mult).then_inc(s_v.h, 1)
                        v.wait_ge(s_v.h, r4[k]); k += 1
                        v.tensor_tensor(rs4a[:, :], rs4a[:, :], rs4b[:, :],
                                        op=ALU.mult).then_inc(s_v.h, 1)
                        v.wait_ge(s_v.h, r4[k]); k += 1
                        if itn == 0:
                            v.tensor_scalar(out=rs4a[:, :], in0=rs4a[:, :], scalar1=1.5,
                                            scalar2=None, op0=ALU.add).then_inc(s_v.h, 1)
                            v.wait_ge(s_v.h, r4[k]); k += 1
                            v.tensor_tensor(rstd4[:, :], rstd4[:, :], rs4a[:, :],
                                            op=ALU.mult).then_inc(s_v.h, 1)
                            v.wait_ge(s_v.h, r4[k]); k += 1
                        else:
                            v.tensor_scalar(out=rs4a[:, :], in0=rs4a[:, :], scalar1=1.5,
                                            scalar2=None, op0=ALU.add).then_inc(s_v.h, 1)
                            v.wait_ge(s_v.h, r4[k]); k += 1
                            v.tensor_tensor(rstd4[:, :], rstd4[:, :], rs4a[:, :],
                                            op=ALU.mult).then_inc(s_rs.h, 1)
                E.add("vector", v_rs4)

                for tt in range(4):
                    g = 4 * s_i + tt
                    need_tp = tp_mark[g - 2] if g >= 2 else 0
                    t_nm = s_ln2.reserve()

                    def v_nm(v, tt=tt, t_rs4=t_rs4, need_tp=need_tp):
                        v.wait_ge(s_rs.h, t_rs4)
                        if need_tp > 0:
                            v.wait_ge(s_tp.h, need_tp)
                        v.tensor_scalar(out=lnb[:, tt % 2, :], in0=xs[:, tt, :],
                                        scalar1=mv4[:, tt, 0:1],
                                        scalar2=rstd4[:, tt:tt + 1],
                                        op0=ALU.subtract, op1=ALU.mult).then_inc(
                                        s_ln2.h, 1)
                    E.add("vector", v_nm)
                    ln2_mark.append(s_ln2.n)

                    for rnd in range(2):
                        need_ev = s_evT.n
                        t_tpv = s_tp.reserve()

                        def t_tp(t, tt=tt, rnd=rnd, need_ln=t_nm, need_ev=need_ev):
                            if rnd == 0:
                                t.wait_ge(s_ln2.h, need_ln)
                            if need_ev > 0:
                                t.wait_ge(s_evT.h, need_ev)
                            for i in range(8):
                                c = rnd * 8 + i
                                mmi = t.transpose(pT[:, i * 128:(i + 1) * 128],
                                                  lnb[:, tt % 2, c * 128:(c + 1) * 128],
                                                  ids[:, :])
                            mmi.then_inc(s_tp.h, 1)
                        E.add("tensor", t_tp)

                        s_evT.reserve()

                        def v_ev(v, tt=tt, rnd=rnd, need=t_tpv):
                            v.wait_ge(s_tp.h, need)
                            v.tensor_copy(
                                lnT[:, rnd * 8:(rnd + 1) * 8, tt * 128:(tt + 1) * 128],
                                pT[:, :].rearrange("p (c f) -> p c f", f=128)).then_inc(
                                s_evT.h, 1)
                        E.add("vector", v_ev)
                    tp_mark.append(s_tp.n)

                ev_all = s_evT.n

                def t_sb_gate(t, need=ev_all):
                    t.wait_ge(s_evT.h, need)
                E.add("tensor", t_sb_gate)

                for h in range(HL):
                    for which, w_s in (("q", wq_s), ("k", wk_s)):
                        pj = s_proj.n % 2
                        need_pe = s_pe.n - 1
                        t_prj = s_proj.reserve()

                        def t_proj(t, h=h, w_s=w_s, pj=pj, need_pe=need_pe):
                            if need_pe > 0:
                                t.wait_ge(s_pe.h, need_pe)
                            for c in range(NCH):
                                mmi = t.matmul(pPJ[pj][:, :], w_s[:, c, h * 128:(h + 1) * 128],
                                               lnT[:, c, :], start=(c == 0),
                                               stop=(c == NCH - 1))
                            mmi.then_inc(s_proj.h, 1)
                        E.add("tensor", t_proj)

                        prslot = s_rope.n % 2
                        need_rope = s_rope.n - 1
                        t_pev = s_pe.reserve()

                        def v_pe(scl, need=t_prj, pj=pj, prslot=prslot,
                                 need_rope=need_rope):
                            scl.wait_ge(s_proj.h, need)
                            if need_rope > 0:
                                scl.wait_ge(s_rope.h, need_rope)
                            scl.copy(out=pre[:, prslot, :],
                                     in_=pPJ[pj][:, :]).then_inc(s_pe.h, 1)
                        E.add("scalar", v_pe)

                        need_rp = s_rope.n - 1
                        t_swv = s_sw.reserve()

                        def t_sw(t, need=t_pev, prslot=prslot, need_rp=need_rp):
                            t.wait_ge(s_pe.h, need)
                            if need_rp > 0:
                                t.wait_ge(s_rope.h, need_rp)
                            t.matmul(pSw[prslot][:, :], psw[:, :], pre[:, prslot, :],
                                     start=True, stop=True).then_inc(s_sw.h, 1)
                        E.add("tensor", t_sw)

                        ridx = s_rope.n
                        rslot = ridx % 2
                        va = s_v.reserve(2)
                        need_wr = s_spq[rslot].n if ridx >= 2 else 0
                        t_rp = s_rope.reserve()

                        def v_rope(v, s_i=s_i, rslot=rslot, need=t_swv, va=va,
                                   need_wr=need_wr):
                            v.wait_ge(s_sw.h, need)
                            v.tensor_tensor(tmpA[:, rslot, :], pre[:, rslot, :],
                                            ccs[:, s_i * SBK:(s_i + 1) * SBK],
                                            op=ALU.mult).then_inc(s_v.h, 1)
                            v.tensor_tensor(swp[:, rslot, :], pSw[rslot][:, :],
                                            ss2s[:, s_i * SBK:(s_i + 1) * SBK],
                                            op=ALU.mult).then_inc(s_v.h, 1)
                            v.wait_ge(s_v.h, va)
                            if need_wr > 0:
                                v.wait_ge(s_spq[rslot].h, need_wr)
                            v.tensor_tensor(ropo[:, rslot, :], tmpA[:, rslot, :],
                                            swp[:, rslot, :], op=ALU.add).then_inc(s_rope.h, 1)
                        E.add("vector", v_rope)

                        s_spq[rslot].reserve(16)

                        def sc_spill(scl, s_i=s_i, h=h, which=which, rslot=rslot,
                                     need=t_rp):
                            scl.wait_ge(s_rope.h, need)
                            if which == "q":
                                dst = qsp[h][:, s_i, :]
                            else:
                                dst = ksp[h][:, 4 * s_i:4 * s_i + 4, :].rearrange(
                                    "p a b -> p (a b)")
                            scl.dma_start(out=dst, in_=ropo[:, rslot, :]).then_inc(
                                s_spq[rslot].h, 16)
                        E.add("scalar", sc_spill)

                for tt in range(4):
                    g = 4 * s_i + tt
                    pj = s_proj.n % 2
                    need_pe = s_pe.n - 1
                    t_vp = s_proj.reserve()

                    def t_v(t, tt=tt, pj=pj, need_pe=need_pe):
                        if need_pe > 0:
                            t.wait_ge(s_pe.h, need_pe)
                        for c in range(NCH):
                            mmi = t.matmul(pPJ[pj][:, :], lnT[:, c, tt * 128:(tt + 1) * 128],
                                           wv_s[:, c, :], start=(c == 0), stop=(c == NCH - 1))
                        mmi.then_inc(s_proj.h, 1)
                    E.add("tensor", t_v)

                    vslot = g % 2
                    need_wr = s_spv[vslot].n if g >= 2 else 0
                    t_vev = s_pe.reserve()

                    def v_vev(scl, vslot=vslot, pj=pj, need=t_vp, need_wr=need_wr):
                        scl.wait_ge(s_proj.h, need)
                        if need_wr > 0:
                            scl.wait_ge(s_spv[vslot].h, need_wr)
                        scl.copy(out=vstg[:, vslot, :],
                                 in_=pPJ[pj][:, :]).then_inc(s_pe.h, 1)
                    E.add("scalar", v_vev)

                    s_spv[vslot].reserve(16 * HL)

                    def sc_vsp(scl, g=g, vslot=vslot, need=t_vev):
                        scl.wait_ge(s_pe.h, need)
                        for h in range(HL):
                            scl.dma_start(
                                out=vsp[h][:, g, :],
                                in_=vstg[:, vslot, h * 128:(h + 1) * 128]).then_inc(
                                s_spv[vslot].h, 16)
                    E.add("scalar", sc_vsp)

            def p1_fin(scl, a0=s_spq[0].n, a1=s_spq[1].n, b0=s_spv[0].n, b1=s_spv[1].n):
                scl.wait_ge(s_spq[0].h, a0)
                scl.wait_ge(s_spq[1].h, a1)
                scl.wait_ge(s_spv[0].h, b0)
                scl.wait_ge(s_spv[1].h, b1)
            E.add("scalar", p1_fin)
            E.flush(blk1)

        with contextlib.ExitStack() as es2:
            sb2 = lambda n_, sh, dt: es2.enter_context(nc.sbuf_tensor(n_, sh, dt))
            ps2 = lambda n_, sh, dt: es2.enter_context(nc.psum_tensor(n_, sh, dt))
            msk = sb2("msk", [128, 4, 512], BF16)
            qbuf = sb2("qbuf", [128, 2, 512], BF16)
            kbuf = sb2("kbuf", [128, 2, CKT * 128], BF16)
            vbuf = sb2("vbuf", [128, 2, CKT * 128], BF16)
            pb = sb2("pb", [128, 4, 512], BF16)
            binv = sb2("binv", [128, 512], F32)
            oout = sb2("oout", [128, 2, 512], BF16)
            pS = ps2("pS", [128, 4, 512], F32)
            pO0 = ps2("pO0", [128, 512], F32)
            pO1 = ps2("pO1", [128, 512], F32)
            pOb = [pO0, pO1]
            pZ = ps2("pZ", [128, 512], F32)

            blk2 = es2.enter_context(nc.Block("p2"))
            E = Emit()

            n_mskld = s_ldx.reserve(16 * 4)

            def ld_msk(sync):
                m_r = masks_in.rearrange("(t p) q -> t p q", p=128)
                for i in range(4):
                    sync.dma_start(out=msk[:, i, :], in_=m_r[i]).then_inc(s_ldx.h, 16)
            E.add("sync", ld_msk)

            def eng_gate2(e, need=n_mskld):
                e.wait_ge(s_ldx.h, need)
            E.add("vector", eng_gate2)
            E.add("tensor", eng_gate2)
            E.add("scalar", eng_gate2)

            mm_mark_iter = []
            kcons_mark = []
            zr_mark = []
            gchunk = 0
            for qb in range(NSB):
                nkt = 4 * (qb + 1)
                nchunks = (nkt + CKT - 1) // CKT
                for h in range(HL):
                    it = qb * HL + h
                    qslot = it % 2
                    need_qg = mm_mark_iter[it - 2] if it >= 2 else 0
                    q_mark = s_q[qslot].reserve(16)

                    def ld_q(sync, qb=qb, h=h, qslot=qslot, need=need_qg):
                        if need > 0:
                            sync.wait_ge(s_mm.h, need)
                        sync.dma_start(out=qbuf[:, qslot, :],
                                       in_=qsp[h][:, qb, :]).then_inc(s_q[qslot].h, 16)
                    E.add("sync", ld_q)

                    ch_base = gchunk
                    ld_marks = {}

                    def emit_ld_kv(ch, qb=qb, h=h, nkt=nkt, ch_base=ch_base,
                                   ld_marks=ld_marks):
                        nonlocal gchunk
                        k0 = ch * CKT
                        nk = min(CKT, nkt - k0)
                        gc = ch_base + ch
                        cs = gc % 2
                        need = kcons_mark[gc - 2] if gc >= 2 else 0
                        ld_marks[ch] = s_kv[cs].reserve(32)

                        def ld_kv(sync, h=h, k0=k0, nk=nk, cs=cs, need=need):
                            if need > 0:
                                sync.wait_ge(s_av.h, need)
                            sync.dma_start(
                                out=kbuf[:, cs, 0:nk * 128],
                                in_=ksp[h][:, k0:k0 + nk, :].rearrange(
                                    "p a b -> p (a b)")).then_inc(s_kv[cs].h, 16)
                            sync.dma_start(
                                out=vbuf[:, cs, 0:nk * 128],
                                in_=vsp[h][:, k0:k0 + nk, :].rearrange(
                                    "p a b -> p (a b)")).then_inc(s_kv[cs].h, 16)
                        E.add("sync", ld_kv)
                        gchunk += 1

                    emit_ld_kv(0)
                    if nchunks > 1:
                        emit_ld_kv(1)

                    o_guard = s_o.n - 1
                    pO = pOb[it % 2]

                    def emit_t_s(kt):
                        ch = kt // CKT
                        cs = (ch_base + ch) % 2
                        ko = kt - ch * CKT
                        sslot = kt % 4
                        need_exp = s_exp.n - 1
                        t_sv = s_mm.reserve()

                        def t_s(t, kt=kt, cs=cs, ko=ko, sslot=sslot, qslot=qslot,
                                need_ld=ld_marks[ch], need_exp=need_exp,
                                o_guard=o_guard, q_mark=q_mark):
                            if kt == 0:
                                t.wait_ge(s_q[qslot].h, q_mark)
                                t.wait_ge(s_kv[cs].h, need_ld)
                            elif ko == 0:
                                t.wait_ge(s_kv[cs].h, need_ld)
                            if need_exp > 0:
                                t.wait_ge(s_exp.h, need_exp)
                            t.matmul(pS[:, sslot, :],
                                     kbuf[:, cs, ko * 128:(ko + 1) * 128],
                                     qbuf[:, qslot, :], start=True,
                                     stop=True).then_inc(s_mm.h, 1)
                        E.add("tensor", t_s)
                        return t_sv

                    ts_marks = {0: emit_t_s(0), 1: emit_t_s(1)}
                    zprev = zr_mark[it - 1] if it >= 1 else 0

                    for bp in range(nkt // 2):
                        kt0 = 2 * bp
                        pp = (bp % 2) * 2
                        need_z = s_zp.n - 2
                        t_ev = s_exp.reserve()

                        def s_e2(scl, pp=pp, need_mm=ts_marks[kt0 + 1],
                                 need_z=need_z):
                            scl.wait_ge(s_mm.h, need_mm)
                            if need_z > 0:
                                scl.wait_ge(s_zp.h, need_z)
                            scl.activation(out=pb[:, pp:pp + 2, :],
                                           in_=pS[:, pp:pp + 2, :],
                                           func=AF.Exp, bias=cexp[:, :],
                                           scale=SCL).then_inc(s_exp.h, 1)
                        E.add("scalar", s_e2)

                        for j in range(2):
                            kt = kt0 + j
                            diag = kt >= 4 * qb
                            if diag:
                                s_msk.reserve()

                                def v_m(v, qb=qb, kt=kt, pslot=pp + j, need=t_ev):
                                    v.wait_ge(s_exp.h, need)
                                    v.tensor_tensor(pb[:, pslot, :], pb[:, pslot, :],
                                                    msk[:, kt - 4 * qb, :],
                                                    op=ALU.mult).then_inc(s_msk.h, 1)
                                E.add("vector", v_m)

                        if kt0 + 2 < nkt:
                            ts_marks[kt0 + 2] = emit_t_s(kt0 + 2)
                            ts_marks[kt0 + 3] = emit_t_s(kt0 + 3)

                        for j in range(2):
                            kt = kt0 + j
                            ch = kt // CKT
                            cs = (ch_base + ch) % 2
                            ko = kt - ch * CKT
                            diag = kt >= 4 * qb
                            lastk = (ko == CKT - 1 or kt == nkt - 1)
                            need_m = s_msk.n
                            t_avv = s_av.reserve()

                            def t_av(t, kt=kt, cs=cs, ko=ko, pslot=pp + j, diag=diag,
                                     need_e=t_ev, need_m=need_m, nkt=nkt,
                                     pO=pO, o_guard=o_guard):
                                if kt == 0 and o_guard > 0:
                                    t.wait_ge(s_o.h, o_guard)
                                if diag:
                                    t.wait_ge(s_msk.h, need_m)
                                else:
                                    t.wait_ge(s_exp.h, need_e)
                                t.matmul(pO[:, :], vbuf[:, cs, ko * 128:(ko + 1) * 128],
                                         pb[:, pslot, :], start=(kt == 0),
                                         stop=(kt == nkt - 1)).then_inc(s_av.h, 1)
                            E.add("tensor", t_av)
                            if lastk:
                                kcons_mark.append(s_av.n)
                                if ch + 2 < nchunks:
                                    emit_ld_kv(ch + 2)

                        for j in range(2):
                            kt = kt0 + j
                            s_zp.reserve()

                            def t_z(t, kt=kt, pslot=pp + j, zprev=zprev, nkt=nkt):
                                if kt == 0 and zprev > 0:
                                    t.wait_ge(s_zi.h, zprev)
                                t.matmul(pZ[:, :], ones128[:, :], pb[:, pslot, :],
                                         start=(kt == 0),
                                         stop=(kt == nkt - 1)).then_inc(s_zp.h, 1)
                            E.add("tensor", t_z)

                    mm_mark_iter.append(s_mm.n)

                    zm_all = s_zp.n
                    t_ziv = s_zi.reserve()
                    zr_mark.append(s_zi.n)

                    def v_zi(v, zm_all=zm_all):
                        v.wait_ge(s_zp.h, zm_all)
                        v.reciprocal(binv[:, :], pZ[:, :]).then_inc(s_zi.h, 1)
                    E.add("vector", v_zi)

                    oslot = it % 2
                    need_st = s_sto[oslot].n if it >= 2 else 0
                    t_ov = s_o.reserve()

                    def v_o(v, oslot=oslot, t_ziv=t_ziv, need_st=need_st,
                            need_av=s_av.n, pO=pOb[it % 2]):
                        v.wait_ge(s_zi.h, t_ziv)
                        v.wait_ge(s_av.h, need_av)
                        if need_st > 0:
                            v.wait_ge(s_sto[oslot].h, need_st)
                        v.tensor_tensor(oout[:, oslot, :], pO[:, :], binv[:, :],
                                        op=ALU.mult).then_inc(s_o.h, 1)
                    E.add("vector", v_o)

                    s_sto[oslot].reserve(16)

                    def gp_o(g, qb=qb, h=h, oslot=oslot, need=t_ov):
                        g.wait_ge(s_o.h, need)
                        k = qb // 4
                        g.dma_start(
                            out=ccg_in[k][h * 128:(h + 1) * 128,
                                          (qb % 4) * 512:(qb % 4 + 1) * 512],
                            in_=oout[:, oslot, :]).then_inc(s_sto[oslot].h, 16)
                    E.add("gpsimd", gp_o)

                if qb % 4 == 3:
                    s_cc.reserve()

                    def g_cc(g, k=qb // 4, need0=s_sto[0].n, need1=s_sto[1].n):
                        g.wait_ge(s_sto[0].h, need0)
                        g.wait_ge(s_sto[1].h, need1)
                        g.collective_compute(
                            "AllGather", ALU.bypass,
                            replica_groups=[[0, 1, 2, 3], [4, 5, 6, 7]],
                            ins=[ccg_in[k].ap().opt()],
                            outs=[ccg_out[k].ap().opt()]).then_inc(s_cc.h, 1)
                    E.add("gpsimd", g_cc)
            E.flush(blk2)

        with contextlib.ExitStack() as es3:
            sb3 = lambda n_, sh, dt: es3.enter_context(nc.sbuf_tensor(n_, sh, dt))
            ps3 = lambda n_, sh, dt: es3.enter_context(nc.psum_tensor(n_, sh, dt))
            wo_s = sb3("wo_s", [128, NCH, D], BF16)
            aT = sb3("aT", [128, NCH, 512], BF16)
            y1T = sb3("y1T", [128, NCH, 512], BF16)
            y1 = sb3("y1", [128, 4, D], BF16)
            xqb = sb3("xqb", [128, 2, D], F32)
            t3 = sb3("t3", [128, D], F32)
            r1b = sb3("r1b", [128, 2, D], F32)
            pY = ps3("pY", [128, 512], F32)
            pT3 = ps3("pT3", [128, 1024], BF16)

            blk3 = es3.enter_context(nc.Block("p3"))
            E = Emit()

            n_wld3 = s_wld.reserve(16 * NCH)

            def ld_wo(g):
                w_r = woT.rearrange("(c p) o -> c p o", p=128)
                for c in range(NCH):
                    g.dma_start(out=wo_s[:, c, :], in_=w_r[c]).then_inc(s_wld.h, 16)
            E.add("gpsimd", ld_wo)

            def t_gate3(t, need=n_wld3):
                t.wait_ge(s_wld.h, need)
            E.add("tensor", t_gate3)

            xq_r = xq_in.rearrange("(g p) d -> g p d", p=128)
            r1_r = r1sp.rearrange("(g p) d -> g p d", p=128)

            r1_mark = []
            mm3_mark = []
            for tb in range(NBLK):
                E.flush(blk3)
                need_mm = mm3_mark[tb - 1] if tb >= 1 else 0
                lda_tb = s_lda.reserve(16 * NCH)

                def sync_aT(sync, tb=tb, need_mm=need_mm):
                    if need_mm > 0:
                        sync.wait_ge(s_mm3.h, need_mm)
                    pid = sync.partition_id()
                    for core in nc.Switch(engines=[sync], index=[pid], n=8):
                        tq = core % 4
                        sync.wait_ge(s_cc.h, tq + 1)
                        co = ccg_out[tq]
                        for cb in range(NCH):
                            sync.dma_start(
                                out=aT[:, cb, :],
                                in_=co[cb * 128:(cb + 1) * 128,
                                       tb * 512:(tb + 1) * 512]).then_inc(s_lda.h, 16)
                blk3.sync(sync_aT)

                for oc in range(NCH):
                    need_ev = s_yev.n
                    t_mmv = s_mm3.reserve()

                    def t_op(t, oc=oc, need_ld=lda_tb, need_ev=need_ev):
                        if oc == 0:
                            t.wait_ge(s_lda.h, need_ld)
                        if need_ev > 0:
                            t.wait_ge(s_yev.h, need_ev)
                        for c in range(NCH):
                            mmi = t.matmul(pY[:, :], wo_s[:, c, oc * 128:(oc + 1) * 128],
                                           aT[:, c, :], start=(c == 0), stop=(c == NCH - 1))
                        mmi.then_inc(s_mm3.h, 1)
                    E.add("tensor", t_op)

                    s_yev.reserve()

                    def v_ye(v, oc=oc, need=t_mmv):
                        v.wait_ge(s_mm3.h, need)
                        v.tensor_copy(y1T[:, oc, :], pY[:, :]).then_inc(s_yev.h, 1)
                    E.add("vector", v_ye)
                mm3_mark.append(s_mm3.n)

                for rnd in range(8):
                    need_ev = s_yev.n
                    need_pt = s_evT.n
                    t_tpv = s_tp.reserve()

                    def t_tp3(t, rnd=rnd, need_ev=need_ev, need_pt=need_pt):
                        if rnd == 0:
                            t.wait_ge(s_yev.h, need_ev)
                        if need_pt > 0:
                            t.wait_ge(s_evT.h, need_pt)
                        for i in range(8):
                            j = rnd * 8 + i
                            oc, sub = j // 4, j % 4
                            mmi = t.transpose(pT3[:, i * 128:(i + 1) * 128],
                                              y1T[:, oc, sub * 128:(sub + 1) * 128],
                                              ids[:, :])
                        mmi.then_inc(s_tp.h, 1)
                    E.add("tensor", t_tp3)

                    s_evT.reserve()

                    def v_pt3(v, rnd=rnd, need=t_tpv):
                        v.wait_ge(s_tp.h, need)
                        oc0 = rnd * 2
                        v.tensor_copy(
                            y1[:, :, oc0 * 128:(oc0 + 2) * 128].rearrange(
                                "p s (c f) -> p c s f", f=128),
                            pT3[:, :].rearrange("p (c s f) -> p c s f",
                                                c=2, s=4)).then_inc(s_evT.h, 1)
                    E.add("vector", v_pt3)

                for sub in range(4):
                    g = tb * 4 + sub
                    need_r1g = r1_mark[g - 2] if g >= 2 else 0
                    t_xqv = s_xq[g % 2].reserve(16)

                    def ld_xq(sync, g=g, need=need_r1g):
                        if need > 0:
                            sync.wait_ge(s_r1.h, need)
                        sync.dma_start(out=xqb[:, g % 2, :], in_=xq_r[g]).then_inc(
                            s_xq[g % 2].h, 16)
                    E.add("sync", ld_xq)

                    need_t = s_evT.n

                    def v_lngate(v, need_t=need_t):
                        v.wait_ge(s_evT.h, need_t)
                    E.add("vector", v_lngate)
                    tnorm = queue_ln(E, (lambda sub=sub: y1[:, sub, :]),
                                     (lambda: t3[:, :]))

                    need_wr = s_sr[g % 2].n if g >= 2 else 0
                    t_r1v = s_r1.reserve()

                    def v_r1(v, g=g, need_ld=t_xqv, need_wr=need_wr, tnorm=tnorm):
                        v.wait_ge(s_xq[g % 2].h, need_ld)
                        v.wait_ge(s_ln2.h, tnorm)
                        if need_wr > 0:
                            v.wait_ge(s_sr[g % 2].h, need_wr)
                        v.tensor_tensor(r1b[:, g % 2, :], t3[:, :],
                                        xqb[:, g % 2, :], op=ALU.add).then_inc(s_r1.h, 1)
                    E.add("vector", v_r1)
                    r1_mark.append(s_r1.n)

                    s_sr[g % 2].reserve(16)

                    def sc_r1(scl, g=g, need=t_r1v):
                        scl.wait_ge(s_r1.h, need)
                        scl.dma_start(out=r1_r[g], in_=r1b[:, g % 2, :]).then_inc(
                            s_sr[g % 2].h, 16)
                    E.add("scalar", sc_r1)

            def p3_fin(scl, a0=s_sr[0].n, a1=s_sr[1].n):
                scl.wait_ge(s_sr[0].h, a0)
                scl.wait_ge(s_sr[1].h, a1)
            E.add("scalar", p3_fin)
            E.flush(blk3)

        with contextlib.ExitStack() as es4:
            sb4 = lambda n_, sh, dt: es4.enter_context(nc.sbuf_tensor(n_, sh, dt))
            ps4 = lambda n_, sh, dt: es4.enter_context(nc.psum_tensor(n_, sh, dt))
            wb_s = sb4("wb_s", [128, NCH, D], BF16)
            awr_s = sb4("awr_s", [128, NCH, 72], BF16)
            bfl_s = sb4("bfl_s", [64, D], BF16)
            bb_s = sb4("bb_s", [128, 16], F32)
            r1a = sb4("r1a", [128, 2, D], F32)
            r1c = sb4("r1c", [128, 2, D], F32)
            ln2 = sb4("ln2", [128, D], BF16)
            ln2T = sb4("ln2T", [128, NCH, 512], BF16)
            hbuf = sb4("hbuf", [128, 512], BF16)
            hbufL = sb4("hbufL", [8, 512], F32)
            lgt = sb4("lgt", [128, 4, 8], F32)
            negm1 = sb4("negm1", [128, 1], F32)
            m2b = sb4("m2b", [128, 1], F32)
            eqb = sb4("eqb", [128, 8], F32)
            lmk = sb4("lmk", [128, 8], F32)
            pexp = sb4("pexp", [128, 8], F32)
            indb = sb4("indb", [128, 8], F32)
            wgt = sb4("wgt", [128, 8], F32)
            zg = sb4("zg", [128, 1], F32)
            zgi = sb4("zgi", [128, 1], F32)
            wn = sb4("wn", [128, 8], F32)
            wnT = sb4("wnT", [8, 128], BF16)
            wexpT = sb4("wexpT", [64, 512], BF16)
            h2 = sb4("h2", [64, 512], BF16)
            ffnT = sb4("ffnT", [128, NCH, 512], BF16)
            ffn = sb4("ffn", [128, 4, D], BF16)
            t4 = sb4("t4", [128, D], F32)
            r2b = sb4("r2b", [128, 2, D], F32)
            pY4 = ps4("pY4", [128, 512], F32)
            pT4 = ps4("pT4", [128, 1024], BF16)
            pH = ps4("pH", [128, 512], F32)
            pG = ps4("pG", [128, 512], F32)

            blk4 = es4.enter_context(nc.Block("p4"))
            E = Emit()

            n_wld4 = s_wld.reserve(16 * (2 * NCH + 1))

            def ld_w4(g):
                w_r = wbT.rearrange("(c p) o -> c p o", p=128)
                for c in range(NCH):
                    g.dma_start(out=wb_s[:, c, :], in_=w_r[c]).then_inc(s_wld.h, 16)
                a_r = awrT.rearrange("(c p) o -> c p o", p=128)
                for c in range(NCH):
                    g.dma_start(out=awr_s[:, c, :], in_=a_r[c]).then_inc(s_wld.h, 16)
                g.dma_start(out=bfl_s[:, :], in_=bflT[:, :]).then_inc(s_wld.h, 16)
            E.add("gpsimd", ld_w4)

            n_ld4 = s_ldx.reserve(16)

            def ld_bb(sync):
                sync.dma_start(out=bb_s[:, :], in_=bbase[:, :]).then_inc(s_ldx.h, 16)
            E.add("sync", ld_bb)

            def gate4(e, a=n_wld4, b_=n_ld4):
                e.wait_ge(s_wld.h, a)
                e.wait_ge(s_ldx.h, b_)
            E.add("tensor", gate4)
            E.add("vector", gate4)
            E.add("scalar", gate4)

            out_r = out_q.rearrange("(g p) d -> g p d", p=128)

            ln3_mark = []
            r2_mark = []
            for tb in range(NBLK):
                for sub in range(4):
                    g = tb * 4 + sub
                    need_lg = ln3_mark[g - 2] if g >= 2 else 0
                    t_rav = s_ra[g % 2].reserve(16)

                    def ld_r1(sync, g=g, need=need_lg):
                        if need > 0:
                            sync.wait_ge(s_ln2.h, need)
                        sync.dma_start(out=r1a[:, g % 2, :], in_=r1_r[g]).then_inc(
                            s_ra[g % 2].h, 16)
                    E.add("sync", ld_r1)

                    need_tp = s_tp.n

                    def v_lngate4(v, g=g, t_rav=t_rav, need_tp=need_tp):
                        v.wait_ge(s_ra[g % 2].h, t_rav)
                        if need_tp > 0:
                            v.wait_ge(s_tp.h, need_tp)
                    E.add("vector", v_lngate4)
                    queue_ln(E, (lambda g=g: r1a[:, g % 2, :]), (lambda: ln2[:, :]))
                    ln3_mark.append(s_ln2.n)

                    for rnd in range(2):
                        need_ln = s_ln2.n
                        need_ev = s_evT.n
                        t_tpv = s_tp.reserve()

                        def t_tp4(t, rnd=rnd, need_ln=need_ln, need_ev=need_ev):
                            if rnd == 0:
                                t.wait_ge(s_ln2.h, need_ln)
                            if need_ev > 0:
                                t.wait_ge(s_evT.h, need_ev)
                            for i in range(8):
                                c = rnd * 8 + i
                                mmi = t.transpose(pT4[:, i * 128:(i + 1) * 128],
                                                  ln2[:, c * 128:(c + 1) * 128], ids[:, :])
                            mmi.then_inc(s_tp.h, 1)
                        E.add("tensor", t_tp4)

                        s_evT.reserve()

                        def v_ev4(v, sub=sub, rnd=rnd, need=t_tpv):
                            v.wait_ge(s_tp.h, need)
                            v.tensor_copy(
                                ln2T[:, rnd * 8:(rnd + 1) * 8, sub * 128:(sub + 1) * 128],
                                pT4[:, :].rearrange("p (c f) -> p c f", f=128)).then_inc(
                                s_evT.h, 1)
                        E.add("vector", v_ev4)

                need_g = s_g.n
                t_awrv = s_mm3.reserve()

                def t_awr(t, need=s_evT.n, need_g=need_g):
                    t.wait_ge(s_evT.h, need)
                    if need_g > 0:
                        t.wait_ge(s_g.h, need_g)
                    for c in range(NCH):
                        mmi = t.matmul(pH[0:72, :], awr_s[:, c, :], ln2T[:, c, :],
                                       start=(c == 0), stop=(c == NCH - 1))
                    mmi.then_inc(s_mm3.h, 1)
                E.add("tensor", t_awr)

                va = s_v.reserve()
                t_gv = s_g.reserve()

                def v_hev(v, need=t_awrv, va=va):
                    v.wait_ge(s_mm3.h, need)
                    v.tensor_copy(hbuf[0:64, :], pH[0:64, :]).then_inc(s_v.h, 1)
                    v.wait_ge(s_v.h, va)
                    v.tensor_copy(hbufL[:, :], pH[64:72, :]).then_inc(s_g.h, 1)
                E.add("vector", v_hev)

                need_cons = s_lda.n
                for sub in range(4):
                    t_lgv = s_tp.reserve()

                    def t_lg(t, sub=sub, need=t_gv, need_cons=need_cons):
                        if sub == 0:
                            t.wait_ge(s_g.h, need)
                            if need_cons > 0:
                                t.wait_ge(s_lda.h, need_cons)
                        t.transpose(pG[:, sub * 8:(sub + 1) * 8],
                                    hbufL[:, sub * 128:(sub + 1) * 128],
                                    idsf[0:8, 0:8]).then_inc(s_tp.h, 1)
                    E.add("tensor", t_lg)

                lg_need = s_tp.n
                t_lgc = s_lda.reserve()

                def v_lg(v, need=lg_need):
                    v.wait_ge(s_tp.h, need)
                    v.tensor_copy(
                        lgt[:, :, :], pG[:, 0:32].rearrange(
                            "p (s e) -> p s e", e=8)).then_inc(s_lda.h, 1)
                E.add("vector", v_lg)

                for sub in range(4):
                    va = s_v.reserve(4)
                    t_ind = s_msk.reserve()

                    def v_g1(v, sub=sub, need=t_lgc, va=va):
                        if sub == 0:
                            v.wait_ge(s_lda.h, need)
                        l = lgt[:, sub, :]
                        v.reduce_max(negm1[:, :], l, axis=AX.X,
                                     negate=True).then_inc(s_v.h, 1)
                        v.wait_ge(s_v.h, va - 3)
                        v.tensor_scalar(out=eqb[:, :], in0=l, scalar1=negm1[:, :],
                                        scalar2=0.0, op0=ALU.add,
                                        op1=ALU.is_ge).then_inc(s_v.h, 1)
                        v.wait_ge(s_v.h, va - 2)
                        v.scalar_tensor_tensor(out=lmk[:, :], in0=eqb[:, :],
                                               scalar=-30000.0, in1=l,
                                               op0=ALU.mult, op1=ALU.add).then_inc(s_v.h, 1)
                        v.wait_ge(s_v.h, va - 1)
                        v.reduce_max(m2b[:, :], lmk[:, :], axis=AX.X).then_inc(s_v.h, 1)
                        v.wait_ge(s_v.h, va)
                        v.tensor_scalar(out=indb[:, :], in0=l, scalar1=m2b[:, :],
                                        scalar2=None, op0=ALU.is_ge).then_inc(s_msk.h, 1)
                    E.add("vector", v_g1)

                    t_px = s_rs.reserve()

                    def s_px(scl, sub=sub, need=t_ind):
                        scl.wait_ge(s_msk.h, need)
                        scl.activation(out=pexp[:, :], in_=lgt[:, sub, :],
                                       func=AF.Exp, bias=negm1[:, :],
                                       scale=1.0).then_inc(s_rs.h, 1)
                    E.add("scalar", s_px)

                    va = s_v.reserve(3)
                    t_wnv = s_ln2.reserve()

                    def v_g2(v, need=t_px, va=va):
                        v.wait_ge(s_rs.h, need)
                        v.tensor_tensor(wgt[:, :], indb[:, :], pexp[:, :],
                                        op=ALU.mult).then_inc(s_v.h, 1)
                        v.wait_ge(s_v.h, va - 2)
                        v.reduce_sum(zg[:, :], wgt[:, :], axis=AX.X).then_inc(s_v.h, 1)
                        v.wait_ge(s_v.h, va - 1)
                        v.reciprocal(zgi[:, :], zg[:, :]).then_inc(s_v.h, 1)
                        v.wait_ge(s_v.h, va)
                        v.tensor_scalar(out=wn[:, :], in0=wgt[:, :],
                                        scalar1=zgi[:, :], scalar2=LORA_SCALE,
                                        op0=ALU.mult, op1=ALU.mult).then_inc(s_ln2.h, 1)
                    E.add("vector", v_g2)

                    need_ev = s_yev.n
                    t_wtv = s_tp.reserve()

                    def t_wnT(t, need=t_wnv, need_ev=need_ev):
                        t.wait_ge(s_ln2.h, need)
                        if need_ev > 0:
                            t.wait_ge(s_yev.h, need_ev)
                        t.transpose(pG[0:8, 128:256], wn[:, :],
                                    idsf[:, :]).then_inc(s_tp.h, 1)
                    E.add("tensor", t_wnT)

                    t_wcv = s_r1.reserve()

                    def v_wnT(v, need=t_wtv):
                        v.wait_ge(s_tp.h, need)
                        v.tensor_copy(wnT[:, :], pG[0:8, 128:256]).then_inc(s_r1.h, 1)
                    E.add("vector", v_wnT)

                    t_e8v = s_bc.reserve()

                    def t_e8(t, need=t_wcv):
                        t.wait_ge(s_r1.h, need)
                        t.matmul(pG[0:64, 256:384], e8s[:, :], wnT[:, :],
                                 start=True, stop=True).then_inc(s_bc.h, 1)
                    E.add("tensor", t_e8)

                    s_yev.reserve()

                    def v_we(v, sub=sub, need=t_e8v):
                        v.wait_ge(s_bc.h, need)
                        v.tensor_copy(wexpT[:, sub * 128:(sub + 1) * 128],
                                      pG[0:64, 256:384]).then_inc(s_yev.h, 1)
                    E.add("vector", v_we)

                t_h2v = s_g.reserve()

                def v_h2(v, need=s_yev.n):
                    v.wait_ge(s_yev.h, need)
                    v.tensor_tensor(h2[:, :], hbuf[0:64, :], wexpT[:, :],
                                    op=ALU.mult).then_inc(s_g.h, 1)
                E.add("vector", v_h2)

                for oc in range(NCH):
                    need_ev = s_exp.n
                    t_fv = s_mm.reserve()

                    def t_ffn(t, oc=oc, need_h2=t_h2v, need_ev=need_ev):
                        if oc == 0:
                            t.wait_ge(s_g.h, need_h2)
                        if need_ev > 0:
                            t.wait_ge(s_exp.h, need_ev)
                        for c in range(NCH):
                            t.matmul(pY4[:, :], wb_s[:, c, oc * 128:(oc + 1) * 128],
                                     ln2T[:, c, :], start=(c == 0), stop=False)
                        t.matmul(pY4[:, :], bfl_s[:, oc * 128:(oc + 1) * 128],
                                 h2[:, :], start=False, stop=True).then_inc(s_mm.h, 1)
                    E.add("tensor", t_ffn)

                    s_exp.reserve()

                    def s_fev(scl, oc=oc, need=t_fv):
                        scl.wait_ge(s_mm.h, need)
                        scl.activation(out=ffnT[:, oc, :], in_=pY4[:, :],
                                       func=AF.Identity, bias=bb_s[:, oc:oc + 1],
                                       scale=1.0).then_inc(s_exp.h, 1)
                    E.add("scalar", s_fev)

                for rnd in range(8):
                    need_ev = s_exp.n
                    need_pt = s_evT.n
                    t_tpv = s_tp.reserve()

                    def t_tpf(t, rnd=rnd, need_ev=need_ev, need_pt=need_pt):
                        if rnd == 0:
                            t.wait_ge(s_exp.h, need_ev)
                        if need_pt > 0:
                            t.wait_ge(s_evT.h, need_pt)
                        for i in range(8):
                            j = rnd * 8 + i
                            oc, sub = j // 4, j % 4
                            mmi = t.transpose(pT4[:, i * 128:(i + 1) * 128],
                                              ffnT[:, oc, sub * 128:(sub + 1) * 128],
                                              ids[:, :])
                        mmi.then_inc(s_tp.h, 1)
                    E.add("tensor", t_tpf)

                    s_evT.reserve()

                    def v_ptf(v, rnd=rnd, need=t_tpv):
                        v.wait_ge(s_tp.h, need)
                        oc0 = rnd * 2
                        v.tensor_copy(
                            ffn[:, :, oc0 * 128:(oc0 + 2) * 128].rearrange(
                                "p s (c f) -> p c s f", f=128),
                            pT4[:, :].rearrange("p (c s f) -> p c s f",
                                                c=2, s=4)).then_inc(s_evT.h, 1)
                    E.add("vector", v_ptf)

                for sub in range(4):
                    g = tb * 4 + sub
                    need_r2g = r2_mark[g - 2] if g >= 2 else 0
                    t_rcv = s_rc[g % 2].reserve(16)

                    def ld_r1b(sync, g=g, need=need_r2g):
                        if need > 0:
                            sync.wait_ge(s_o.h, need)
                        sync.dma_start(out=r1c[:, g % 2, :], in_=r1_r[g]).then_inc(
                            s_rc[g % 2].h, 16)
                    E.add("sync", ld_r1b)

                    need_t = s_evT.n

                    def v_lngate5(v, need_t=need_t):
                        v.wait_ge(s_evT.h, need_t)
                    E.add("vector", v_lngate5)
                    tnorm = queue_ln(E, (lambda sub=sub: ffn[:, sub, :]),
                                     (lambda: t4[:, :]))

                    need_st = s_so[g % 2].n if g >= 2 else 0
                    t_r2v = s_o.reserve()

                    def v_r2(v, g=g, need_ld=t_rcv, need_st=need_st, tnorm=tnorm):
                        v.wait_ge(s_rc[g % 2].h, need_ld)
                        v.wait_ge(s_ln2.h, tnorm)
                        if need_st > 0:
                            v.wait_ge(s_so[g % 2].h, need_st)
                        v.tensor_tensor(r2b[:, g % 2, :], t4[:, :],
                                        r1c[:, g % 2, :], op=ALU.add).then_inc(s_o.h, 1)
                    E.add("vector", v_r2)
                    r2_mark.append(s_o.n)

                    s_so[g % 2].reserve(16)

                    def sc_out(scl, g=g, need=t_r2v):
                        scl.wait_ge(s_o.h, need)
                        scl.dma_start(out=out_r[g], in_=r2b[:, g % 2, :]).then_inc(
                            s_so[g % 2].h, 16)
                    E.add("scalar", sc_out)

            def p4_fin(scl, a0=s_so[0].n, a1=s_so[1].n):
                scl.wait_ge(s_so[0].h, a0)
                scl.wait_ge(s_so[1].h, a1)
            E.add("scalar", p4_fin)
            E.flush(blk4)

    return nc


def _host_tables():
    half = DK // 2
    inv_freq = 10000.0 ** (-2.0 * np.arange(half) / DK)
    ang = np.arange(S)[:, None] * inv_freq[None, :]
    cosT = np.cos(ang).T.astype(np.float32)
    sinT = np.sin(ang).T.astype(np.float32)
    cc = np.concatenate([cosT, cosT], axis=0).astype(ml_dtypes.bfloat16)
    ss2 = np.concatenate([-sinT, sinT], axis=0).astype(ml_dtypes.bfloat16)
    masks = np.zeros((4 * 128, 512), dtype=ml_dtypes.bfloat16)
    for i in range(4):
        for kk in range(128):
            masks[i * 128 + kk, i * 128 + kk:] = 1.0
    ident = np.eye(128, dtype=ml_dtypes.bfloat16)
    pswap = np.zeros((128, 128), dtype=ml_dtypes.bfloat16)
    for i in range(64):
        pswap[i + 64, i] = 1.0
        pswap[i, i + 64] = 1.0
    e8 = np.zeros((8, 64), dtype=ml_dtypes.bfloat16)
    for e in range(8):
        e8[e, e * 8:(e + 1) * 8] = 1.0
    return cc, ss2, masks, ident, pswap, e8


def _perm_rows(w):
    perm = []
    for h in range(H):
        base = h * DK
        perm += [base + 2 * i for i in range(DK // 2)]
        perm += [base + 2 * i + 1 for i in range(DK // 2)]
    return w[perm, :]


_NC_CACHE = {}


def kernel(L0_attn_x, L0_attn_wq, L0_attn_wk, L0_attn_wv, L0_attn_wo,
           L0_ffn_W_base, L0_ffn_b_base, L0_ffn_W_router, L0_ffn_A_w, L0_ffn_B_w):
    x = np.ascontiguousarray(np.asarray(L0_attn_x, dtype=np.float32))
    wq = np.asarray(L0_attn_wq, dtype=np.float32)
    wk = np.asarray(L0_attn_wk, dtype=np.float32)
    wv = np.asarray(L0_attn_wv, dtype=np.float32)
    wo = np.asarray(L0_attn_wo, dtype=np.float32)
    wb = np.asarray(L0_ffn_W_base, dtype=np.float32)
    bb = np.asarray(L0_ffn_b_base, dtype=np.float32)
    wr = np.asarray(L0_ffn_W_router, dtype=np.float32)
    aw = np.asarray(L0_ffn_A_w, dtype=np.float32)
    bw = np.asarray(L0_ffn_B_w, dtype=np.float32)

    cc, ss2, masks, ident, pswap, e8 = _host_tables()

    wqTf = np.ascontiguousarray(_perm_rows(wq).T)
    wkTf = np.ascontiguousarray(_perm_rows(wk).T)
    wvTf = np.ascontiguousarray(wv.T)
    woTf = np.ascontiguousarray(wo.T)
    wbTf = np.ascontiguousarray(wb.T)
    aflT = aw.reshape(64, D).T
    wrTf = wr.T
    awrTf = np.ascontiguousarray(np.concatenate([aflT, wrTf], axis=1))
    bflTf = np.ascontiguousarray(bw.transpose(0, 2, 1).reshape(64, D))
    bbasef = np.ascontiguousarray(bb.reshape(16, 128).T)

    if "nc" not in _NC_CACHE:
        _NC_CACHE["nc"] = build_program()
    nc = _NC_CACHE["nc"]

    in_maps = []
    for c in range(8):
        b = c // 4
        hg = c % 4
        tq = c % 4
        in_maps.append({
            "x": x[b],
            "xq": np.ascontiguousarray(x[b, tq * QT:(tq + 1) * QT, :]),
            "wqT": np.ascontiguousarray(wqTf[:, hg * 512:(hg + 1) * 512]),
            "wkT": np.ascontiguousarray(wkTf[:, hg * 512:(hg + 1) * 512]),
            "wvT": np.ascontiguousarray(wvTf[:, hg * 512:(hg + 1) * 512]),
            "woT": woTf, "wbT": wbTf, "awrT": awrTf, "bflT": bflTf, "bbase": bbasef,
            "ident": ident, "identf": np.eye(128, dtype=np.float32),
            "pswap": pswap, "e8": e8,
            "cctab": cc, "ss2tab": ss2, "masks": masks,
        })
    res = run_bass_kernel_spmd(nc, in_maps, core_ids=list(range(8)))
    out = np.empty((B, S, D), dtype=np.float32)
    for c in range(8):
        b = c // 4
        tq = c % 4
        out[b, tq * QT:(tq + 1) * QT, :] = res.results[c]["out_q"]
    return out


# version 09 (final)
# speedup vs baseline: 1.4813x; 1.0282x over previous
import contextlib
import numpy as np
import ml_dtypes
import concourse.bass as bass
import concourse.mybir as mybir
from concourse.bass_utils import run_bass_kernel_spmd

F32 = mybir.dt.float32
I32 = mybir.dt.int32
BF16 = mybir.dt.bfloat16
AF = mybir.ActivationFunctionType
ALU = mybir.AluOpType
AX = mybir.AxisListType

B = 2
S = 8192
D = 2048
H = 16
HL = 4
DK = 128
NCH = D // 128
SBK = 512
NSB = S // SBK
QT = 2048
NBLK = 4
NKT = S // 128
CKT = 16
CEXP = 16.0
EPS = 1e-5
SCL = 1.0 / np.sqrt(DK)
LORA_SCALE = 2.0
RSQRT_MAGIC = 1597463007.0


class Sem:
    def __init__(self, nc, es, name):
        self.h = es.enter_context(nc.semaphore(name))
        self.n = 0

    def reserve(self, k=1):
        self.n += k
        return self.n


class Emit:
    ENGS = ("sync", "gpsimd", "tensor", "scalar", "vector")

    def __init__(self):
        self.q = {e: [] for e in self.ENGS}

    def add(self, eng, fn):
        self.q[eng].append(fn)

    def flush(self, block):
        for e in self.ENGS:
            fns = self.q[e]
            self.q[e] = []
            if fns:
                def body(hdl, fns=fns):
                    for f in fns:
                        f(hdl)
                getattr(block, e)(body)


def build_program():
    nc = bass.Bass("TRN2", target_bir_lowering=False, debug=False)

    x_in = nc.dram_tensor("x", [S, D], F32, kind="ExternalInput")
    xq_in = nc.dram_tensor("xq", [QT, D], F32, kind="ExternalInput")
    wqT = nc.dram_tensor("wqT", [D, HL * DK], F32, kind="ExternalInput")
    wkT = nc.dram_tensor("wkT", [D, HL * DK], F32, kind="ExternalInput")
    wvT = nc.dram_tensor("wvT", [D, HL * DK], F32, kind="ExternalInput")
    woT = nc.dram_tensor("woT", [D, D], F32, kind="ExternalInput")
    wbT = nc.dram_tensor("wbT", [D, D], F32, kind="ExternalInput")
    awrT = nc.dram_tensor("awrT", [D, 72], F32, kind="ExternalInput")
    bflT = nc.dram_tensor("bflT", [64, D], F32, kind="ExternalInput")
    bbase = nc.dram_tensor("bbase", [128, 16], F32, kind="ExternalInput")
    ident_in = nc.dram_tensor("ident", [128, 128], BF16, kind="ExternalInput")
    identf_in = nc.dram_tensor("identf", [128, 128], F32, kind="ExternalInput")
    pswap_in = nc.dram_tensor("pswap", [128, 128], BF16, kind="ExternalInput")
    e8_in = nc.dram_tensor("e8", [8, 64], BF16, kind="ExternalInput")
    cc_tab = nc.dram_tensor("cctab", [128, S], BF16, kind="ExternalInput")
    ss2_tab = nc.dram_tensor("ss2tab", [128, S], BF16, kind="ExternalInput")
    masks_in = nc.dram_tensor("masks", [4 * 128, 512], BF16, kind="ExternalInput")

    out_q = nc.dram_tensor("out_q", [QT, D], F32, kind="ExternalOutput")

    qsp = [nc.dram_tensor(f"qsp{h}", [128, NSB, SBK], BF16) for h in range(HL)]
    ksp = [nc.dram_tensor(f"ksp{h}", [128, NKT, 128], BF16) for h in range(HL)]
    vsp = [nc.dram_tensor(f"vsp{h}", [128, NKT, 128], BF16) for h in range(HL)]
    ccg_in = [nc.dram_tensor(f"ccin{k}", [HL * DK, QT], BF16) for k in range(4)]
    ccg_out = [nc.dram_tensor(f"ccout{k}", [4 * HL * DK, QT], BF16) for k in range(4)]
    r1sp = nc.dram_tensor("r1sp", [QT, D], F32)

    es = contextlib.ExitStack()
    with es:
        sb = lambda name, shape, dt: es.enter_context(nc.sbuf_tensor(name, shape, dt))

        ids = sb("ids", [128, 128], BF16)
        idsf = sb("idsf", [128, 128], F32)
        psw = sb("psw", [128, 128], BF16)
        e8s = sb("e8s", [8, 64], BF16)
        ones128 = sb("ones128", [128, 128], BF16)
        onecol = sb("onecol", [128, 1], BF16)
        onecf = sb("onecf", [128, 1], F32)
        onerow = sb("onerow", [1, 128], F32)
        cexp = sb("cexp", [128, 1], F32)
        bnst = sb("bnst", [128, 24], F32)
        mv = sb("mv", [128, 2], F32)
        rstd = sb("rstd", [128, 1], F32)
        rs_t1 = sb("rs_t1", [128, 1], F32)
        rs_t2 = sb("rs_t2", [128, 1], F32)

        s_v = Sem(nc, es, "s_v")
        s_cst = Sem(nc, es, "s_cst")
        s_ldx = Sem(nc, es, "s_ldx")
        s_wld = Sem(nc, es, "s_wld")
        s_x = [Sem(nc, es, f"s_x{i}") for i in range(4)]
        s_spq = [Sem(nc, es, f"s_spq{i}") for i in range(2)]
        s_spv = [Sem(nc, es, f"s_spv{i}") for i in range(2)]
        s_q = [Sem(nc, es, f"s_q{i}") for i in range(2)]
        s_kv = [Sem(nc, es, f"s_kv{i}") for i in range(4)]
        s_sto = [Sem(nc, es, f"s_sto{i}") for i in range(2)]
        s_xq = [Sem(nc, es, f"s_xq{i}") for i in range(2)]
        s_sr = [Sem(nc, es, f"s_sr{i}") for i in range(2)]
        s_ra = [Sem(nc, es, f"s_ra{i}") for i in range(2)]
        s_rc = [Sem(nc, es, f"s_rc{i}") for i in range(2)]
        s_so = [Sem(nc, es, f"s_so{i}") for i in range(2)]
        s_ln = Sem(nc, es, "s_ln")
        s_ln2 = Sem(nc, es, "s_ln2")
        s_tp = Sem(nc, es, "s_tp")
        s_evT = Sem(nc, es, "s_evT")
        s_proj = Sem(nc, es, "s_proj")
        s_pe = Sem(nc, es, "s_pe")
        s_sw = Sem(nc, es, "s_sw")
        s_rope = Sem(nc, es, "s_rope")
        s_vev = Sem(nc, es, "s_vev")
        s_mm = Sem(nc, es, "s_mm")
        s_exp = Sem(nc, es, "s_exp")
        s_msk = Sem(nc, es, "s_msk")
        s_av = Sem(nc, es, "s_av")
        s_zi = Sem(nc, es, "s_zi")
        s_bc = Sem(nc, es, "s_bc")
        s_o = Sem(nc, es, "s_o")
        s_cc = Sem(nc, es, "s_cc")
        s_lda = Sem(nc, es, "s_lda")
        s_mm3 = Sem(nc, es, "s_mm3")
        s_yev = Sem(nc, es, "s_yev")
        s_r1 = Sem(nc, es, "s_r1")
        s_g = Sem(nc, es, "s_g")
        s_rs = Sem(nc, es, "s_rs")
        s_zd = Sem(nc, es, "s_zd")
        s_zp = Sem(nc, es, "s_zp")
        s_zc = Sem(nc, es, "s_zc")
        s_rp1 = Sem(nc, es, "s_rp1")
        s_pl = Sem(nc, es, "s_pl")

        def queue_rsqrt(E):
            r = [s_v.reserve() for _ in range(9)]
            dn = s_rs.reserve()

            def body(v, r=r, dn=dn):
                v.tensor_scalar(out=rs_t1[:, :], in0=mv[:, 1:2], scalar1=EPS,
                                scalar2=None, op0=ALU.add).then_inc(s_v.h, 1)
                v.wait_ge(s_v.h, r[0])
                v.tensor_scalar(out=rs_t2[:, :], in0=rs_t1[:, :], scalar1=-0.5,
                                scalar2=None, op0=ALU.mult).then_inc(s_v.h, 1)
                v.wait_ge(s_v.h, r[1])
                v.tensor_scalar(out=rstd[:, :].bitcast(I32),
                                in0=rs_t1[:, :].bitcast(I32), scalar1=1,
                                scalar2=None,
                                op0=ALU.logical_shift_right).then_inc(s_v.h, 1)
                v.wait_ge(s_v.h, r[2])
                v.tensor_scalar(out=rstd[:, :].bitcast(I32),
                                in0=rstd[:, :].bitcast(I32), scalar1=-1.0,
                                scalar2=RSQRT_MAGIC, op0=ALU.mult,
                                op1=ALU.add).then_inc(s_v.h, 1)
                v.wait_ge(s_v.h, r[3])
                k = 4
                for it in range(2):
                    v.tensor_tensor(rs_t1[:, :], rstd[:, :], rstd[:, :],
                                    op=ALU.mult).then_inc(s_v.h, 1)
                    v.wait_ge(s_v.h, r[k]); k += 1
                    v.tensor_scalar(out=rs_t1[:, :], in0=rs_t1[:, :],
                                    scalar1=rs_t2[:, :], scalar2=1.5,
                                    op0=ALU.mult, op1=ALU.add).then_inc(s_v.h, 1)
                    v.wait_ge(s_v.h, r[k]); k += 1
                    inst = v.tensor_tensor(rstd[:, :], rstd[:, :], rs_t1[:, :],
                                           op=ALU.mult)
                    if it == 1:
                        inst.then_inc(s_rs.h, 1)
                    else:
                        inst.then_inc(s_v.h, 1)
                        v.wait_ge(s_v.h, r[k]); k += 1
            E.add("vector", body)
            return dn

        def queue_ln(E, src_fn, out_fn):
            t_stats = s_v.reserve()
            t_aggr = s_ln.reserve()

            def v_stats(v, src_fn=src_fn, t_stats=t_stats, t_aggr=t_aggr):
                src = src_fn()
                for i in range(3):
                    v.bn_stats(bnst[:, i * 6:(i + 1) * 6], src[:, i * 512:(i + 1) * 512])
                v.bn_stats(bnst[:, 18:24], src[:, 1536:2048]).then_inc(s_v.h, 1)
                v.wait_ge(s_v.h, t_stats)
                v.bn_aggr(mv[:, :], bnst[:, :]).then_inc(s_ln.h, 1)
                v.wait_ge(s_ln.h, t_aggr)
            E.add("vector", v_stats)
            dn = queue_rsqrt(E)
            t_norm = s_ln2.reserve()

            def v_norm(v, src_fn=src_fn, out_fn=out_fn, dn=dn):
                v.wait_ge(s_rs.h, dn)
                v.tensor_scalar(out=out_fn(), in0=src_fn(),
                                scalar1=mv[:, 0:1], scalar2=rstd[:, :],
                                op0=ALU.subtract, op1=ALU.mult).then_inc(s_ln2.h, 1)
            E.add("vector", v_norm)
            return t_norm

        with contextlib.ExitStack() as es1:
            sb1 = lambda n_, sh, dt: es1.enter_context(nc.sbuf_tensor(n_, sh, dt))
            ps1 = lambda n_, sh, dt: es1.enter_context(nc.psum_tensor(n_, sh, dt))
            xs = sb1("xs", [128, 4, D], F32)
            lnb = sb1("lnb", [128, 2, D], BF16)
            mv4 = sb1("mv4", [128, 4, 2], F32)
            rstd4 = sb1("rstd4", [128, 4], F32)
            rs4a = sb1("rs4a", [128, 4], F32)
            rs4b = sb1("rs4b", [128, 4], F32)
            lnT = sb1("lnT", [128, NCH, SBK], BF16)
            wq_s = sb1("wq_s", [128, NCH, HL * DK], BF16)
            wk_s = sb1("wk_s", [128, NCH, HL * DK], BF16)
            wv_s = sb1("wv_s", [128, NCH, HL * DK], BF16)
            ccs = sb1("ccs", [128, S], BF16)
            ss2s = sb1("ss2s", [128, S], BF16)
            pre = sb1("pre", [128, 2, SBK], BF16)
            swp = sb1("swp", [128, 2, SBK], BF16)
            tmpA = sb1("tmpA", [128, 2, SBK], BF16)
            ropo = sb1("ropo", [128, 2, SBK], BF16)
            vstg = sb1("vstg", [128, 2, HL * DK], BF16)
            pTa = ps1("pTa", [128, 1024], BF16)
            pTb = ps1("pTb", [128, 1024], BF16)
            pTd = [pTa, pTb]
            pPJ0 = ps1("pPJ0", [128, 512], F32)
            pPJ1 = ps1("pPJ1", [128, 512], F32)
            pSw0 = ps1("pSw0", [128, 512], F32)
            pSw1 = ps1("pSw1", [128, 512], F32)
            pPJ = [pPJ0, pPJ1]
            pSw = [pSw0, pSw1]

            blk1 = es1.enter_context(nc.Block("p1"))
            E = Emit()

            n_ldi1 = s_ldx.reserve(16 * 6)

            def ld_init(sync):
                sync.dma_start(out=ids[:, :], in_=ident_in[:, :]).then_inc(s_ldx.h, 16)
                sync.dma_start(out=idsf[:, :], in_=identf_in[:, :]).then_inc(s_ldx.h, 16)
                sync.dma_start(out=psw[:, :], in_=pswap_in[:, :]).then_inc(s_ldx.h, 16)
                sync.dma_start(out=e8s[:, :], in_=e8_in[:, :]).then_inc(s_ldx.h, 16)
                sync.dma_start(out=ccs[:, :], in_=cc_tab[:, :]).then_inc(s_ldx.h, 16)
                sync.dma_start(out=ss2s[:, :], in_=ss2_tab[:, :]).then_inc(s_ldx.h, 16)
            E.add("sync", ld_init)

            n_wld1 = s_wld.reserve(16 * 3 * NCH)

            def ld_w(g):
                for w_d, w_s in ((wqT, wq_s), (wkT, wk_s), (wvT, wv_s)):
                    w_r = w_d.rearrange("(c p) o -> c p o", p=128)
                    for c in range(NCH):
                        g.dma_start(out=w_s[:, c, :], in_=w_r[c]).then_inc(s_wld.h, 16)
            E.add("gpsimd", ld_w)

            n_cst = s_cst.reserve()

            def consts(v):
                v.memset(onecol[:, :], 1.0)
                v.memset(onecf[:, :], 1.0)
                v.memset(onerow[:, :], 1.0)
                v.memset(ones128[:, :], 1.0)
                v.memset(cexp[:, :], -CEXP).then_inc(s_cst.h, 1)
                v.wait_ge(s_ldx.h, n_ldi1)
                v.wait_ge(s_wld.h, n_wld1)
            E.add("vector", consts)

            def eng_gate1(e, a=n_cst, b_=n_ldi1, c_=n_wld1):
                e.wait_ge(s_cst.h, a)
                e.wait_ge(s_ldx.h, b_)
                e.wait_ge(s_wld.h, c_)
            E.add("scalar", eng_gate1)
            E.add("tensor", eng_gate1)

            x_r = x_in.rearrange("(g p) d -> g p d", p=128)
            ln2_mark = []
            tp_mark = []
            for s_i in range(NSB):
                for tt in range(4):
                    g = 4 * s_i + tt
                    t_x = s_x[tt].reserve(16)
                    need_gx = ln2_mark[4 * (s_i - 1) + tt] if s_i >= 1 else 0

                    def ld_x(sync, g=g, tt=tt, need=need_gx):
                        if need > 0:
                            sync.wait_ge(s_ln2.h, need)
                        sync.dma_start(out=xs[:, tt, :], in_=x_r[g]).then_inc(
                            s_x[tt].h, 16)
                    E.add("sync", ld_x)

                    va = s_v.reserve()
                    t_ag = s_ln.reserve()

                    def v_st(v, tt=tt, t_x=t_x, va=va, t_ag=t_ag):
                        v.wait_ge(s_x[tt].h, t_x)
                        v.wait_ge(s_ln.h, t_ag - 1)
                        for i in range(3):
                            v.bn_stats(bnst[:, i * 6:(i + 1) * 6],
                                       xs[:, tt, i * 512:(i + 1) * 512])
                        v.bn_stats(bnst[:, 18:24], xs[:, tt, 1536:2048]).then_inc(s_v.h, 1)
                        v.wait_ge(s_v.h, va)
                        v.bn_aggr(mv4[:, tt, :], bnst[:, :]).then_inc(s_ln.h, 1)
                    E.add("vector", v_st)

                agg_all = s_ln.n
                r4 = [s_v.reserve() for _ in range(11)]
                t_rs4 = s_rs.reserve()

                def v_rs4(v, agg_all=agg_all, r4=r4):
                    v.wait_ge(s_ln.h, agg_all)
                    v.tensor_scalar(out=rs4a[:, :], in0=mv4[:, :, 1], scalar1=EPS,
                                    scalar2=None, op0=ALU.add).then_inc(s_v.h, 1)
                    v.wait_ge(s_v.h, r4[0])
                    v.tensor_scalar(out=rs4b[:, :], in0=rs4a[:, :], scalar1=-0.5,
                                    scalar2=None, op0=ALU.mult).then_inc(s_v.h, 1)
                    v.wait_ge(s_v.h, r4[1])
                    v.tensor_scalar(out=rstd4[:, :].bitcast(I32),
                                    in0=rs4a[:, :].bitcast(I32), scalar1=1,
                                    scalar2=None,
                                    op0=ALU.logical_shift_right).then_inc(s_v.h, 1)
                    v.wait_ge(s_v.h, r4[2])
                    v.tensor_scalar(out=rstd4[:, :].bitcast(I32),
                                    in0=rstd4[:, :].bitcast(I32), scalar1=-1.0,
                                    scalar2=RSQRT_MAGIC, op0=ALU.mult,
                                    op1=ALU.add).then_inc(s_v.h, 1)
                    v.wait_ge(s_v.h, r4[3])
                    k = 4
                    for itn in range(2):
                        v.tensor_tensor(rs4a[:, :], rstd4[:, :], rstd4[:, :],
                                        op=ALU.mult).then_inc(s_v.h, 1)
                        v.wait_ge(s_v.h, r4[k]); k += 1
                        v.tensor_tensor(rs4a[:, :], rs4a[:, :], rs4b[:, :],
                                        op=ALU.mult).then_inc(s_v.h, 1)
                        v.wait_ge(s_v.h, r4[k]); k += 1
                        if itn == 0:
                            v.tensor_scalar(out=rs4a[:, :], in0=rs4a[:, :], scalar1=1.5,
                                            scalar2=None, op0=ALU.add).then_inc(s_v.h, 1)
                            v.wait_ge(s_v.h, r4[k]); k += 1
                            v.tensor_tensor(rstd4[:, :], rstd4[:, :], rs4a[:, :],
                                            op=ALU.mult).then_inc(s_v.h, 1)
                            v.wait_ge(s_v.h, r4[k]); k += 1
                        else:
                            v.tensor_scalar(out=rs4a[:, :], in0=rs4a[:, :], scalar1=1.5,
                                            scalar2=None, op0=ALU.add).then_inc(s_v.h, 1)
                            v.wait_ge(s_v.h, r4[k]); k += 1
                            v.tensor_tensor(rstd4[:, :], rstd4[:, :], rs4a[:, :],
                                            op=ALU.mult).then_inc(s_rs.h, 1)
                E.add("vector", v_rs4)

                for tt in range(4):
                    g = 4 * s_i + tt
                    need_tp = tp_mark[g - 2] if g >= 2 else 0
                    t_nm = s_ln2.reserve()

                    def v_nm(v, tt=tt, t_rs4=t_rs4, need_tp=need_tp):
                        v.wait_ge(s_rs.h, t_rs4)
                        if need_tp > 0:
                            v.wait_ge(s_tp.h, need_tp)
                        v.tensor_scalar(out=lnb[:, tt % 2, :], in0=xs[:, tt, :],
                                        scalar1=mv4[:, tt, 0:1],
                                        scalar2=rstd4[:, tt:tt + 1],
                                        op0=ALU.subtract, op1=ALU.mult).then_inc(
                                        s_ln2.h, 1)
                    E.add("vector", v_nm)
                    ln2_mark.append(s_ln2.n)

                    for rnd in range(2):
                        tps = s_evT.n % 2
                        need_ev = s_evT.n - 1
                        t_tpv = s_tp.reserve()

                        def t_tp(t, tt=tt, rnd=rnd, need_ln=t_nm, need_ev=need_ev,
                                 tps=tps):
                            if rnd == 0:
                                t.wait_ge(s_ln2.h, need_ln)
                            if need_ev > 0:
                                t.wait_ge(s_evT.h, need_ev)
                            for i in range(8):
                                c = rnd * 8 + i
                                mmi = t.transpose(pTd[tps][:, i * 128:(i + 1) * 128],
                                                  lnb[:, tt % 2, c * 128:(c + 1) * 128],
                                                  ids[:, :])
                            mmi.then_inc(s_tp.h, 1)
                        E.add("tensor", t_tp)

                        s_evT.reserve()

                        def v_ev(v, tt=tt, rnd=rnd, need=t_tpv, tps=tps):
                            v.wait_ge(s_tp.h, need)
                            v.tensor_copy(
                                lnT[:, rnd * 8:(rnd + 1) * 8, tt * 128:(tt + 1) * 128],
                                pTd[tps][:, :].rearrange("p (c f) -> p c f",
                                                         f=128)).then_inc(
                                s_evT.h, 1)
                        E.add("vector", v_ev)
                    tp_mark.append(s_tp.n)

                ev_all = s_evT.n

                def t_sb_gate(t, need=ev_all):
                    t.wait_ge(s_evT.h, need)
                E.add("tensor", t_sb_gate)

                for h in range(HL):
                    for which, w_s in (("q", wq_s), ("k", wk_s)):
                        pj = s_proj.n % 2
                        need_pe = s_pe.n - 1
                        t_prj = s_proj.reserve()

                        def t_proj(t, h=h, w_s=w_s, pj=pj, need_pe=need_pe):
                            if need_pe > 0:
                                t.wait_ge(s_pe.h, need_pe)
                            for c in range(NCH):
                                mmi = t.matmul(pPJ[pj][:, :], w_s[:, c, h * 128:(h + 1) * 128],
                                               lnT[:, c, :], start=(c == 0),
                                               stop=(c == NCH - 1))
                            mmi.then_inc(s_proj.h, 1)
                        E.add("tensor", t_proj)

                        prslot = s_rope.n % 2
                        need_rope = s_rope.n - 1
                        t_pev = s_pe.reserve()

                        def v_pe(scl, need=t_prj, pj=pj, prslot=prslot,
                                 need_rope=need_rope):
                            scl.wait_ge(s_proj.h, need)
                            if need_rope > 0:
                                scl.wait_ge(s_rope.h, need_rope)
                            scl.copy(out=pre[:, prslot, :],
                                     in_=pPJ[pj][:, :]).then_inc(s_pe.h, 1)
                        E.add("scalar", v_pe)

                        need_rp = s_rope.n - 1
                        t_swv = s_sw.reserve()

                        def t_sw(t, need=t_pev, prslot=prslot, need_rp=need_rp):
                            t.wait_ge(s_pe.h, need)
                            if need_rp > 0:
                                t.wait_ge(s_rope.h, need_rp)
                            t.matmul(pSw[prslot][:, :], psw[:, :], pre[:, prslot, :],
                                     start=True, stop=True).then_inc(s_sw.h, 1)
                        E.add("tensor", t_sw)

                        ridx = s_rope.n
                        rslot = ridx % 2
                        va = s_v.reserve(2)
                        need_wr = s_spq[rslot].n if ridx >= 2 else 0
                        t_rp = s_rope.reserve()

                        def v_rope(v, s_i=s_i, rslot=rslot, need=t_swv, va=va,
                                   need_wr=need_wr):
                            v.wait_ge(s_sw.h, need)
                            v.tensor_tensor(tmpA[:, rslot, :], pre[:, rslot, :],
                                            ccs[:, s_i * SBK:(s_i + 1) * SBK],
                                            op=ALU.mult).then_inc(s_v.h, 1)
                            v.tensor_tensor(swp[:, rslot, :], pSw[rslot][:, :],
                                            ss2s[:, s_i * SBK:(s_i + 1) * SBK],
                                            op=ALU.mult).then_inc(s_v.h, 1)
                            v.wait_ge(s_v.h, va)
                            if need_wr > 0:
                                v.wait_ge(s_spq[rslot].h, need_wr)
                            v.tensor_tensor(ropo[:, rslot, :], tmpA[:, rslot, :],
                                            swp[:, rslot, :], op=ALU.add).then_inc(s_rope.h, 1)
                        E.add("vector", v_rope)

                        s_spq[rslot].reserve(16)

                        def sc_spill(scl, s_i=s_i, h=h, which=which, rslot=rslot,
                                     need=t_rp):
                            scl.wait_ge(s_rope.h, need)
                            if which == "q":
                                dst = qsp[h][:, s_i, :]
                            else:
                                dst = ksp[h][:, 4 * s_i:4 * s_i + 4, :].rearrange(
                                    "p a b -> p (a b)")
                            scl.dma_start(out=dst, in_=ropo[:, rslot, :]).then_inc(
                                s_spq[rslot].h, 16)
                        E.add("scalar", sc_spill)

                for tt in range(4):
                    g = 4 * s_i + tt
                    pj = s_proj.n % 2
                    need_pe = s_pe.n - 1
                    t_vp = s_proj.reserve()

                    def t_v(t, tt=tt, pj=pj, need_pe=need_pe):
                        if need_pe > 0:
                            t.wait_ge(s_pe.h, need_pe)
                        for c in range(NCH):
                            mmi = t.matmul(pPJ[pj][:, :], lnT[:, c, tt * 128:(tt + 1) * 128],
                                           wv_s[:, c, :], start=(c == 0), stop=(c == NCH - 1))
                        mmi.then_inc(s_proj.h, 1)
                    E.add("tensor", t_v)

                    vslot = g % 2
                    need_wr = s_spv[vslot].n if g >= 2 else 0
                    t_vev = s_pe.reserve()

                    def v_vev(scl, vslot=vslot, pj=pj, need=t_vp, need_wr=need_wr):
                        scl.wait_ge(s_proj.h, need)
                        if need_wr > 0:
                            scl.wait_ge(s_spv[vslot].h, need_wr)
                        scl.copy(out=vstg[:, vslot, :],
                                 in_=pPJ[pj][:, :]).then_inc(s_pe.h, 1)
                    E.add("scalar", v_vev)

                    s_spv[vslot].reserve(16 * HL)

                    def sc_vsp(scl, g=g, vslot=vslot, need=t_vev):
                        scl.wait_ge(s_pe.h, need)
                        for h in range(HL):
                            scl.dma_start(
                                out=vsp[h][:, g, :],
                                in_=vstg[:, vslot, h * 128:(h + 1) * 128]).then_inc(
                                s_spv[vslot].h, 16)
                    E.add("scalar", sc_vsp)

            def p1_fin(scl, a0=s_spq[0].n, a1=s_spq[1].n, b0=s_spv[0].n, b1=s_spv[1].n):
                scl.wait_ge(s_spq[0].h, a0)
                scl.wait_ge(s_spq[1].h, a1)
                scl.wait_ge(s_spv[0].h, b0)
                scl.wait_ge(s_spv[1].h, b1)
            E.add("scalar", p1_fin)
            E.flush(blk1)

        with contextlib.ExitStack() as es2:
            sb2 = lambda n_, sh, dt: es2.enter_context(nc.sbuf_tensor(n_, sh, dt))
            ps2 = lambda n_, sh, dt: es2.enter_context(nc.psum_tensor(n_, sh, dt))
            msk = sb2("msk", [128, 4, 512], BF16)
            qbuf = sb2("qbuf", [128, 2, 512], BF16)
            kbuf = sb2("kbuf", [128, 4, CKT * 128], BF16)
            vbuf = sb2("vbuf", [128, 4, CKT * 128], BF16)
            pb = sb2("pb", [128, 4, 512], BF16)
            binv = sb2("binv", [128, 512], F32)
            oout = sb2("oout", [128, 2, 512], BF16)
            pS = ps2("pS", [128, 4, 512], F32)
            pO0 = ps2("pO0", [128, 512], F32)
            pO1 = ps2("pO1", [128, 512], F32)
            pOb = [pO0, pO1]
            pZ = ps2("pZ", [128, 512], F32)

            blk2 = es2.enter_context(nc.Block("p2"))
            E = Emit()

            n_mskld = s_ldx.reserve(16 * 4)

            def ld_msk(sync):
                m_r = masks_in.rearrange("(t p) q -> t p q", p=128)
                for i in range(4):
                    sync.dma_start(out=msk[:, i, :], in_=m_r[i]).then_inc(s_ldx.h, 16)
            E.add("sync", ld_msk)

            def eng_gate2(e, need=n_mskld):
                e.wait_ge(s_ldx.h, need)
            E.add("vector", eng_gate2)
            E.add("tensor", eng_gate2)
            E.add("scalar", eng_gate2)

            mm_mark_iter = []
            kcons_mark = []
            zr_mark = []
            gchunk = 0
            for qb in range(NSB):
                nkt = 4 * (qb + 1)
                nchunks = (nkt + CKT - 1) // CKT
                for h in range(HL):
                    it = qb * HL + h
                    qslot = it % 2
                    need_qg = mm_mark_iter[it - 2] if it >= 2 else 0
                    q_mark = s_q[qslot].reserve(16)

                    def ld_q(sync, qb=qb, h=h, qslot=qslot, need=need_qg):
                        if need > 0:
                            sync.wait_ge(s_mm.h, need)
                        sync.dma_start(out=qbuf[:, qslot, :],
                                       in_=qsp[h][:, qb, :]).then_inc(s_q[qslot].h, 16)
                    E.add("sync", ld_q)

                    ch_base = gchunk
                    ld_marks = {}

                    def emit_ld_kv(ch, qb=qb, h=h, nkt=nkt, ch_base=ch_base,
                                   ld_marks=ld_marks):
                        nonlocal gchunk
                        k0 = ch * CKT
                        nk = min(CKT, nkt - k0)
                        gc = ch_base + ch
                        cs = gc % 4
                        need = kcons_mark[gc - 4] if gc >= 4 else 0
                        ld_marks[ch] = s_kv[cs].reserve(32)

                        def ld_kv(sync, h=h, k0=k0, nk=nk, cs=cs, need=need):
                            if need > 0:
                                sync.wait_ge(s_av.h, need)
                            sync.dma_start(
                                out=kbuf[:, cs, 0:nk * 128],
                                in_=ksp[h][:, k0:k0 + nk, :].rearrange(
                                    "p a b -> p (a b)")).then_inc(s_kv[cs].h, 16)
                            sync.dma_start(
                                out=vbuf[:, cs, 0:nk * 128],
                                in_=vsp[h][:, k0:k0 + nk, :].rearrange(
                                    "p a b -> p (a b)")).then_inc(s_kv[cs].h, 16)
                        E.add("sync", ld_kv)
                        gchunk += 1

                    for ch in range(nchunks):
                        emit_ld_kv(ch)

                    o_guard = s_o.n - 1
                    pO = pOb[it % 2]

                    def emit_t_s(kt):
                        ch = kt // CKT
                        cs = (ch_base + ch) % 4
                        ko = kt - ch * CKT
                        sslot = kt % 4
                        need_exp = s_exp.n - 1
                        t_sv = s_mm.reserve() if kt % 2 == 1 else s_mm.n

                        def t_s(t, kt=kt, cs=cs, ko=ko, sslot=sslot, qslot=qslot,
                                need_ld=ld_marks[ch], need_exp=need_exp,
                                o_guard=o_guard, q_mark=q_mark):
                            if kt == 0:
                                t.wait_ge(s_q[qslot].h, q_mark)
                                t.wait_ge(s_kv[cs].h, need_ld)
                            elif ko == 0:
                                t.wait_ge(s_kv[cs].h, need_ld)
                            if kt % 2 == 0 and need_exp > 0:
                                t.wait_ge(s_exp.h, need_exp)
                            mmi = t.matmul(pS[:, sslot, :],
                                           kbuf[:, cs, ko * 128:(ko + 1) * 128],
                                           qbuf[:, qslot, :], start=True,
                                           stop=True)
                            if kt % 2 == 1:
                                mmi.then_inc(s_mm.h, 1)
                        E.add("tensor", t_s)
                        return t_sv

                    ts_marks = {0: emit_t_s(0), 1: emit_t_s(1)}
                    zprev = zr_mark[it - 1] if it >= 1 else 0

                    for bp in range(nkt // 2):
                        kt0 = 2 * bp
                        pp = (bp % 2) * 2
                        need_z = s_zp.n - 1
                        t_ev = s_exp.reserve()

                        def s_e2(scl, pp=pp, need_mm=ts_marks[kt0 + 1],
                                 need_z=need_z):
                            scl.wait_ge(s_mm.h, need_mm)
                            if need_z > 0:
                                scl.wait_ge(s_zp.h, need_z)
                            scl.activation(out=pb[:, pp:pp + 2, :],
                                           in_=pS[:, pp:pp + 2, :],
                                           func=AF.Exp, bias=cexp[:, :],
                                           scale=SCL).then_inc(s_exp.h, 1)
                        E.add("scalar", s_e2)

                        ndiag = 0
                        for j in range(2):
                            kt = kt0 + j
                            diag = kt >= 4 * qb
                            if diag:
                                ndiag += 1
                                s_msk.reserve()

                                def v_m(v, qb=qb, kt=kt, pslot=pp + j, need=t_ev):
                                    v.wait_ge(s_exp.h, need)
                                    v.tensor_tensor(pb[:, pslot, :], pb[:, pslot, :],
                                                    msk[:, kt - 4 * qb, :],
                                                    op=ALU.mult).then_inc(s_msk.h, 1)
                                E.add("vector", v_m)

                        if kt0 + 2 < nkt:
                            ts_marks[kt0 + 2] = emit_t_s(kt0 + 2)
                            ts_marks[kt0 + 3] = emit_t_s(kt0 + 3)

                        need_m = s_msk.n
                        for j in range(2):
                            kt = kt0 + j
                            ch = kt // CKT
                            cs = (ch_base + ch) % 4
                            ko = kt - ch * CKT
                            lastk = (ko == CKT - 1 or kt == nkt - 1)
                            if lastk:
                                s_av.reserve()

                            def t_av(t, kt=kt, cs=cs, ko=ko, pslot=pp + j, j=j,
                                     ndiag=ndiag, need_e=t_ev, need_m=need_m,
                                     nkt=nkt, pO=pO, o_guard=o_guard,
                                     lastk=lastk):
                                if kt == 0 and o_guard > 0:
                                    t.wait_ge(s_o.h, o_guard)
                                if j == 0:
                                    if ndiag > 0:
                                        t.wait_ge(s_msk.h, need_m)
                                    else:
                                        t.wait_ge(s_exp.h, need_e)
                                mmi = t.matmul(pO[:, :],
                                               vbuf[:, cs, ko * 128:(ko + 1) * 128],
                                               pb[:, pslot, :], start=(kt == 0),
                                               stop=(kt == nkt - 1))
                                if lastk:
                                    mmi.then_inc(s_av.h, 1)
                            E.add("tensor", t_av)
                            if lastk:
                                kcons_mark.append(s_av.n)

                        s_zp.reserve()
                        for j in range(2):
                            kt = kt0 + j

                            def t_z(t, kt=kt, pslot=pp + j, j=j, zprev=zprev,
                                    nkt=nkt):
                                if kt == 0 and zprev > 0:
                                    t.wait_ge(s_zi.h, zprev)
                                mmi = t.matmul(pZ[:, :], ones128[:, :],
                                               pb[:, pslot, :], start=(kt == 0),
                                               stop=(kt == nkt - 1))
                                if j == 1:
                                    mmi.then_inc(s_zp.h, 1)
                            E.add("tensor", t_z)

                    mm_mark_iter.append(s_mm.n)

                    zm_all = s_zp.n
                    t_ziv = s_zi.reserve()
                    zr_mark.append(s_zi.n)

                    def v_zi(v, zm_all=zm_all):
                        v.wait_ge(s_zp.h, zm_all)
                        v.reciprocal(binv[:, :], pZ[:, :]).then_inc(s_zi.h, 1)
                    E.add("vector", v_zi)

                    oslot = it % 2
                    need_st = s_sto[oslot].n if it >= 2 else 0
                    t_ov = s_o.reserve()

                    def v_o(v, oslot=oslot, t_ziv=t_ziv, need_st=need_st,
                            need_av=s_av.n, pO=pOb[it % 2]):
                        v.wait_ge(s_zi.h, t_ziv)
                        v.wait_ge(s_av.h, need_av)
                        if need_st > 0:
                            v.wait_ge(s_sto[oslot].h, need_st)
                        v.tensor_tensor(oout[:, oslot, :], pO[:, :], binv[:, :],
                                        op=ALU.mult).then_inc(s_o.h, 1)
                    E.add("vector", v_o)

                    s_sto[oslot].reserve(16)

                    def gp_o(g, qb=qb, h=h, oslot=oslot, need=t_ov):
                        g.wait_ge(s_o.h, need)
                        k = qb // 4
                        g.dma_start(
                            out=ccg_in[k][h * 128:(h + 1) * 128,
                                          (qb % 4) * 512:(qb % 4 + 1) * 512],
                            in_=oout[:, oslot, :]).then_inc(s_sto[oslot].h, 16)
                    E.add("gpsimd", gp_o)

                if qb % 4 == 3:
                    s_cc.reserve()

                    def g_cc(g, k=qb // 4, need0=s_sto[0].n, need1=s_sto[1].n):
                        g.wait_ge(s_sto[0].h, need0)
                        g.wait_ge(s_sto[1].h, need1)
                        g.collective_compute(
                            "AllGather", ALU.bypass,
                            replica_groups=[[0, 1, 2, 3], [4, 5, 6, 7]],
                            ins=[ccg_in[k].ap().opt()],
                            outs=[ccg_out[k].ap().opt()]).then_inc(s_cc.h, 1)
                    E.add("gpsimd", g_cc)
            E.flush(blk2)

        with contextlib.ExitStack() as es3:
            sb3 = lambda n_, sh, dt: es3.enter_context(nc.sbuf_tensor(n_, sh, dt))
            ps3 = lambda n_, sh, dt: es3.enter_context(nc.psum_tensor(n_, sh, dt))
            wo_s = sb3("wo_s", [128, NCH, D], BF16)
            aT = sb3("aT", [128, NCH, 512], BF16)
            y1T = sb3("y1T", [128, NCH, 512], BF16)
            y1 = sb3("y1", [128, 4, D], BF16)
            xqb = sb3("xqb", [128, 2, D], F32)
            t3 = sb3("t3", [128, D], F32)
            r1b = sb3("r1b", [128, 2, D], F32)
            pY = ps3("pY", [128, 512], F32)
            pT3 = ps3("pT3", [128, 1024], BF16)

            blk3 = es3.enter_context(nc.Block("p3"))
            E = Emit()

            n_wld3 = s_wld.reserve(16 * NCH)

            def ld_wo(g):
                w_r = woT.rearrange("(c p) o -> c p o", p=128)
                for c in range(NCH):
                    g.dma_start(out=wo_s[:, c, :], in_=w_r[c]).then_inc(s_wld.h, 16)
            E.add("gpsimd", ld_wo)

            def t_gate3(t, need=n_wld3):
                t.wait_ge(s_wld.h, need)
            E.add("tensor", t_gate3)

            xq_r = xq_in.rearrange("(g p) d -> g p d", p=128)
            r1_r = r1sp.rearrange("(g p) d -> g p d", p=128)

            r1_mark = []
            mm3_mark = []
            for tb in range(NBLK):
                E.flush(blk3)
                need_mm = mm3_mark[tb - 1] if tb >= 1 else 0
                lda_tb = s_lda.reserve(16 * NCH)

                def sync_aT(sync, tb=tb, need_mm=need_mm):
                    if need_mm > 0:
                        sync.wait_ge(s_mm3.h, need_mm)
                    pid = sync.partition_id()
                    for core in nc.Switch(engines=[sync], index=[pid], n=8):
                        tq = core % 4
                        sync.wait_ge(s_cc.h, tq + 1)
                        co = ccg_out[tq]
                        for cb in range(NCH):
                            sync.dma_start(
                                out=aT[:, cb, :],
                                in_=co[cb * 128:(cb + 1) * 128,
                                       tb * 512:(tb + 1) * 512]).then_inc(s_lda.h, 16)
                blk3.sync(sync_aT)

                for oc in range(NCH):
                    need_ev = s_yev.n
                    t_mmv = s_mm3.reserve()

                    def t_op(t, oc=oc, need_ld=lda_tb, need_ev=need_ev):
                        if oc == 0:
                            t.wait_ge(s_lda.h, need_ld)
                        if need_ev > 0:
                            t.wait_ge(s_yev.h, need_ev)
                        for c in range(NCH):
                            mmi = t.matmul(pY[:, :], wo_s[:, c, oc * 128:(oc + 1) * 128],
                                           aT[:, c, :], start=(c == 0), stop=(c == NCH - 1))
                        mmi.then_inc(s_mm3.h, 1)
                    E.add("tensor", t_op)

                    s_yev.reserve()

                    def v_ye(v, oc=oc, need=t_mmv):
                        v.wait_ge(s_mm3.h, need)
                        v.tensor_copy(y1T[:, oc, :], pY[:, :]).then_inc(s_yev.h, 1)
                    E.add("vector", v_ye)
                mm3_mark.append(s_mm3.n)

                for rnd in range(8):
                    need_ev = s_yev.n
                    need_pt = s_evT.n
                    t_tpv = s_tp.reserve()

                    def t_tp3(t, rnd=rnd, need_ev=need_ev, need_pt=need_pt):
                        if rnd == 0:
                            t.wait_ge(s_yev.h, need_ev)
                        if need_pt > 0:
                            t.wait_ge(s_evT.h, need_pt)
                        for i in range(8):
                            j = rnd * 8 + i
                            oc, sub = j // 4, j % 4
                            mmi = t.transpose(pT3[:, i * 128:(i + 1) * 128],
                                              y1T[:, oc, sub * 128:(sub + 1) * 128],
                                              ids[:, :])
                        mmi.then_inc(s_tp.h, 1)
                    E.add("tensor", t_tp3)

                    s_evT.reserve()

                    def v_pt3(v, rnd=rnd, need=t_tpv):
                        v.wait_ge(s_tp.h, need)
                        oc0 = rnd * 2
                        v.tensor_copy(
                            y1[:, :, oc0 * 128:(oc0 + 2) * 128].rearrange(
                                "p s (c f) -> p c s f", f=128),
                            pT3[:, :].rearrange("p (c s f) -> p c s f",
                                                c=2, s=4)).then_inc(s_evT.h, 1)
                    E.add("vector", v_pt3)

                for sub in range(4):
                    g = tb * 4 + sub
                    need_r1g = r1_mark[g - 2] if g >= 2 else 0
                    t_xqv = s_xq[g % 2].reserve(16)

                    def ld_xq(sync, g=g, need=need_r1g):
                        if need > 0:
                            sync.wait_ge(s_r1.h, need)
                        sync.dma_start(out=xqb[:, g % 2, :], in_=xq_r[g]).then_inc(
                            s_xq[g % 2].h, 16)
                    E.add("sync", ld_xq)

                    need_t = s_evT.n

                    def v_lngate(v, need_t=need_t):
                        v.wait_ge(s_evT.h, need_t)
                    E.add("vector", v_lngate)
                    tnorm = queue_ln(E, (lambda sub=sub: y1[:, sub, :]),
                                     (lambda: t3[:, :]))

                    need_wr = s_sr[g % 2].n if g >= 2 else 0
                    t_r1v = s_r1.reserve()

                    def v_r1(v, g=g, need_ld=t_xqv, need_wr=need_wr, tnorm=tnorm):
                        v.wait_ge(s_xq[g % 2].h, need_ld)
                        v.wait_ge(s_ln2.h, tnorm)
                        if need_wr > 0:
                            v.wait_ge(s_sr[g % 2].h, need_wr)
                        v.tensor_tensor(r1b[:, g % 2, :], t3[:, :],
                                        xqb[:, g % 2, :], op=ALU.add).then_inc(s_r1.h, 1)
                    E.add("vector", v_r1)
                    r1_mark.append(s_r1.n)

                    s_sr[g % 2].reserve(16)

                    def sc_r1(scl, g=g, need=t_r1v):
                        scl.wait_ge(s_r1.h, need)
                        scl.dma_start(out=r1_r[g], in_=r1b[:, g % 2, :]).then_inc(
                            s_sr[g % 2].h, 16)
                    E.add("scalar", sc_r1)

            def p3_fin(scl, a0=s_sr[0].n, a1=s_sr[1].n):
                scl.wait_ge(s_sr[0].h, a0)
                scl.wait_ge(s_sr[1].h, a1)
            E.add("scalar", p3_fin)
            E.flush(blk3)

        with contextlib.ExitStack() as es4:
            sb4 = lambda n_, sh, dt: es4.enter_context(nc.sbuf_tensor(n_, sh, dt))
            ps4 = lambda n_, sh, dt: es4.enter_context(nc.psum_tensor(n_, sh, dt))
            wb_s = sb4("wb_s", [128, NCH, D], BF16)
            awr_s = sb4("awr_s", [128, NCH, 72], BF16)
            bfl_s = sb4("bfl_s", [64, D], BF16)
            bb_s = sb4("bb_s", [128, 16], F32)
            r1a = sb4("r1a", [128, 2, D], F32)
            r1c = sb4("r1c", [128, 2, D], F32)
            ln2 = sb4("ln2", [128, D], BF16)
            ln2T = sb4("ln2T", [128, NCH, 512], BF16)
            hbuf = sb4("hbuf", [128, 512], BF16)
            hbufL = sb4("hbufL", [8, 512], F32)
            lgt = sb4("lgt", [128, 4, 8], F32)
            negm1 = sb4("negm1", [128, 1], F32)
            m2b = sb4("m2b", [128, 1], F32)
            eqb = sb4("eqb", [128, 8], F32)
            lmk = sb4("lmk", [128, 8], F32)
            pexp = sb4("pexp", [128, 8], F32)
            indb = sb4("indb", [128, 8], F32)
            wgt = sb4("wgt", [128, 8], F32)
            zg = sb4("zg", [128, 1], F32)
            zgi = sb4("zgi", [128, 1], F32)
            wn = sb4("wn", [128, 8], F32)
            wnT = sb4("wnT", [8, 128], BF16)
            wexpT = sb4("wexpT", [64, 512], BF16)
            h2 = sb4("h2", [64, 512], BF16)
            ffnT = sb4("ffnT", [128, NCH, 512], BF16)
            ffn = sb4("ffn", [128, 4, D], BF16)
            t4 = sb4("t4", [128, D], F32)
            r2b = sb4("r2b", [128, 2, D], F32)
            pY4 = ps4("pY4", [128, 512], F32)
            pT4 = ps4("pT4", [128, 1024], BF16)
            pH = ps4("pH", [128, 512], F32)
            pG = ps4("pG", [128, 512], F32)

            blk4 = es4.enter_context(nc.Block("p4"))
            E = Emit()

            n_wld4 = s_wld.reserve(16 * (2 * NCH + 1))

            def ld_w4(g):
                w_r = wbT.rearrange("(c p) o -> c p o", p=128)
                for c in range(NCH):
                    g.dma_start(out=wb_s[:, c, :], in_=w_r[c]).then_inc(s_wld.h, 16)
                a_r = awrT.rearrange("(c p) o -> c p o", p=128)
                for c in range(NCH):
                    g.dma_start(out=awr_s[:, c, :], in_=a_r[c]).then_inc(s_wld.h, 16)
                g.dma_start(out=bfl_s[:, :], in_=bflT[:, :]).then_inc(s_wld.h, 16)
            E.add("gpsimd", ld_w4)

            n_ld4 = s_ldx.reserve(16)

            def ld_bb(sync):
                sync.dma_start(out=bb_s[:, :], in_=bbase[:, :]).then_inc(s_ldx.h, 16)
            E.add("sync", ld_bb)

            def gate4(e, a=n_wld4, b_=n_ld4):
                e.wait_ge(s_wld.h, a)
                e.wait_ge(s_ldx.h, b_)
            E.add("tensor", gate4)
            E.add("vector", gate4)
            E.add("scalar", gate4)

            out_r = out_q.rearrange("(g p) d -> g p d", p=128)

            ln3_mark = []
            r2_mark = []
            for tb in range(NBLK):
                for sub in range(4):
                    g = tb * 4 + sub
                    need_lg = ln3_mark[g - 2] if g >= 2 else 0
                    t_rav = s_ra[g % 2].reserve(16)

                    def ld_r1(sync, g=g, need=need_lg):
                        if need > 0:
                            sync.wait_ge(s_ln2.h, need)
                        sync.dma_start(out=r1a[:, g % 2, :], in_=r1_r[g]).then_inc(
                            s_ra[g % 2].h, 16)
                    E.add("sync", ld_r1)

                    need_tp = s_tp.n

                    def v_lngate4(v, g=g, t_rav=t_rav, need_tp=need_tp):
                        v.wait_ge(s_ra[g % 2].h, t_rav)
                        if need_tp > 0:
                            v.wait_ge(s_tp.h, need_tp)
                    E.add("vector", v_lngate4)
                    queue_ln(E, (lambda g=g: r1a[:, g % 2, :]), (lambda: ln2[:, :]))
                    ln3_mark.append(s_ln2.n)

                    for rnd in range(2):
                        need_ln = s_ln2.n
                        need_ev = s_evT.n
                        t_tpv = s_tp.reserve()

                        def t_tp4(t, rnd=rnd, need_ln=need_ln, need_ev=need_ev):
                            if rnd == 0:
                                t.wait_ge(s_ln2.h, need_ln)
                            if need_ev > 0:
                                t.wait_ge(s_evT.h, need_ev)
                            for i in range(8):
                                c = rnd * 8 + i
                                mmi = t.transpose(pT4[:, i * 128:(i + 1) * 128],
                                                  ln2[:, c * 128:(c + 1) * 128], ids[:, :])
                            mmi.then_inc(s_tp.h, 1)
                        E.add("tensor", t_tp4)

                        s_evT.reserve()

                        def v_ev4(v, sub=sub, rnd=rnd, need=t_tpv):
                            v.wait_ge(s_tp.h, need)
                            v.tensor_copy(
                                ln2T[:, rnd * 8:(rnd + 1) * 8, sub * 128:(sub + 1) * 128],
                                pT4[:, :].rearrange("p (c f) -> p c f", f=128)).then_inc(
                                s_evT.h, 1)
                        E.add("vector", v_ev4)

                need_g = s_g.n
                t_awrv = s_mm3.reserve()

                def t_awr(t, need=s_evT.n, need_g=need_g):
                    t.wait_ge(s_evT.h, need)
                    if need_g > 0:
                        t.wait_ge(s_g.h, need_g)
                    for c in range(NCH):
                        mmi = t.matmul(pH[0:72, :], awr_s[:, c, :], ln2T[:, c, :],
                                       start=(c == 0), stop=(c == NCH - 1))
                    mmi.then_inc(s_mm3.h, 1)
                E.add("tensor", t_awr)

                va = s_v.reserve()
                t_gv = s_g.reserve()

                def v_hev(v, need=t_awrv, va=va):
                    v.wait_ge(s_mm3.h, need)
                    v.tensor_copy(hbuf[0:64, :], pH[0:64, :]).then_inc(s_v.h, 1)
                    v.wait_ge(s_v.h, va)
                    v.tensor_copy(hbufL[:, :], pH[64:72, :]).then_inc(s_g.h, 1)
                E.add("vector", v_hev)

                need_cons = s_lda.n
                for sub in range(4):
                    t_lgv = s_tp.reserve()

                    def t_lg(t, sub=sub, need=t_gv, need_cons=need_cons):
                        if sub == 0:
                            t.wait_ge(s_g.h, need)
                            if need_cons > 0:
                                t.wait_ge(s_lda.h, need_cons)
                        t.transpose(pG[:, sub * 8:(sub + 1) * 8],
                                    hbufL[:, sub * 128:(sub + 1) * 128],
                                    idsf[0:8, 0:8]).then_inc(s_tp.h, 1)
                    E.add("tensor", t_lg)

                lg_need = s_tp.n
                t_lgc = s_lda.reserve()

                def v_lg(v, need=lg_need):
                    v.wait_ge(s_tp.h, need)
                    v.tensor_copy(
                        lgt[:, :, :], pG[:, 0:32].rearrange(
                            "p (s e) -> p s e", e=8)).then_inc(s_lda.h, 1)
                E.add("vector", v_lg)

                for sub in range(4):
                    va = s_v.reserve(4)
                    t_ind = s_msk.reserve()

                    def v_g1(v, sub=sub, need=t_lgc, va=va):
                        if sub == 0:
                            v.wait_ge(s_lda.h, need)
                        l = lgt[:, sub, :]
                        v.reduce_max(negm1[:, :], l, axis=AX.X,
                                     negate=True).then_inc(s_v.h, 1)
                        v.wait_ge(s_v.h, va - 3)
                        v.tensor_scalar(out=eqb[:, :], in0=l, scalar1=negm1[:, :],
                                        scalar2=0.0, op0=ALU.add,
                                        op1=ALU.is_ge).then_inc(s_v.h, 1)
                        v.wait_ge(s_v.h, va - 2)
                        v.scalar_tensor_tensor(out=lmk[:, :], in0=eqb[:, :],
                                               scalar=-30000.0, in1=l,
                                               op0=ALU.mult, op1=ALU.add).then_inc(s_v.h, 1)
                        v.wait_ge(s_v.h, va - 1)
                        v.reduce_max(m2b[:, :], lmk[:, :], axis=AX.X).then_inc(s_v.h, 1)
                        v.wait_ge(s_v.h, va)
                        v.tensor_scalar(out=indb[:, :], in0=l, scalar1=m2b[:, :],
                                        scalar2=None, op0=ALU.is_ge).then_inc(s_msk.h, 1)
                    E.add("vector", v_g1)

                    t_px = s_rs.reserve()

                    def s_px(scl, sub=sub, need=t_ind):
                        scl.wait_ge(s_msk.h, need)
                        scl.activation(out=pexp[:, :], in_=lgt[:, sub, :],
                                       func=AF.Exp, bias=negm1[:, :],
                                       scale=1.0).then_inc(s_rs.h, 1)
                    E.add("scalar", s_px)

                    va = s_v.reserve(3)
                    t_wnv = s_ln2.reserve()

                    def v_g2(v, need=t_px, va=va):
                        v.wait_ge(s_rs.h, need)
                        v.tensor_tensor(wgt[:, :], indb[:, :], pexp[:, :],
                                        op=ALU.mult).then_inc(s_v.h, 1)
                        v.wait_ge(s_v.h, va - 2)
                        v.reduce_sum(zg[:, :], wgt[:, :], axis=AX.X).then_inc(s_v.h, 1)
                        v.wait_ge(s_v.h, va - 1)
                        v.reciprocal(zgi[:, :], zg[:, :]).then_inc(s_v.h, 1)
                        v.wait_ge(s_v.h, va)
                        v.tensor_scalar(out=wn[:, :], in0=wgt[:, :],
                                        scalar1=zgi[:, :], scalar2=LORA_SCALE,
                                        op0=ALU.mult, op1=ALU.mult).then_inc(s_ln2.h, 1)
                    E.add("vector", v_g2)

                    need_ev = s_yev.n
                    t_wtv = s_tp.reserve()

                    def t_wnT(t, need=t_wnv, need_ev=need_ev):
                        t.wait_ge(s_ln2.h, need)
                        if need_ev > 0:
                            t.wait_ge(s_yev.h, need_ev)
                        t.transpose(pG[0:8, 128:256], wn[:, :],
                                    idsf[:, :]).then_inc(s_tp.h, 1)
                    E.add("tensor", t_wnT)

                    t_wcv = s_r1.reserve()

                    def v_wnT(v, need=t_wtv):
                        v.wait_ge(s_tp.h, need)
                        v.tensor_copy(wnT[:, :], pG[0:8, 128:256]).then_inc(s_r1.h, 1)
                    E.add("vector", v_wnT)

                    t_e8v = s_bc.reserve()

                    def t_e8(t, need=t_wcv):
                        t.wait_ge(s_r1.h, need)
                        t.matmul(pG[0:64, 256:384], e8s[:, :], wnT[:, :],
                                 start=True, stop=True).then_inc(s_bc.h, 1)
                    E.add("tensor", t_e8)

                    s_yev.reserve()

                    def v_we(v, sub=sub, need=t_e8v):
                        v.wait_ge(s_bc.h, need)
                        v.tensor_copy(wexpT[:, sub * 128:(sub + 1) * 128],
                                      pG[0:64, 256:384]).then_inc(s_yev.h, 1)
                    E.add("vector", v_we)

                t_h2v = s_g.reserve()

                def v_h2(v, need=s_yev.n):
                    v.wait_ge(s_yev.h, need)
                    v.tensor_tensor(h2[:, :], hbuf[0:64, :], wexpT[:, :],
                                    op=ALU.mult).then_inc(s_g.h, 1)
                E.add("vector", v_h2)

                for oc in range(NCH):
                    need_ev = s_exp.n
                    t_fv = s_mm.reserve()

                    def t_ffn(t, oc=oc, need_h2=t_h2v, need_ev=need_ev):
                        if oc == 0:
                            t.wait_ge(s_g.h, need_h2)
                        if need_ev > 0:
                            t.wait_ge(s_exp.h, need_ev)
                        for c in range(NCH):
                            t.matmul(pY4[:, :], wb_s[:, c, oc * 128:(oc + 1) * 128],
                                     ln2T[:, c, :], start=(c == 0), stop=False)
                        t.matmul(pY4[:, :], bfl_s[:, oc * 128:(oc + 1) * 128],
                                 h2[:, :], start=False, stop=True).then_inc(s_mm.h, 1)
                    E.add("tensor", t_ffn)

                    s_exp.reserve()

                    def s_fev(scl, oc=oc, need=t_fv):
                        scl.wait_ge(s_mm.h, need)
                        scl.activation(out=ffnT[:, oc, :], in_=pY4[:, :],
                                       func=AF.Identity, bias=bb_s[:, oc:oc + 1],
                                       scale=1.0).then_inc(s_exp.h, 1)
                    E.add("scalar", s_fev)

                for rnd in range(8):
                    need_ev = s_exp.n
                    need_pt = s_evT.n
                    t_tpv = s_tp.reserve()

                    def t_tpf(t, rnd=rnd, need_ev=need_ev, need_pt=need_pt):
                        if rnd == 0:
                            t.wait_ge(s_exp.h, need_ev)
                        if need_pt > 0:
                            t.wait_ge(s_evT.h, need_pt)
                        for i in range(8):
                            j = rnd * 8 + i
                            oc, sub = j // 4, j % 4
                            mmi = t.transpose(pT4[:, i * 128:(i + 1) * 128],
                                              ffnT[:, oc, sub * 128:(sub + 1) * 128],
                                              ids[:, :])
                        mmi.then_inc(s_tp.h, 1)
                    E.add("tensor", t_tpf)

                    s_evT.reserve()

                    def v_ptf(v, rnd=rnd, need=t_tpv):
                        v.wait_ge(s_tp.h, need)
                        oc0 = rnd * 2
                        v.tensor_copy(
                            ffn[:, :, oc0 * 128:(oc0 + 2) * 128].rearrange(
                                "p s (c f) -> p c s f", f=128),
                            pT4[:, :].rearrange("p (c s f) -> p c s f",
                                                c=2, s=4)).then_inc(s_evT.h, 1)
                    E.add("vector", v_ptf)

                for sub in range(4):
                    g = tb * 4 + sub
                    need_r2g = r2_mark[g - 2] if g >= 2 else 0
                    t_rcv = s_rc[g % 2].reserve(16)

                    def ld_r1b(sync, g=g, need=need_r2g):
                        if need > 0:
                            sync.wait_ge(s_o.h, need)
                        sync.dma_start(out=r1c[:, g % 2, :], in_=r1_r[g]).then_inc(
                            s_rc[g % 2].h, 16)
                    E.add("sync", ld_r1b)

                    need_t = s_evT.n

                    def v_lngate5(v, need_t=need_t):
                        v.wait_ge(s_evT.h, need_t)
                    E.add("vector", v_lngate5)
                    tnorm = queue_ln(E, (lambda sub=sub: ffn[:, sub, :]),
                                     (lambda: t4[:, :]))

                    need_st = s_so[g % 2].n if g >= 2 else 0
                    t_r2v = s_o.reserve()

                    def v_r2(v, g=g, need_ld=t_rcv, need_st=need_st, tnorm=tnorm):
                        v.wait_ge(s_rc[g % 2].h, need_ld)
                        v.wait_ge(s_ln2.h, tnorm)
                        if need_st > 0:
                            v.wait_ge(s_so[g % 2].h, need_st)
                        v.tensor_tensor(r2b[:, g % 2, :], t4[:, :],
                                        r1c[:, g % 2, :], op=ALU.add).then_inc(s_o.h, 1)
                    E.add("vector", v_r2)
                    r2_mark.append(s_o.n)

                    s_so[g % 2].reserve(16)

                    def sc_out(scl, g=g, need=t_r2v):
                        scl.wait_ge(s_o.h, need)
                        scl.dma_start(out=out_r[g], in_=r2b[:, g % 2, :]).then_inc(
                            s_so[g % 2].h, 16)
                    E.add("scalar", sc_out)

            def p4_fin(scl, a0=s_so[0].n, a1=s_so[1].n):
                scl.wait_ge(s_so[0].h, a0)
                scl.wait_ge(s_so[1].h, a1)
            E.add("scalar", p4_fin)
            E.flush(blk4)

    return nc


def _host_tables():
    half = DK // 2
    inv_freq = 10000.0 ** (-2.0 * np.arange(half) / DK)
    ang = np.arange(S)[:, None] * inv_freq[None, :]
    cosT = np.cos(ang).T.astype(np.float32)
    sinT = np.sin(ang).T.astype(np.float32)
    cc = np.concatenate([cosT, cosT], axis=0).astype(ml_dtypes.bfloat16)
    ss2 = np.concatenate([-sinT, sinT], axis=0).astype(ml_dtypes.bfloat16)
    masks = np.zeros((4 * 128, 512), dtype=ml_dtypes.bfloat16)
    for i in range(4):
        for kk in range(128):
            masks[i * 128 + kk, i * 128 + kk:] = 1.0
    ident = np.eye(128, dtype=ml_dtypes.bfloat16)
    pswap = np.zeros((128, 128), dtype=ml_dtypes.bfloat16)
    for i in range(64):
        pswap[i + 64, i] = 1.0
        pswap[i, i + 64] = 1.0
    e8 = np.zeros((8, 64), dtype=ml_dtypes.bfloat16)
    for e in range(8):
        e8[e, e * 8:(e + 1) * 8] = 1.0
    return cc, ss2, masks, ident, pswap, e8


def _perm_rows(w):
    perm = []
    for h in range(H):
        base = h * DK
        perm += [base + 2 * i for i in range(DK // 2)]
        perm += [base + 2 * i + 1 for i in range(DK // 2)]
    return w[perm, :]


_NC_CACHE = {}


def kernel(L0_attn_x, L0_attn_wq, L0_attn_wk, L0_attn_wv, L0_attn_wo,
           L0_ffn_W_base, L0_ffn_b_base, L0_ffn_W_router, L0_ffn_A_w, L0_ffn_B_w):
    x = np.ascontiguousarray(np.asarray(L0_attn_x, dtype=np.float32))
    wq = np.asarray(L0_attn_wq, dtype=np.float32)
    wk = np.asarray(L0_attn_wk, dtype=np.float32)
    wv = np.asarray(L0_attn_wv, dtype=np.float32)
    wo = np.asarray(L0_attn_wo, dtype=np.float32)
    wb = np.asarray(L0_ffn_W_base, dtype=np.float32)
    bb = np.asarray(L0_ffn_b_base, dtype=np.float32)
    wr = np.asarray(L0_ffn_W_router, dtype=np.float32)
    aw = np.asarray(L0_ffn_A_w, dtype=np.float32)
    bw = np.asarray(L0_ffn_B_w, dtype=np.float32)

    cc, ss2, masks, ident, pswap, e8 = _host_tables()

    wqTf = np.ascontiguousarray(_perm_rows(wq).T)
    wkTf = np.ascontiguousarray(_perm_rows(wk).T)
    wvTf = np.ascontiguousarray(wv.T)
    woTf = np.ascontiguousarray(wo.T)
    wbTf = np.ascontiguousarray(wb.T)
    aflT = aw.reshape(64, D).T
    wrTf = wr.T
    awrTf = np.ascontiguousarray(np.concatenate([aflT, wrTf], axis=1))
    bflTf = np.ascontiguousarray(bw.transpose(0, 2, 1).reshape(64, D))
    bbasef = np.ascontiguousarray(bb.reshape(16, 128).T)

    if "nc" not in _NC_CACHE:
        _NC_CACHE["nc"] = build_program()
    nc = _NC_CACHE["nc"]

    in_maps = []
    for c in range(8):
        b = c // 4
        hg = c % 4
        tq = c % 4
        in_maps.append({
            "x": x[b],
            "xq": np.ascontiguousarray(x[b, tq * QT:(tq + 1) * QT, :]),
            "wqT": np.ascontiguousarray(wqTf[:, hg * 512:(hg + 1) * 512]),
            "wkT": np.ascontiguousarray(wkTf[:, hg * 512:(hg + 1) * 512]),
            "wvT": np.ascontiguousarray(wvTf[:, hg * 512:(hg + 1) * 512]),
            "woT": woTf, "wbT": wbTf, "awrT": awrTf, "bflT": bflTf, "bbase": bbasef,
            "ident": ident, "identf": np.eye(128, dtype=np.float32),
            "pswap": pswap, "e8": e8,
            "cctab": cc, "ss2tab": ss2, "masks": masks,
        })
    res = run_bass_kernel_spmd(nc, in_maps, core_ids=list(range(8)))
    out = np.empty((B, S, D), dtype=np.float32)
    for c in range(8):
        b = c // 4
        tq = c % 4
        out[b, tq * QT:(tq + 1) * QT, :] = res.results[c]["out_q"]
    return out


# version 11
# speedup vs baseline: 1.5756x; 1.0637x over previous
import contextlib
import numpy as np
import ml_dtypes
import concourse.bass as bass
import concourse.mybir as mybir
from concourse.bass_utils import run_bass_kernel_spmd

F32 = mybir.dt.float32
I32 = mybir.dt.int32
BF16 = mybir.dt.bfloat16
AF = mybir.ActivationFunctionType
ALU = mybir.AluOpType
AX = mybir.AxisListType

B = 2
S = 8192
D = 2048
H = 16
HL = 4
DK = 128
NCH = D // 128
SBK = 512
NSB = S // SBK
QT = 2048
NBLK = 4
NKT = S // 128
CKT = 16
CEXP = 16.0
EPS = 1e-5
SCL = 1.0 / np.sqrt(DK)
LORA_SCALE = 2.0
RSQRT_MAGIC = 1597463007.0


class Sem:
    def __init__(self, nc, es, name):
        self.h = es.enter_context(nc.semaphore(name))
        self.n = 0

    def reserve(self, k=1):
        self.n += k
        return self.n


class Emit:
    ENGS = ("sync", "gpsimd", "tensor", "scalar", "vector")

    def __init__(self):
        self.q = {e: [] for e in self.ENGS}

    def add(self, eng, fn):
        self.q[eng].append(fn)

    def flush(self, block):
        for e in self.ENGS:
            fns = self.q[e]
            self.q[e] = []
            if fns:
                def body(hdl, fns=fns):
                    for f in fns:
                        f(hdl)
                getattr(block, e)(body)


def build_program():
    nc = bass.Bass("TRN2", target_bir_lowering=False, debug=False)

    x_in = nc.dram_tensor("x", [S, D], F32, kind="ExternalInput")
    xq_in = nc.dram_tensor("xq", [QT, D], F32, kind="ExternalInput")
    wqT = nc.dram_tensor("wqT", [D, HL * DK], F32, kind="ExternalInput")
    wkT = nc.dram_tensor("wkT", [D, HL * DK], F32, kind="ExternalInput")
    wvT = nc.dram_tensor("wvT", [D, HL * DK], F32, kind="ExternalInput")
    woT = nc.dram_tensor("woT", [D, D], F32, kind="ExternalInput")
    wbT = nc.dram_tensor("wbT", [D, D], F32, kind="ExternalInput")
    awrT = nc.dram_tensor("awrT", [D, 72], F32, kind="ExternalInput")
    bflT = nc.dram_tensor("bflT", [64, D], F32, kind="ExternalInput")
    bbase = nc.dram_tensor("bbase", [128, 16], F32, kind="ExternalInput")
    ident_in = nc.dram_tensor("ident", [128, 128], BF16, kind="ExternalInput")
    identf_in = nc.dram_tensor("identf", [128, 128], F32, kind="ExternalInput")
    pswap_in = nc.dram_tensor("pswap", [128, 128], BF16, kind="ExternalInput")
    e8_in = nc.dram_tensor("e8", [8, 64], BF16, kind="ExternalInput")
    cc_tab = nc.dram_tensor("cctab", [128, S], BF16, kind="ExternalInput")
    ss2_tab = nc.dram_tensor("ss2tab", [128, S], BF16, kind="ExternalInput")
    masks_in = nc.dram_tensor("masks", [4 * 128, 512], BF16, kind="ExternalInput")

    out_q = nc.dram_tensor("out_q", [QT, D], F32, kind="ExternalOutput")

    qsp = [nc.dram_tensor(f"qsp{h}", [128, NSB, SBK], BF16) for h in range(HL)]
    ksp = [nc.dram_tensor(f"ksp{h}", [128, NKT, 128], BF16) for h in range(HL)]
    vsp = [nc.dram_tensor(f"vsp{h}", [128, NKT, 128], BF16) for h in range(HL)]
    ccg_in = [nc.dram_tensor(f"ccin{k}", [HL * DK, QT], BF16) for k in range(4)]
    ccg_out = [nc.dram_tensor(f"ccout{k}", [4 * HL * DK, QT], BF16) for k in range(4)]
    r1sp = nc.dram_tensor("r1sp", [QT, D], F32)

    es = contextlib.ExitStack()
    with es:
        sb = lambda name, shape, dt: es.enter_context(nc.sbuf_tensor(name, shape, dt))

        ids = sb("ids", [128, 128], BF16)
        idsf = sb("idsf", [128, 128], F32)
        psw = sb("psw", [128, 128], BF16)
        e8s = sb("e8s", [8, 64], BF16)
        ones128 = sb("ones128", [128, 128], BF16)
        onecol = sb("onecol", [128, 1], BF16)
        onecf = sb("onecf", [128, 1], F32)
        onerow = sb("onerow", [1, 128], F32)
        cexp = sb("cexp", [128, 1], F32)
        bnst = sb("bnst", [128, 24], F32)
        mv = sb("mv", [128, 2], F32)
        rstd = sb("rstd", [128, 1], F32)
        rs_t1 = sb("rs_t1", [128, 1], F32)
        rs_t2 = sb("rs_t2", [128, 1], F32)

        s_v = Sem(nc, es, "s_v")
        s_cst = Sem(nc, es, "s_cst")
        s_ldx = Sem(nc, es, "s_ldx")
        s_wld = Sem(nc, es, "s_wld")
        s_x = [Sem(nc, es, f"s_x{i}") for i in range(4)]
        s_spq = [Sem(nc, es, f"s_spq{i}") for i in range(2)]
        s_spv = [Sem(nc, es, f"s_spv{i}") for i in range(2)]
        s_q = [Sem(nc, es, f"s_q{i}") for i in range(2)]
        s_kv = [Sem(nc, es, f"s_kv{i}") for i in range(4)]
        s_sto = [Sem(nc, es, f"s_sto{i}") for i in range(2)]
        s_xq = [Sem(nc, es, f"s_xq{i}") for i in range(2)]
        s_sr = [Sem(nc, es, f"s_sr{i}") for i in range(2)]
        s_ra = [Sem(nc, es, f"s_ra{i}") for i in range(2)]
        s_rc = [Sem(nc, es, f"s_rc{i}") for i in range(2)]
        s_so = [Sem(nc, es, f"s_so{i}") for i in range(2)]
        s_ln = Sem(nc, es, "s_ln")
        s_ln2 = Sem(nc, es, "s_ln2")
        s_tp = Sem(nc, es, "s_tp")
        s_evT = Sem(nc, es, "s_evT")
        s_proj = Sem(nc, es, "s_proj")
        s_pe = Sem(nc, es, "s_pe")
        s_sw = Sem(nc, es, "s_sw")
        s_rope = Sem(nc, es, "s_rope")
        s_vev = Sem(nc, es, "s_vev")
        s_mm = Sem(nc, es, "s_mm")
        s_exp = Sem(nc, es, "s_exp")
        s_msk = Sem(nc, es, "s_msk")
        s_av = Sem(nc, es, "s_av")
        s_zi = Sem(nc, es, "s_zi")
        s_bc = Sem(nc, es, "s_bc")
        s_o = Sem(nc, es, "s_o")
        s_cc = Sem(nc, es, "s_cc")
        s_lda = Sem(nc, es, "s_lda")
        s_mm3 = Sem(nc, es, "s_mm3")
        s_yev = Sem(nc, es, "s_yev")
        s_r1 = Sem(nc, es, "s_r1")
        s_g = Sem(nc, es, "s_g")
        s_rs = Sem(nc, es, "s_rs")
        s_zd = Sem(nc, es, "s_zd")
        s_zp = Sem(nc, es, "s_zp")
        s_zc = Sem(nc, es, "s_zc")
        s_rp1 = Sem(nc, es, "s_rp1")
        s_pl = Sem(nc, es, "s_pl")

        def queue_rsqrt(E):
            r = [s_v.reserve() for _ in range(9)]
            dn = s_rs.reserve()

            def body(v, r=r, dn=dn):
                v.tensor_scalar(out=rs_t1[:, :], in0=mv[:, 1:2], scalar1=EPS,
                                scalar2=None, op0=ALU.add).then_inc(s_v.h, 1)
                v.wait_ge(s_v.h, r[0])
                v.tensor_scalar(out=rs_t2[:, :], in0=rs_t1[:, :], scalar1=-0.5,
                                scalar2=None, op0=ALU.mult).then_inc(s_v.h, 1)
                v.wait_ge(s_v.h, r[1])
                v.tensor_scalar(out=rstd[:, :].bitcast(I32),
                                in0=rs_t1[:, :].bitcast(I32), scalar1=1,
                                scalar2=None,
                                op0=ALU.logical_shift_right).then_inc(s_v.h, 1)
                v.wait_ge(s_v.h, r[2])
                v.tensor_scalar(out=rstd[:, :].bitcast(I32),
                                in0=rstd[:, :].bitcast(I32), scalar1=-1.0,
                                scalar2=RSQRT_MAGIC, op0=ALU.mult,
                                op1=ALU.add).then_inc(s_v.h, 1)
                v.wait_ge(s_v.h, r[3])
                k = 4
                for it in range(2):
                    v.tensor_tensor(rs_t1[:, :], rstd[:, :], rstd[:, :],
                                    op=ALU.mult).then_inc(s_v.h, 1)
                    v.wait_ge(s_v.h, r[k]); k += 1
                    v.tensor_scalar(out=rs_t1[:, :], in0=rs_t1[:, :],
                                    scalar1=rs_t2[:, :], scalar2=1.5,
                                    op0=ALU.mult, op1=ALU.add).then_inc(s_v.h, 1)
                    v.wait_ge(s_v.h, r[k]); k += 1
                    inst = v.tensor_tensor(rstd[:, :], rstd[:, :], rs_t1[:, :],
                                           op=ALU.mult)
                    if it == 1:
                        inst.then_inc(s_rs.h, 1)
                    else:
                        inst.then_inc(s_v.h, 1)
                        v.wait_ge(s_v.h, r[k]); k += 1
            E.add("vector", body)
            return dn

        def queue_ln(E, src_fn, out_fn):
            t_stats = s_v.reserve()
            t_aggr = s_ln.reserve()

            def v_stats(v, src_fn=src_fn, t_stats=t_stats, t_aggr=t_aggr):
                src = src_fn()
                for i in range(3):
                    v.bn_stats(bnst[:, i * 6:(i + 1) * 6], src[:, i * 512:(i + 1) * 512])
                v.bn_stats(bnst[:, 18:24], src[:, 1536:2048]).then_inc(s_v.h, 1)
                v.wait_ge(s_v.h, t_stats)
                v.bn_aggr(mv[:, :], bnst[:, :]).then_inc(s_ln.h, 1)
                v.wait_ge(s_ln.h, t_aggr)
            E.add("vector", v_stats)
            dn = queue_rsqrt(E)
            t_norm = s_ln2.reserve()

            def v_norm(v, src_fn=src_fn, out_fn=out_fn, dn=dn):
                v.wait_ge(s_rs.h, dn)
                v.tensor_scalar(out=out_fn(), in0=src_fn(),
                                scalar1=mv[:, 0:1], scalar2=rstd[:, :],
                                op0=ALU.subtract, op1=ALU.mult).then_inc(s_ln2.h, 1)
            E.add("vector", v_norm)
            return t_norm

        with contextlib.ExitStack() as es1:
            sb1 = lambda n_, sh, dt: es1.enter_context(nc.sbuf_tensor(n_, sh, dt))
            ps1 = lambda n_, sh, dt: es1.enter_context(nc.psum_tensor(n_, sh, dt))
            xs = sb1("xs", [128, 4, D], F32)
            lnb = sb1("lnb", [128, 2, D], BF16)
            mv4 = sb1("mv4", [128, 4, 2], F32)
            rstd4 = sb1("rstd4", [128, 4], F32)
            rs4a = sb1("rs4a", [128, 4], F32)
            rs4b = sb1("rs4b", [128, 4], F32)
            lnT = sb1("lnT", [128, NCH, SBK], BF16)
            wq_s = sb1("wq_s", [128, NCH, HL * DK], BF16)
            wk_s = sb1("wk_s", [128, NCH, HL * DK], BF16)
            wv_s = sb1("wv_s", [128, NCH, HL * DK], BF16)
            ccs = sb1("ccs", [128, S], BF16)
            ss2s = sb1("ss2s", [128, S], BF16)
            pre = sb1("pre", [128, 2, SBK], BF16)
            swp = sb1("swp", [128, 2, SBK], BF16)
            tmpA = sb1("tmpA", [128, 2, SBK], BF16)
            ropo = sb1("ropo", [128, 2, SBK], BF16)
            vstg = sb1("vstg", [128, 2, HL * DK], BF16)
            pTa = ps1("pTa", [128, 1024], BF16)
            pTb = ps1("pTb", [128, 1024], BF16)
            pTd = [pTa, pTb]
            pPJ0 = ps1("pPJ0", [128, 512], F32)
            pPJ1 = ps1("pPJ1", [128, 512], F32)
            pSw0 = ps1("pSw0", [128, 512], F32)
            pSw1 = ps1("pSw1", [128, 512], F32)
            pPJ = [pPJ0, pPJ1]
            pSw = [pSw0, pSw1]

            blk1 = es1.enter_context(nc.Block("p1"))
            E = Emit()

            n_ldi1 = s_ldx.reserve(16 * 6)

            def ld_init(sync):
                sync.dma_start(out=ids[:, :], in_=ident_in[:, :]).then_inc(s_ldx.h, 16)
                sync.dma_start(out=idsf[:, :], in_=identf_in[:, :]).then_inc(s_ldx.h, 16)
                sync.dma_start(out=psw[:, :], in_=pswap_in[:, :]).then_inc(s_ldx.h, 16)
                sync.dma_start(out=e8s[:, :], in_=e8_in[:, :]).then_inc(s_ldx.h, 16)
                sync.dma_start(out=ccs[:, :], in_=cc_tab[:, :]).then_inc(s_ldx.h, 16)
                sync.dma_start(out=ss2s[:, :], in_=ss2_tab[:, :]).then_inc(s_ldx.h, 16)
            E.add("sync", ld_init)

            n_wld1 = s_wld.reserve(16 * 3 * NCH)

            def ld_w(g):
                for w_d, w_s in ((wqT, wq_s), (wkT, wk_s), (wvT, wv_s)):
                    w_r = w_d.rearrange("(c p) o -> c p o", p=128)
                    for c in range(NCH):
                        g.dma_start(out=w_s[:, c, :], in_=w_r[c]).then_inc(s_wld.h, 16)
            E.add("gpsimd", ld_w)

            n_cst = s_cst.reserve()

            def consts(v):
                v.memset(onecol[:, :], 1.0)
                v.memset(onecf[:, :], 1.0)
                v.memset(onerow[:, :], 1.0)
                v.memset(ones128[:, :], 1.0)
                v.memset(cexp[:, :], -CEXP).then_inc(s_cst.h, 1)
                v.wait_ge(s_ldx.h, n_ldi1)
                v.wait_ge(s_wld.h, n_wld1)
            E.add("vector", consts)

            def eng_gate1(e, a=n_cst, b_=n_ldi1, c_=n_wld1):
                e.wait_ge(s_cst.h, a)
                e.wait_ge(s_ldx.h, b_)
                e.wait_ge(s_wld.h, c_)
            E.add("scalar", eng_gate1)
            E.add("tensor", eng_gate1)

            x_r = x_in.rearrange("(g p) d -> g p d", p=128)
            ln2_mark = []
            tp_mark = []
            for s_i in range(NSB):
                for tt in range(4):
                    g = 4 * s_i + tt
                    t_x = s_x[tt].reserve(16)
                    need_gx = ln2_mark[4 * (s_i - 1) + tt] if s_i >= 1 else 0

                    def ld_x(sync, g=g, tt=tt, need=need_gx):
                        if need > 0:
                            sync.wait_ge(s_ln2.h, need)
                        sync.dma_start(out=xs[:, tt, :], in_=x_r[g]).then_inc(
                            s_x[tt].h, 16)
                    E.add("sync", ld_x)

                    va = s_v.reserve()
                    t_ag = s_ln.reserve()

                    def v_st(v, tt=tt, t_x=t_x, va=va, t_ag=t_ag):
                        v.wait_ge(s_x[tt].h, t_x)
                        v.wait_ge(s_ln.h, t_ag - 1)
                        for i in range(3):
                            v.bn_stats(bnst[:, i * 6:(i + 1) * 6],
                                       xs[:, tt, i * 512:(i + 1) * 512])
                        v.bn_stats(bnst[:, 18:24], xs[:, tt, 1536:2048]).then_inc(s_v.h, 1)
                        v.wait_ge(s_v.h, va)
                        v.bn_aggr(mv4[:, tt, :], bnst[:, :]).then_inc(s_ln.h, 1)
                    E.add("vector", v_st)

                agg_all = s_ln.n
                r4 = [s_v.reserve() for _ in range(11)]
                t_rs4 = s_rs.reserve()

                def v_rs4(v, agg_all=agg_all, r4=r4):
                    v.wait_ge(s_ln.h, agg_all)
                    v.tensor_scalar(out=rs4a[:, :], in0=mv4[:, :, 1], scalar1=EPS,
                                    scalar2=None, op0=ALU.add).then_inc(s_v.h, 1)
                    v.wait_ge(s_v.h, r4[0])
                    v.tensor_scalar(out=rs4b[:, :], in0=rs4a[:, :], scalar1=-0.5,
                                    scalar2=None, op0=ALU.mult).then_inc(s_v.h, 1)
                    v.wait_ge(s_v.h, r4[1])
                    v.tensor_scalar(out=rstd4[:, :].bitcast(I32),
                                    in0=rs4a[:, :].bitcast(I32), scalar1=1,
                                    scalar2=None,
                                    op0=ALU.logical_shift_right).then_inc(s_v.h, 1)
                    v.wait_ge(s_v.h, r4[2])
                    v.tensor_scalar(out=rstd4[:, :].bitcast(I32),
                                    in0=rstd4[:, :].bitcast(I32), scalar1=-1.0,
                                    scalar2=RSQRT_MAGIC, op0=ALU.mult,
                                    op1=ALU.add).then_inc(s_v.h, 1)
                    v.wait_ge(s_v.h, r4[3])
                    k = 4
                    for itn in range(2):
                        v.tensor_tensor(rs4a[:, :], rstd4[:, :], rstd4[:, :],
                                        op=ALU.mult).then_inc(s_v.h, 1)
                        v.wait_ge(s_v.h, r4[k]); k += 1
                        v.tensor_tensor(rs4a[:, :], rs4a[:, :], rs4b[:, :],
                                        op=ALU.mult).then_inc(s_v.h, 1)
                        v.wait_ge(s_v.h, r4[k]); k += 1
                        if itn == 0:
                            v.tensor_scalar(out=rs4a[:, :], in0=rs4a[:, :], scalar1=1.5,
                                            scalar2=None, op0=ALU.add).then_inc(s_v.h, 1)
                            v.wait_ge(s_v.h, r4[k]); k += 1
                            v.tensor_tensor(rstd4[:, :], rstd4[:, :], rs4a[:, :],
                                            op=ALU.mult).then_inc(s_v.h, 1)
                            v.wait_ge(s_v.h, r4[k]); k += 1
                        else:
                            v.tensor_scalar(out=rs4a[:, :], in0=rs4a[:, :], scalar1=1.5,
                                            scalar2=None, op0=ALU.add).then_inc(s_v.h, 1)
                            v.wait_ge(s_v.h, r4[k]); k += 1
                            v.tensor_tensor(rstd4[:, :], rstd4[:, :], rs4a[:, :],
                                            op=ALU.mult).then_inc(s_rs.h, 1)
                E.add("vector", v_rs4)

                for tt in range(4):
                    g = 4 * s_i + tt
                    need_tp = tp_mark[g - 2] if g >= 2 else 0
                    t_nm = s_ln2.reserve()

                    def v_nm(v, tt=tt, t_rs4=t_rs4, need_tp=need_tp):
                        v.wait_ge(s_rs.h, t_rs4)
                        if need_tp > 0:
                            v.wait_ge(s_tp.h, need_tp)
                        v.tensor_scalar(out=lnb[:, tt % 2, :], in0=xs[:, tt, :],
                                        scalar1=mv4[:, tt, 0:1],
                                        scalar2=rstd4[:, tt:tt + 1],
                                        op0=ALU.subtract, op1=ALU.mult).then_inc(
                                        s_ln2.h, 1)
                    E.add("vector", v_nm)
                    ln2_mark.append(s_ln2.n)

                    for rnd in range(2):
                        tps = s_evT.n % 2
                        need_ev = s_evT.n - 1
                        t_tpv = s_tp.reserve()

                        def t_tp(t, tt=tt, rnd=rnd, need_ln=t_nm, need_ev=need_ev,
                                 tps=tps):
                            if rnd == 0:
                                t.wait_ge(s_ln2.h, need_ln)
                            if need_ev > 0:
                                t.wait_ge(s_evT.h, need_ev)
                            for i in range(8):
                                c = rnd * 8 + i
                                mmi = t.transpose(pTd[tps][:, i * 128:(i + 1) * 128],
                                                  lnb[:, tt % 2, c * 128:(c + 1) * 128],
                                                  ids[:, :])
                            mmi.then_inc(s_tp.h, 1)
                        E.add("tensor", t_tp)

                        s_evT.reserve()

                        def v_ev(v, tt=tt, rnd=rnd, need=t_tpv, tps=tps):
                            v.wait_ge(s_tp.h, need)
                            v.tensor_copy(
                                lnT[:, rnd * 8:(rnd + 1) * 8, tt * 128:(tt + 1) * 128],
                                pTd[tps][:, :].rearrange("p (c f) -> p c f",
                                                         f=128)).then_inc(
                                s_evT.h, 1)
                        E.add("vector", v_ev)
                    tp_mark.append(s_tp.n)

                ev_all = s_evT.n

                def t_sb_gate(t, need=ev_all):
                    t.wait_ge(s_evT.h, need)
                E.add("tensor", t_sb_gate)

                for h in range(HL):
                    for which, w_s in (("q", wq_s), ("k", wk_s)):
                        pj = s_proj.n % 2
                        need_pe = s_pe.n - 1
                        t_prj = s_proj.reserve()

                        def t_proj(t, h=h, w_s=w_s, pj=pj, need_pe=need_pe):
                            if need_pe > 0:
                                t.wait_ge(s_pe.h, need_pe)
                            for c in range(NCH):
                                mmi = t.matmul(pPJ[pj][:, :], w_s[:, c, h * 128:(h + 1) * 128],
                                               lnT[:, c, :], start=(c == 0),
                                               stop=(c == NCH - 1))
                            mmi.then_inc(s_proj.h, 1)
                        E.add("tensor", t_proj)

                        prslot = s_rope.n % 2
                        need_rope = s_rope.n - 1
                        t_pev = s_pe.reserve()

                        def v_pe(scl, need=t_prj, pj=pj, prslot=prslot,
                                 need_rope=need_rope):
                            scl.wait_ge(s_proj.h, need)
                            if need_rope > 0:
                                scl.wait_ge(s_rope.h, need_rope)
                            scl.copy(out=pre[:, prslot, :],
                                     in_=pPJ[pj][:, :]).then_inc(s_pe.h, 1)
                        E.add("scalar", v_pe)

                        need_rp = s_rope.n - 1
                        t_swv = s_sw.reserve()

                        def t_sw(t, need=t_pev, prslot=prslot, need_rp=need_rp):
                            t.wait_ge(s_pe.h, need)
                            if need_rp > 0:
                                t.wait_ge(s_rope.h, need_rp)
                            t.matmul(pSw[prslot][:, :], psw[:, :], pre[:, prslot, :],
                                     start=True, stop=True).then_inc(s_sw.h, 1)
                        E.add("tensor", t_sw)

                        ridx = s_rope.n
                        rslot = ridx % 2
                        va = s_v.reserve(2)
                        need_wr = s_spq[rslot].n if ridx >= 2 else 0
                        t_rp = s_rope.reserve()

                        def v_rope(v, s_i=s_i, rslot=rslot, need=t_swv, va=va,
                                   need_wr=need_wr):
                            v.wait_ge(s_sw.h, need)
                            v.tensor_tensor(tmpA[:, rslot, :], pre[:, rslot, :],
                                            ccs[:, s_i * SBK:(s_i + 1) * SBK],
                                            op=ALU.mult).then_inc(s_v.h, 1)
                            v.tensor_tensor(swp[:, rslot, :], pSw[rslot][:, :],
                                            ss2s[:, s_i * SBK:(s_i + 1) * SBK],
                                            op=ALU.mult).then_inc(s_v.h, 1)
                            v.wait_ge(s_v.h, va)
                            if need_wr > 0:
                                v.wait_ge(s_spq[rslot].h, need_wr)
                            v.tensor_tensor(ropo[:, rslot, :], tmpA[:, rslot, :],
                                            swp[:, rslot, :], op=ALU.add).then_inc(s_rope.h, 1)
                        E.add("vector", v_rope)

                        s_spq[rslot].reserve(16)

                        def sc_spill(scl, s_i=s_i, h=h, which=which, rslot=rslot,
                                     need=t_rp):
                            scl.wait_ge(s_rope.h, need)
                            if which == "q":
                                dst = qsp[h][:, s_i, :]
                            else:
                                dst = ksp[h][:, 4 * s_i:4 * s_i + 4, :].rearrange(
                                    "p a b -> p (a b)")
                            scl.dma_start(out=dst, in_=ropo[:, rslot, :]).then_inc(
                                s_spq[rslot].h, 16)
                        E.add("scalar", sc_spill)

                for tt in range(4):
                    g = 4 * s_i + tt
                    pj = s_proj.n % 2
                    need_pe = s_pe.n - 1
                    t_vp = s_proj.reserve()

                    def t_v(t, tt=tt, pj=pj, need_pe=need_pe):
                        if need_pe > 0:
                            t.wait_ge(s_pe.h, need_pe)
                        for c in range(NCH):
                            mmi = t.matmul(pPJ[pj][:, :], lnT[:, c, tt * 128:(tt + 1) * 128],
                                           wv_s[:, c, :], start=(c == 0), stop=(c == NCH - 1))
                        mmi.then_inc(s_proj.h, 1)
                    E.add("tensor", t_v)

                    vslot = g % 2
                    need_wr = s_spv[vslot].n if g >= 2 else 0
                    t_vev = s_pe.reserve()

                    def v_vev(scl, vslot=vslot, pj=pj, need=t_vp, need_wr=need_wr):
                        scl.wait_ge(s_proj.h, need)
                        if need_wr > 0:
                            scl.wait_ge(s_spv[vslot].h, need_wr)
                        scl.copy(out=vstg[:, vslot, :],
                                 in_=pPJ[pj][:, :]).then_inc(s_pe.h, 1)
                    E.add("scalar", v_vev)

                    s_spv[vslot].reserve(16 * HL)

                    def sc_vsp(scl, g=g, vslot=vslot, need=t_vev):
                        scl.wait_ge(s_pe.h, need)
                        for h in range(HL):
                            scl.dma_start(
                                out=vsp[h][:, g, :],
                                in_=vstg[:, vslot, h * 128:(h + 1) * 128]).then_inc(
                                s_spv[vslot].h, 16)
                    E.add("scalar", sc_vsp)

            def p1_fin(scl, a0=s_spq[0].n, a1=s_spq[1].n, b0=s_spv[0].n, b1=s_spv[1].n):
                scl.wait_ge(s_spq[0].h, a0)
                scl.wait_ge(s_spq[1].h, a1)
                scl.wait_ge(s_spv[0].h, b0)
                scl.wait_ge(s_spv[1].h, b1)
            E.add("scalar", p1_fin)
            E.flush(blk1)

        with contextlib.ExitStack() as es2:
            sb2 = lambda n_, sh, dt: es2.enter_context(nc.sbuf_tensor(n_, sh, dt))
            ps2 = lambda n_, sh, dt: es2.enter_context(nc.psum_tensor(n_, sh, dt))
            msk = sb2("msk", [128, 4, 512], BF16)
            qbuf = sb2("qbuf", [128, 2, 512], BF16)
            kbuf = sb2("kbuf", [128, 4, CKT * 128], BF16)
            vbuf = sb2("vbuf", [128, 4, CKT * 128], BF16)
            pb = sb2("pb", [128, 4, 512], BF16)
            binv = sb2("binv", [128, 512], F32)
            oout = sb2("oout", [128, 2, 512], BF16)
            pS = ps2("pS", [128, 4, 512], F32)
            pO0 = ps2("pO0", [128, 512], F32)
            pO1 = ps2("pO1", [128, 512], F32)
            pOb = [pO0, pO1]
            pZ = ps2("pZ", [128, 512], F32)

            blk2 = es2.enter_context(nc.Block("p2"))
            E = Emit()

            n_mskld = s_ldx.reserve(16 * 4)

            def ld_msk(sync):
                m_r = masks_in.rearrange("(t p) q -> t p q", p=128)
                for i in range(4):
                    sync.dma_start(out=msk[:, i, :], in_=m_r[i]).then_inc(s_ldx.h, 16)
            E.add("sync", ld_msk)

            def eng_gate2(e, need=n_mskld):
                e.wait_ge(s_ldx.h, need)
            E.add("vector", eng_gate2)
            E.add("tensor", eng_gate2)
            E.add("scalar", eng_gate2)

            mm_mark_iter = []
            kcons_mark = []
            zr_mark = []
            gchunk = 0
            for qb in range(NSB):
                nkt = 4 * (qb + 1)
                nchunks = (nkt + CKT - 1) // CKT
                for h in range(HL):
                    it = qb * HL + h
                    qslot = it % 2
                    need_qg = mm_mark_iter[it - 2] if it >= 2 else 0
                    q_mark = s_q[qslot].reserve(16)

                    def ld_q(sync, qb=qb, h=h, qslot=qslot, need=need_qg):
                        if need > 0:
                            sync.wait_ge(s_mm.h, need)
                        sync.dma_start(out=qbuf[:, qslot, :],
                                       in_=qsp[h][:, qb, :]).then_inc(s_q[qslot].h, 16)
                    E.add("sync", ld_q)

                    ch_base = gchunk
                    ld_marks = {}

                    def emit_ld_kv(ch, qb=qb, h=h, nkt=nkt, ch_base=ch_base,
                                   ld_marks=ld_marks):
                        nonlocal gchunk
                        k0 = ch * CKT
                        nk = min(CKT, nkt - k0)
                        gc = ch_base + ch
                        cs = gc % 4
                        need = kcons_mark[gc - 4] if gc >= 4 else 0
                        ld_marks[ch] = s_kv[cs].reserve(32)

                        def ld_kv(sync, h=h, k0=k0, nk=nk, cs=cs, need=need):
                            if need > 0:
                                sync.wait_ge(s_av.h, need)
                            sync.dma_start(
                                out=kbuf[:, cs, 0:nk * 128],
                                in_=ksp[h][:, k0:k0 + nk, :].rearrange(
                                    "p a b -> p (a b)")).then_inc(s_kv[cs].h, 16)
                            sync.dma_start(
                                out=vbuf[:, cs, 0:nk * 128],
                                in_=vsp[h][:, k0:k0 + nk, :].rearrange(
                                    "p a b -> p (a b)")).then_inc(s_kv[cs].h, 16)
                        E.add("sync", ld_kv)
                        gchunk += 1

                    for ch in range(nchunks):
                        emit_ld_kv(ch)

                    o_guard = s_o.n - 1
                    pO = pOb[it % 2]

                    def emit_t_s(kt):
                        ch = kt // CKT
                        cs = (ch_base + ch) % 4
                        ko = kt - ch * CKT
                        sslot = kt % 4
                        need_exp = s_exp.n - 1
                        t_sv = s_mm.reserve() if kt % 2 == 1 else s_mm.n

                        def t_s(t, kt=kt, cs=cs, ko=ko, sslot=sslot, qslot=qslot,
                                need_ld=ld_marks[ch], need_exp=need_exp,
                                o_guard=o_guard, q_mark=q_mark):
                            if kt == 0:
                                t.wait_ge(s_q[qslot].h, q_mark)
                                t.wait_ge(s_kv[cs].h, need_ld)
                            elif ko == 0:
                                t.wait_ge(s_kv[cs].h, need_ld)
                            if kt % 2 == 0 and need_exp > 0:
                                t.wait_ge(s_exp.h, need_exp)
                            mmi = t.matmul(pS[:, sslot, :],
                                           kbuf[:, cs, ko * 128:(ko + 1) * 128],
                                           qbuf[:, qslot, :], start=True,
                                           stop=True)
                            if kt % 2 == 1:
                                mmi.then_inc(s_mm.h, 1)
                        E.add("tensor", t_s)
                        return t_sv

                    ts_marks = {0: emit_t_s(0), 1: emit_t_s(1)}
                    zprev = zr_mark[it - 1] if it >= 1 else 0

                    for bp in range(nkt // 2):
                        kt0 = 2 * bp
                        pp = (bp % 2) * 2
                        need_z = s_zp.n - 1
                        t_ev = s_exp.reserve()

                        def s_e2(scl, pp=pp, need_mm=ts_marks[kt0 + 1],
                                 need_z=need_z):
                            scl.wait_ge(s_mm.h, need_mm)
                            if need_z > 0:
                                scl.wait_ge(s_zp.h, need_z)
                            scl.activation(out=pb[:, pp:pp + 2, :],
                                           in_=pS[:, pp:pp + 2, :],
                                           func=AF.Exp, bias=cexp[:, :],
                                           scale=SCL).then_inc(s_exp.h, 1)
                        E.add("scalar", s_e2)

                        ndiag = 0
                        for j in range(2):
                            kt = kt0 + j
                            diag = kt >= 4 * qb
                            if diag:
                                ndiag += 1
                                s_msk.reserve()

                                def v_m(v, qb=qb, kt=kt, pslot=pp + j, need=t_ev):
                                    v.wait_ge(s_exp.h, need)
                                    v.tensor_tensor(pb[:, pslot, :], pb[:, pslot, :],
                                                    msk[:, kt - 4 * qb, :],
                                                    op=ALU.mult).then_inc(s_msk.h, 1)
                                E.add("vector", v_m)

                        if kt0 + 2 < nkt:
                            ts_marks[kt0 + 2] = emit_t_s(kt0 + 2)
                            ts_marks[kt0 + 3] = emit_t_s(kt0 + 3)

                        need_m = s_msk.n
                        for j in range(2):
                            kt = kt0 + j
                            ch = kt // CKT
                            cs = (ch_base + ch) % 4
                            ko = kt - ch * CKT
                            lastk = (ko == CKT - 1 or kt == nkt - 1)
                            if lastk:
                                s_av.reserve()

                            def t_av(t, kt=kt, cs=cs, ko=ko, pslot=pp + j, j=j,
                                     ndiag=ndiag, need_e=t_ev, need_m=need_m,
                                     nkt=nkt, pO=pO, o_guard=o_guard,
                                     lastk=lastk):
                                if kt == 0 and o_guard > 0:
                                    t.wait_ge(s_o.h, o_guard)
                                if j == 0:
                                    if ndiag > 0:
                                        t.wait_ge(s_msk.h, need_m)
                                    else:
                                        t.wait_ge(s_exp.h, need_e)
                                mmi = t.matmul(pO[:, :],
                                               vbuf[:, cs, ko * 128:(ko + 1) * 128],
                                               pb[:, pslot, :], start=(kt == 0),
                                               stop=(kt == nkt - 1))
                                if lastk:
                                    mmi.then_inc(s_av.h, 1)
                            E.add("tensor", t_av)
                            if lastk:
                                kcons_mark.append(s_av.n)

                        s_zp.reserve()
                        for j in range(2):
                            kt = kt0 + j

                            def t_z(t, kt=kt, pslot=pp + j, j=j, zprev=zprev,
                                    nkt=nkt):
                                if kt == 0 and zprev > 0:
                                    t.wait_ge(s_zi.h, zprev)
                                mmi = t.matmul(pZ[:, :], ones128[:, :],
                                               pb[:, pslot, :], start=(kt == 0),
                                               stop=(kt == nkt - 1))
                                if j == 1:
                                    mmi.then_inc(s_zp.h, 1)
                            E.add("tensor", t_z)

                    mm_mark_iter.append(s_mm.n)

                    zm_all = s_zp.n
                    t_ziv = s_zi.reserve()
                    zr_mark.append(s_zi.n)

                    def v_zi(v, zm_all=zm_all):
                        v.wait_ge(s_zp.h, zm_all)
                        v.reciprocal(binv[:, :], pZ[:, :]).then_inc(s_zi.h, 1)
                    E.add("vector", v_zi)

                    oslot = it % 2
                    need_st = s_sto[oslot].n if it >= 2 else 0
                    t_ov = s_o.reserve()

                    def v_o(v, oslot=oslot, t_ziv=t_ziv, need_st=need_st,
                            need_av=s_av.n, pO=pOb[it % 2]):
                        v.wait_ge(s_zi.h, t_ziv)
                        v.wait_ge(s_av.h, need_av)
                        if need_st > 0:
                            v.wait_ge(s_sto[oslot].h, need_st)
                        v.tensor_tensor(oout[:, oslot, :], pO[:, :], binv[:, :],
                                        op=ALU.mult).then_inc(s_o.h, 1)
                    E.add("vector", v_o)

                    s_sto[oslot].reserve(16)

                    def gp_o(g, qb=qb, h=h, oslot=oslot, need=t_ov):
                        g.wait_ge(s_o.h, need)
                        k = qb // 4
                        g.dma_start(
                            out=ccg_in[k][h * 128:(h + 1) * 128,
                                          (qb % 4) * 512:(qb % 4 + 1) * 512],
                            in_=oout[:, oslot, :]).then_inc(s_sto[oslot].h, 16)
                    E.add("gpsimd", gp_o)

                if qb % 4 == 3:
                    s_cc.reserve()

                    def g_cc(g, k=qb // 4, need0=s_sto[0].n, need1=s_sto[1].n):
                        g.wait_ge(s_sto[0].h, need0)
                        g.wait_ge(s_sto[1].h, need1)
                        g.collective_compute(
                            "AllGather", ALU.bypass,
                            replica_groups=[[0, 1, 2, 3], [4, 5, 6, 7]],
                            ins=[ccg_in[k].ap().opt()],
                            outs=[ccg_out[k].ap().opt()]).then_inc(s_cc.h, 1)
                    E.add("gpsimd", g_cc)
            E.flush(blk2)

        with contextlib.ExitStack() as es3:
            sb3 = lambda n_, sh, dt: es3.enter_context(nc.sbuf_tensor(n_, sh, dt))
            ps3 = lambda n_, sh, dt: es3.enter_context(nc.psum_tensor(n_, sh, dt))
            wo_s = sb3("wo_s", [128, NCH, D], BF16)
            aT = sb3("aT", [128, NCH, 512], BF16)
            y1T = sb3("y1T", [128, NCH, 512], BF16)
            y1 = sb3("y1", [128, 4, D], BF16)
            xqb = sb3("xqb", [128, 2, D], F32)
            t3 = sb3("t3", [128, D], F32)
            r1b = sb3("r1b", [128, 2, D], F32)
            pY = ps3("pY", [128, 1024], F32)
            pT3 = ps3("pT3", [128, 2048], BF16)

            blk3 = es3.enter_context(nc.Block("p3"))
            E = Emit()

            n_wld3 = s_wld.reserve(16 * NCH)

            def ld_wo(g):
                w_r = woT.rearrange("(c p) o -> c p o", p=128)
                for c in range(NCH):
                    g.dma_start(out=wo_s[:, c, :], in_=w_r[c]).then_inc(s_wld.h, 16)
            E.add("gpsimd", ld_wo)

            def t_gate3(t, need=n_wld3):
                t.wait_ge(s_wld.h, need)
            E.add("tensor", t_gate3)

            xq_r = xq_in.rearrange("(g p) d -> g p d", p=128)
            r1_r = r1sp.rearrange("(g p) d -> g p d", p=128)

            r1_mark = []
            mm3_mark = []
            for tb in range(NBLK):
                need_mm = mm3_mark[tb - 1] if tb >= 1 else 0
                lda_tb = s_lda.reserve(16 * NCH)

                def sync_aT(sync, tb=tb, need_mm=need_mm):
                    if need_mm > 0:
                        sync.wait_ge(s_mm3.h, need_mm)
                    pid = sync.partition_id()
                    for core in nc.Switch(engines=[sync], index=[pid], n=8):
                        tq = core % 4
                        sync.wait_ge(s_cc.h, tq + 1)
                        co = ccg_out[tq]
                        for cb in range(NCH):
                            sync.dma_start(
                                out=aT[:, cb, :],
                                in_=co[cb * 128:(cb + 1) * 128,
                                       tb * 512:(tb + 1) * 512]).then_inc(s_lda.h, 16)
                E.add("sync", sync_aT)

                for oc in range(NCH):
                    yslot = s_yev.n % 2
                    need_ev = s_yev.n - 1
                    t_mmv = s_mm3.reserve()

                    def t_op(t, oc=oc, yslot=yslot, need_ld=lda_tb, need_ev=need_ev):
                        if oc == 0:
                            t.wait_ge(s_lda.h, need_ld)
                        if need_ev > 0:
                            t.wait_ge(s_yev.h, need_ev)
                        for c in range(NCH):
                            mmi = t.matmul(pY[:, yslot * 512:(yslot + 1) * 512],
                                           wo_s[:, c, oc * 128:(oc + 1) * 128],
                                           aT[:, c, :], start=(c == 0), stop=(c == NCH - 1))
                        mmi.then_inc(s_mm3.h, 1)
                    E.add("tensor", t_op)

                    s_yev.reserve()

                    def s_ye(scl, oc=oc, yslot=yslot, need=t_mmv):
                        scl.wait_ge(s_mm3.h, need)
                        scl.copy(y1T[:, oc, :],
                                 pY[:, yslot * 512:(yslot + 1) * 512]).then_inc(s_yev.h, 1)
                    E.add("scalar", s_ye)
                mm3_mark.append(s_mm3.n)

                for rnd in range(8):
                    tslot = s_evT.n % 2
                    need_ev = s_yev.n
                    need_pt = s_evT.n - 1
                    t_tpv = s_tp.reserve()

                    def t_tp3(t, rnd=rnd, tslot=tslot, need_ev=need_ev, need_pt=need_pt):
                        if rnd == 0:
                            t.wait_ge(s_yev.h, need_ev)
                        if need_pt > 0:
                            t.wait_ge(s_evT.h, need_pt)
                        for i in range(8):
                            j = rnd * 8 + i
                            oc, sub = j // 4, j % 4
                            mmi = t.transpose(
                                pT3[:, tslot * 1024 + i * 128:tslot * 1024 + (i + 1) * 128],
                                y1T[:, oc, sub * 128:(sub + 1) * 128],
                                ids[:, :])
                        mmi.then_inc(s_tp.h, 1)
                    E.add("tensor", t_tp3)

                    s_evT.reserve()

                    def v_pt3(v, rnd=rnd, tslot=tslot, need=t_tpv):
                        v.wait_ge(s_tp.h, need)
                        oc0 = rnd * 2
                        v.tensor_copy(
                            y1[:, :, oc0 * 128:(oc0 + 2) * 128].rearrange(
                                "p s (c f) -> p c s f", f=128),
                            pT3[:, tslot * 1024:(tslot + 1) * 1024].rearrange(
                                "p (c s f) -> p c s f",
                                c=2, s=4)).then_inc(s_evT.h, 1)
                    E.add("vector", v_pt3)

                for sub in range(4):
                    g = tb * 4 + sub
                    need_r1g = r1_mark[g - 2] if g >= 2 else 0
                    t_xqv = s_xq[g % 2].reserve(16)

                    def ld_xq(sync, g=g, need=need_r1g):
                        if need > 0:
                            sync.wait_ge(s_r1.h, need)
                        sync.dma_start(out=xqb[:, g % 2, :], in_=xq_r[g]).then_inc(
                            s_xq[g % 2].h, 16)
                    E.add("sync", ld_xq)

                    need_t = s_evT.n

                    def v_lngate(v, need_t=need_t):
                        v.wait_ge(s_evT.h, need_t)
                    E.add("vector", v_lngate)
                    tnorm = queue_ln(E, (lambda sub=sub: y1[:, sub, :]),
                                     (lambda: t3[:, :]))

                    need_wr = s_sr[g % 2].n if g >= 2 else 0
                    t_r1v = s_r1.reserve()

                    def v_r1(v, g=g, need_ld=t_xqv, need_wr=need_wr, tnorm=tnorm):
                        v.wait_ge(s_xq[g % 2].h, need_ld)
                        v.wait_ge(s_ln2.h, tnorm)
                        if need_wr > 0:
                            v.wait_ge(s_sr[g % 2].h, need_wr)
                        v.tensor_tensor(r1b[:, g % 2, :], t3[:, :],
                                        xqb[:, g % 2, :], op=ALU.add).then_inc(s_r1.h, 1)
                    E.add("vector", v_r1)
                    r1_mark.append(s_r1.n)

                    s_sr[g % 2].reserve(16)

                    def gp_r1(gp, g=g, need=t_r1v):
                        gp.wait_ge(s_r1.h, need)
                        gp.dma_start(out=r1_r[g], in_=r1b[:, g % 2, :]).then_inc(
                            s_sr[g % 2].h, 16)
                    E.add("gpsimd", gp_r1)

            def p3_fin(scl, a0=s_sr[0].n, a1=s_sr[1].n):
                scl.wait_ge(s_sr[0].h, a0)
                scl.wait_ge(s_sr[1].h, a1)
            E.add("scalar", p3_fin)
            E.flush(blk3)

        with contextlib.ExitStack() as es4:
            sb4 = lambda n_, sh, dt: es4.enter_context(nc.sbuf_tensor(n_, sh, dt))
            ps4 = lambda n_, sh, dt: es4.enter_context(nc.psum_tensor(n_, sh, dt))
            wb_s = sb4("wb_s", [128, NCH, D], BF16)
            awr_s = sb4("awr_s", [128, NCH, 72], BF16)
            bfl_s = sb4("bfl_s", [64, D], BF16)
            bb_s = sb4("bb_s", [128, 16], F32)
            r1a = sb4("r1a", [128, 2, D], F32)
            r1c = sb4("r1c", [128, 2, D], F32)
            ln2 = sb4("ln2", [128, D], BF16)
            ln2T = sb4("ln2T", [128, NCH, 512], BF16)
            hbuf = sb4("hbuf", [128, 512], BF16)
            hbufL = sb4("hbufL", [8, 512], F32)
            lgt = sb4("lgt", [128, 4, 8], F32)
            negm1 = sb4("negm1", [128, 1], F32)
            m2b = sb4("m2b", [128, 1], F32)
            eqb = sb4("eqb", [128, 8], F32)
            lmk = sb4("lmk", [128, 8], F32)
            pexp = sb4("pexp", [128, 8], F32)
            indb = sb4("indb", [128, 8], F32)
            wgt = sb4("wgt", [128, 8], F32)
            zg = sb4("zg", [128, 1], F32)
            zgi = sb4("zgi", [128, 1], F32)
            wn = sb4("wn", [128, 8], F32)
            wnT = sb4("wnT", [8, 128], BF16)
            wexpT = sb4("wexpT", [64, 512], BF16)
            h2 = sb4("h2", [64, 512], BF16)
            ffnT = sb4("ffnT", [128, NCH, 512], BF16)
            ffn = sb4("ffn", [128, 4, D], BF16)
            t4 = sb4("t4", [128, D], F32)
            r2b = sb4("r2b", [128, 2, D], F32)
            pY4 = ps4("pY4", [128, 1024], F32)
            pT4 = ps4("pT4", [128, 2048], BF16)
            pH = ps4("pH", [128, 512], F32)
            pG = ps4("pG", [128, 512], F32)

            blk4 = es4.enter_context(nc.Block("p4"))
            E = Emit()

            n_wld4 = s_wld.reserve(16 * (2 * NCH + 1))

            def ld_w4(g):
                w_r = wbT.rearrange("(c p) o -> c p o", p=128)
                for c in range(NCH):
                    g.dma_start(out=wb_s[:, c, :], in_=w_r[c]).then_inc(s_wld.h, 16)
                a_r = awrT.rearrange("(c p) o -> c p o", p=128)
                for c in range(NCH):
                    g.dma_start(out=awr_s[:, c, :], in_=a_r[c]).then_inc(s_wld.h, 16)
                g.dma_start(out=bfl_s[:, :], in_=bflT[:, :]).then_inc(s_wld.h, 16)
            E.add("gpsimd", ld_w4)

            n_ld4 = s_ldx.reserve(16)

            def ld_bb(sync):
                sync.dma_start(out=bb_s[:, :], in_=bbase[:, :]).then_inc(s_ldx.h, 16)
            E.add("sync", ld_bb)

            def gate4(e, a=n_wld4, b_=n_ld4):
                e.wait_ge(s_wld.h, a)
                e.wait_ge(s_ldx.h, b_)
            E.add("tensor", gate4)
            E.add("vector", gate4)
            E.add("scalar", gate4)

            out_r = out_q.rearrange("(g p) d -> g p d", p=128)

            ln3_mark = []
            r2_mark = []
            for tb in range(NBLK):
                for sub in range(4):
                    g = tb * 4 + sub
                    need_lg = ln3_mark[g - 2] if g >= 2 else 0
                    t_rav = s_ra[g % 2].reserve(16)

                    def ld_r1(sync, g=g, need=need_lg):
                        if need > 0:
                            sync.wait_ge(s_ln2.h, need)
                        sync.dma_start(out=r1a[:, g % 2, :], in_=r1_r[g]).then_inc(
                            s_ra[g % 2].h, 16)
                    E.add("sync", ld_r1)

                    need_tp = s_tp.n

                    def v_lngate4(v, g=g, t_rav=t_rav, need_tp=need_tp):
                        v.wait_ge(s_ra[g % 2].h, t_rav)
                        if need_tp > 0:
                            v.wait_ge(s_tp.h, need_tp)
                    E.add("vector", v_lngate4)
                    queue_ln(E, (lambda g=g: r1a[:, g % 2, :]), (lambda: ln2[:, :]))
                    ln3_mark.append(s_ln2.n)

                    for rnd in range(2):
                        tslot = s_evT.n % 2
                        need_ln = s_ln2.n
                        need_ev = s_evT.n - 1
                        t_tpv = s_tp.reserve()

                        def t_tp4(t, rnd=rnd, tslot=tslot, need_ln=need_ln, need_ev=need_ev):
                            if rnd == 0:
                                t.wait_ge(s_ln2.h, need_ln)
                            if need_ev > 0:
                                t.wait_ge(s_evT.h, need_ev)
                            for i in range(8):
                                c = rnd * 8 + i
                                mmi = t.transpose(
                                    pT4[:, tslot * 1024 + i * 128:tslot * 1024 + (i + 1) * 128],
                                    ln2[:, c * 128:(c + 1) * 128], ids[:, :])
                            mmi.then_inc(s_tp.h, 1)
                        E.add("tensor", t_tp4)

                        s_evT.reserve()

                        def v_ev4(v, sub=sub, rnd=rnd, tslot=tslot, need=t_tpv):
                            v.wait_ge(s_tp.h, need)
                            v.tensor_copy(
                                ln2T[:, rnd * 8:(rnd + 1) * 8, sub * 128:(sub + 1) * 128],
                                pT4[:, tslot * 1024:(tslot + 1) * 1024].rearrange(
                                    "p (c f) -> p c f", f=128)).then_inc(
                                s_evT.h, 1)
                        E.add("vector", v_ev4)

                need_g = s_g.n
                t_awrv = s_mm3.reserve()

                def t_awr(t, need=s_evT.n, need_g=need_g):
                    t.wait_ge(s_evT.h, need)
                    if need_g > 0:
                        t.wait_ge(s_g.h, need_g)
                    for c in range(NCH):
                        mmi = t.matmul(pH[0:72, :], awr_s[:, c, :], ln2T[:, c, :],
                                       start=(c == 0), stop=(c == NCH - 1))
                    mmi.then_inc(s_mm3.h, 1)
                E.add("tensor", t_awr)

                va = s_v.reserve()
                t_gv = s_g.reserve()

                def v_hev(v, need=t_awrv, va=va):
                    v.wait_ge(s_mm3.h, need)
                    v.tensor_copy(hbuf[0:64, :], pH[0:64, :]).then_inc(s_v.h, 1)
                    v.wait_ge(s_v.h, va)
                    v.tensor_copy(hbufL[:, :], pH[64:72, :]).then_inc(s_g.h, 1)
                E.add("vector", v_hev)

                need_cons = s_lda.n
                for sub in range(4):
                    t_lgv = s_tp.reserve()

                    def t_lg(t, sub=sub, need=t_gv, need_cons=need_cons):
                        if sub == 0:
                            t.wait_ge(s_g.h, need)
                            if need_cons > 0:
                                t.wait_ge(s_lda.h, need_cons)
                        t.transpose(pG[:, sub * 8:(sub + 1) * 8],
                                    hbufL[:, sub * 128:(sub + 1) * 128],
                                    idsf[0:8, 0:8]).then_inc(s_tp.h, 1)
                    E.add("tensor", t_lg)

                lg_need = s_tp.n
                t_lgc = s_lda.reserve()

                def v_lg(v, need=lg_need):
                    v.wait_ge(s_tp.h, need)
                    v.tensor_copy(
                        lgt[:, :, :], pG[:, 0:32].rearrange(
                            "p (s e) -> p s e", e=8)).then_inc(s_lda.h, 1)
                E.add("vector", v_lg)

                for sub in range(4):
                    va = s_v.reserve(4)
                    t_ind = s_msk.reserve()

                    def v_g1(v, sub=sub, need=t_lgc, va=va):
                        if sub == 0:
                            v.wait_ge(s_lda.h, need)
                        l = lgt[:, sub, :]
                        v.reduce_max(negm1[:, :], l, axis=AX.X,
                                     negate=True).then_inc(s_v.h, 1)
                        v.wait_ge(s_v.h, va - 3)
                        v.tensor_scalar(out=eqb[:, :], in0=l, scalar1=negm1[:, :],
                                        scalar2=0.0, op0=ALU.add,
                                        op1=ALU.is_ge).then_inc(s_v.h, 1)
                        v.wait_ge(s_v.h, va - 2)
                        v.scalar_tensor_tensor(out=lmk[:, :], in0=eqb[:, :],
                                               scalar=-30000.0, in1=l,
                                               op0=ALU.mult, op1=ALU.add).then_inc(s_v.h, 1)
                        v.wait_ge(s_v.h, va - 1)
                        v.reduce_max(m2b[:, :], lmk[:, :], axis=AX.X).then_inc(s_v.h, 1)
                        v.wait_ge(s_v.h, va)
                        v.tensor_scalar(out=indb[:, :], in0=l, scalar1=m2b[:, :],
                                        scalar2=None, op0=ALU.is_ge).then_inc(s_msk.h, 1)
                    E.add("vector", v_g1)

                    t_px = s_rs.reserve()

                    def s_px(scl, sub=sub, need=t_ind):
                        scl.wait_ge(s_msk.h, need)
                        scl.activation(out=pexp[:, :], in_=lgt[:, sub, :],
                                       func=AF.Exp, bias=negm1[:, :],
                                       scale=1.0).then_inc(s_rs.h, 1)
                    E.add("scalar", s_px)

                    va = s_v.reserve(3)
                    t_wnv = s_ln2.reserve()

                    def v_g2(v, need=t_px, va=va):
                        v.wait_ge(s_rs.h, need)
                        v.tensor_tensor(wgt[:, :], indb[:, :], pexp[:, :],
                                        op=ALU.mult).then_inc(s_v.h, 1)
                        v.wait_ge(s_v.h, va - 2)
                        v.reduce_sum(zg[:, :], wgt[:, :], axis=AX.X).then_inc(s_v.h, 1)
                        v.wait_ge(s_v.h, va - 1)
                        v.reciprocal(zgi[:, :], zg[:, :]).then_inc(s_v.h, 1)
                        v.wait_ge(s_v.h, va)
                        v.tensor_scalar(out=wn[:, :], in0=wgt[:, :],
                                        scalar1=zgi[:, :], scalar2=LORA_SCALE,
                                        op0=ALU.mult, op1=ALU.mult).then_inc(s_ln2.h, 1)
                    E.add("vector", v_g2)

                    need_ev = s_yev.n
                    t_wtv = s_tp.reserve()

                    def t_wnT(t, need=t_wnv, need_ev=need_ev):
                        t.wait_ge(s_ln2.h, need)
                        if need_ev > 0:
                            t.wait_ge(s_yev.h, need_ev)
                        t.transpose(pG[0:8, 128:256], wn[:, :],
                                    idsf[:, :]).then_inc(s_tp.h, 1)
                    E.add("tensor", t_wnT)

                    t_wcv = s_r1.reserve()

                    def v_wnT(v, need=t_wtv):
                        v.wait_ge(s_tp.h, need)
                        v.tensor_copy(wnT[:, :], pG[0:8, 128:256]).then_inc(s_r1.h, 1)
                    E.add("vector", v_wnT)

                    t_e8v = s_bc.reserve()

                    def t_e8(t, need=t_wcv):
                        t.wait_ge(s_r1.h, need)
                        t.matmul(pG[0:64, 256:384], e8s[:, :], wnT[:, :],
                                 start=True, stop=True).then_inc(s_bc.h, 1)
                    E.add("tensor", t_e8)

                    s_yev.reserve()

                    def v_we(v, sub=sub, need=t_e8v):
                        v.wait_ge(s_bc.h, need)
                        v.tensor_copy(wexpT[:, sub * 128:(sub + 1) * 128],
                                      pG[0:64, 256:384]).then_inc(s_yev.h, 1)
                    E.add("vector", v_we)

                t_h2v = s_g.reserve()

                def v_h2(v, need=s_yev.n):
                    v.wait_ge(s_yev.h, need)
                    v.tensor_tensor(h2[:, :], hbuf[0:64, :], wexpT[:, :],
                                    op=ALU.mult).then_inc(s_g.h, 1)
                E.add("vector", v_h2)

                for oc in range(NCH):
                    yslot = s_exp.n % 2
                    need_ev = s_exp.n - 1
                    t_fv = s_mm.reserve()

                    def t_ffn(t, oc=oc, yslot=yslot, need_h2=t_h2v, need_ev=need_ev):
                        if oc == 0:
                            t.wait_ge(s_g.h, need_h2)
                        if need_ev > 0:
                            t.wait_ge(s_exp.h, need_ev)
                        for c in range(NCH):
                            t.matmul(pY4[:, yslot * 512:(yslot + 1) * 512],
                                     wb_s[:, c, oc * 128:(oc + 1) * 128],
                                     ln2T[:, c, :], start=(c == 0), stop=False)
                        t.matmul(pY4[:, yslot * 512:(yslot + 1) * 512],
                                 bfl_s[:, oc * 128:(oc + 1) * 128],
                                 h2[:, :], start=False, stop=True).then_inc(s_mm.h, 1)
                    E.add("tensor", t_ffn)

                    s_exp.reserve()

                    def s_fev(scl, oc=oc, yslot=yslot, need=t_fv):
                        scl.wait_ge(s_mm.h, need)
                        scl.activation(out=ffnT[:, oc, :],
                                       in_=pY4[:, yslot * 512:(yslot + 1) * 512],
                                       func=AF.Identity, bias=bb_s[:, oc:oc + 1],
                                       scale=1.0).then_inc(s_exp.h, 1)
                    E.add("scalar", s_fev)

                for rnd in range(8):
                    tslot = s_evT.n % 2
                    need_ev = s_exp.n
                    need_pt = s_evT.n - 1
                    t_tpv = s_tp.reserve()

                    def t_tpf(t, rnd=rnd, tslot=tslot, need_ev=need_ev, need_pt=need_pt):
                        if rnd == 0:
                            t.wait_ge(s_exp.h, need_ev)
                        if need_pt > 0:
                            t.wait_ge(s_evT.h, need_pt)
                        for i in range(8):
                            j = rnd * 8 + i
                            oc, sub = j // 4, j % 4
                            mmi = t.transpose(
                                pT4[:, tslot * 1024 + i * 128:tslot * 1024 + (i + 1) * 128],
                                ffnT[:, oc, sub * 128:(sub + 1) * 128],
                                ids[:, :])
                        mmi.then_inc(s_tp.h, 1)
                    E.add("tensor", t_tpf)

                    s_evT.reserve()

                    def v_ptf(v, rnd=rnd, tslot=tslot, need=t_tpv):
                        v.wait_ge(s_tp.h, need)
                        oc0 = rnd * 2
                        v.tensor_copy(
                            ffn[:, :, oc0 * 128:(oc0 + 2) * 128].rearrange(
                                "p s (c f) -> p c s f", f=128),
                            pT4[:, tslot * 1024:(tslot + 1) * 1024].rearrange(
                                "p (c s f) -> p c s f",
                                c=2, s=4)).then_inc(s_evT.h, 1)
                    E.add("vector", v_ptf)

                for sub in range(4):
                    g = tb * 4 + sub
                    need_r2g = r2_mark[g - 2] if g >= 2 else 0
                    t_rcv = s_rc[g % 2].reserve(16)

                    def ld_r1b(sync, g=g, need=need_r2g):
                        if need > 0:
                            sync.wait_ge(s_o.h, need)
                        sync.dma_start(out=r1c[:, g % 2, :], in_=r1_r[g]).then_inc(
                            s_rc[g % 2].h, 16)
                    E.add("sync", ld_r1b)

                    need_t = s_evT.n

                    def v_lngate5(v, need_t=need_t):
                        v.wait_ge(s_evT.h, need_t)
                    E.add("vector", v_lngate5)
                    tnorm = queue_ln(E, (lambda sub=sub: ffn[:, sub, :]),
                                     (lambda: t4[:, :]))

                    need_st = s_so[g % 2].n if g >= 2 else 0
                    t_r2v = s_o.reserve()

                    def v_r2(v, g=g, need_ld=t_rcv, need_st=need_st, tnorm=tnorm):
                        v.wait_ge(s_rc[g % 2].h, need_ld)
                        v.wait_ge(s_ln2.h, tnorm)
                        if need_st > 0:
                            v.wait_ge(s_so[g % 2].h, need_st)
                        v.tensor_tensor(r2b[:, g % 2, :], t4[:, :],
                                        r1c[:, g % 2, :], op=ALU.add).then_inc(s_o.h, 1)
                    E.add("vector", v_r2)
                    r2_mark.append(s_o.n)

                    s_so[g % 2].reserve(16)

                    def gp_out(gp, g=g, need=t_r2v):
                        gp.wait_ge(s_o.h, need)
                        gp.dma_start(out=out_r[g], in_=r2b[:, g % 2, :]).then_inc(
                            s_so[g % 2].h, 16)
                    E.add("gpsimd", gp_out)

            def p4_fin(scl, a0=s_so[0].n, a1=s_so[1].n):
                scl.wait_ge(s_so[0].h, a0)
                scl.wait_ge(s_so[1].h, a1)
            E.add("scalar", p4_fin)
            E.flush(blk4)

    return nc


def _host_tables():
    half = DK // 2
    inv_freq = 10000.0 ** (-2.0 * np.arange(half) / DK)
    ang = np.arange(S)[:, None] * inv_freq[None, :]
    cosT = np.cos(ang).T.astype(np.float32)
    sinT = np.sin(ang).T.astype(np.float32)
    cc = np.concatenate([cosT, cosT], axis=0).astype(ml_dtypes.bfloat16)
    ss2 = np.concatenate([-sinT, sinT], axis=0).astype(ml_dtypes.bfloat16)
    masks = np.zeros((4 * 128, 512), dtype=ml_dtypes.bfloat16)
    for i in range(4):
        for kk in range(128):
            masks[i * 128 + kk, i * 128 + kk:] = 1.0
    ident = np.eye(128, dtype=ml_dtypes.bfloat16)
    pswap = np.zeros((128, 128), dtype=ml_dtypes.bfloat16)
    for i in range(64):
        pswap[i + 64, i] = 1.0
        pswap[i, i + 64] = 1.0
    e8 = np.zeros((8, 64), dtype=ml_dtypes.bfloat16)
    for e in range(8):
        e8[e, e * 8:(e + 1) * 8] = 1.0
    return cc, ss2, masks, ident, pswap, e8


def _perm_rows(w):
    perm = []
    for h in range(H):
        base = h * DK
        perm += [base + 2 * i for i in range(DK // 2)]
        perm += [base + 2 * i + 1 for i in range(DK // 2)]
    return w[perm, :]


_NC_CACHE = {}


def kernel(L0_attn_x, L0_attn_wq, L0_attn_wk, L0_attn_wv, L0_attn_wo,
           L0_ffn_W_base, L0_ffn_b_base, L0_ffn_W_router, L0_ffn_A_w, L0_ffn_B_w):
    x = np.ascontiguousarray(np.asarray(L0_attn_x, dtype=np.float32))
    wq = np.asarray(L0_attn_wq, dtype=np.float32)
    wk = np.asarray(L0_attn_wk, dtype=np.float32)
    wv = np.asarray(L0_attn_wv, dtype=np.float32)
    wo = np.asarray(L0_attn_wo, dtype=np.float32)
    wb = np.asarray(L0_ffn_W_base, dtype=np.float32)
    bb = np.asarray(L0_ffn_b_base, dtype=np.float32)
    wr = np.asarray(L0_ffn_W_router, dtype=np.float32)
    aw = np.asarray(L0_ffn_A_w, dtype=np.float32)
    bw = np.asarray(L0_ffn_B_w, dtype=np.float32)

    cc, ss2, masks, ident, pswap, e8 = _host_tables()

    wqTf = np.ascontiguousarray(_perm_rows(wq).T)
    wkTf = np.ascontiguousarray(_perm_rows(wk).T)
    wvTf = np.ascontiguousarray(wv.T)
    woTf = np.ascontiguousarray(wo.T)
    wbTf = np.ascontiguousarray(wb.T)
    aflT = aw.reshape(64, D).T
    wrTf = wr.T
    awrTf = np.ascontiguousarray(np.concatenate([aflT, wrTf], axis=1))
    bflTf = np.ascontiguousarray(bw.transpose(0, 2, 1).reshape(64, D))
    bbasef = np.ascontiguousarray(bb.reshape(16, 128).T)

    if "nc" not in _NC_CACHE:
        _NC_CACHE["nc"] = build_program()
    nc = _NC_CACHE["nc"]

    in_maps = []
    for c in range(8):
        b = c // 4
        hg = c % 4
        tq = c % 4
        in_maps.append({
            "x": x[b],
            "xq": np.ascontiguousarray(x[b, tq * QT:(tq + 1) * QT, :]),
            "wqT": np.ascontiguousarray(wqTf[:, hg * 512:(hg + 1) * 512]),
            "wkT": np.ascontiguousarray(wkTf[:, hg * 512:(hg + 1) * 512]),
            "wvT": np.ascontiguousarray(wvTf[:, hg * 512:(hg + 1) * 512]),
            "woT": woTf, "wbT": wbTf, "awrT": awrTf, "bflT": bflTf, "bbase": bbasef,
            "ident": ident, "identf": np.eye(128, dtype=np.float32),
            "pswap": pswap, "e8": e8,
            "cctab": cc, "ss2tab": ss2, "masks": masks,
        })
    res = run_bass_kernel_spmd(nc, in_maps, core_ids=list(range(8)))
    out = np.empty((B, S, D), dtype=np.float32)
    for c in range(8):
        b = c // 4
        tq = c % 4
        out[b, tq * QT:(tq + 1) * QT, :] = res.results[c]["out_q"]
    return out
